# version 02
# baseline (speedup 1.0000x reference)
import numpy as np
import ml_dtypes
from contextlib import ExitStack
import concourse.bass as bass
import concourse.mybir as mybir
from concourse.bass_utils import run_bass_kernel_spmd

F32 = mybir.dt.float32
BF16 = mybir.dt.bfloat16
AF = mybir.ActivationFunctionType
ALU = mybir.AluOpType
AX = mybir.AxisListType

MAGIC = 12582912.0
SC_ATT = 0.08838834764831845
NEG_HALF = -0.5
EPS = 1e-5
T, D, DI = 2048, 2048, 8192
NT, ND = 16, 16
NKV = 8
NOB, NOC = 16, 64

DEBUG = False
UPTO = 99
ENGS = ("sync", "tensor", "vector", "scalar", "gpsimd")


class B:
    def __init__(self):
        self.ops = {k: [] for k in ENGS}
        self.cnt = {}
        self.semh = {}
        self.lastw = {e: {} for e in ENGS}
        self.phase = "init"

    def emit(self, eng, fn):
        self.ops[eng].append((self.phase, fn))

    def wait(self, eng, evt):
        if evt is None:
            return
        sem, val, peng = evt
        if peng == eng or val <= 0:
            return
        if self.lastw[eng].get(sem, 0) >= val:
            return
        self.lastw[eng][sem] = val

        def f(e, sem=sem, val=val):
            e.wait_ge(self.semh[sem], val)
        self.emit(eng, f)

    def waits(self, eng, evts):
        for ev in evts:
            self.wait(eng, ev)

    def bump(self, sem, k):
        self.cnt[sem] = self.cnt.get(sem, 0) + k
        return self.cnt[sem]

    DMA_POOL = {"sync": ["dl0", "dl1", "dl2", "dl3"],
                "scalar": ["ds0", "ds1"],
                "gpsimd": ["dg0", "dg1", "dg2", "dg3"]}

    def dma(self, eng, dst, src):
        pool = self.DMA_POOL[eng]
        k = self.cnt.get("_rr_" + eng, 0)
        self.cnt["_rr_" + eng] = k + 1
        sem = pool[k % len(pool)]
        before = self.cnt.get(sem, 0)
        self.wait(eng, (sem, before, None))
        val = self.bump(sem, 16)

        def f(e, dst=dst, src=src, sem=sem):
            e.dma_start(out=dst, in_=src).then_inc(self.semh[sem], 16)
        self.emit(eng, f)
        return (sem, val, None)

    def do(self, eng, fn, mark=None):
        if mark is None:
            self.emit(eng, lambda e, fn=fn: fn(e))
            return None
        val = self.bump(mark, 1)

        def f(e, fn=fn, mark=mark):
            fn(e).then_inc(self.semh[mark], 1)
        self.emit(eng, f)
        return (mark, val, eng)

    def barrier(self):
        for eng, pool in self.DMA_POOL.items():
            for sem in pool:
                self.wait(eng, (sem, self.cnt.get(sem, 0), None))
        tgt = self.bump("bar", len(ENGS))
        for eng in ENGS:
            def f(e):
                try:
                    e.drain()
                except Exception:
                    pass
                e.nop().then_inc(self.semh["bar"], 1)
            self.emit(eng, f)
        for eng in ENGS:
            def f(e, tgt=tgt):
                e.wait_ge(self.semh["bar"], tgt)
            self.emit(eng, f)
            self.lastw[eng]["bar"] = tgt


def build_nc(debug=False, upto=99):
    nc = bass.Bass("TRN2", target_bir_lowering=False, debug=False)
    b = B()

    def inp(name, shape, dtype=F32):
        return nc.dram_tensor(name, shape, dtype, kind="ExternalInput").ap()

    x_h = inp("x", [T, D])
    wkvq_h = inp("wkvq", [D, 6144])
    wout_h = inp("woutT", [4096, D])
    w1_h = inp("w1T", [D, DI])
    w2_h = inp("w2T", [DI, D])
    b1_h = inp("b1", [1, DI])
    g2_h = inp("lng", [1, DI])
    bb2_h = inp("lnb", [1, DI])
    b2_h = inp("b2", [1, D])
    cos_h = inp("cosT", [128, T])
    sin_h = inp("sinT", [128, T])
    tri_h = inp("trimask", [128, 512])
    id_h = inp("ident", [128, 128])
    y_h = nc.dram_tensor("y", [T, D], F32, kind="ExternalOutput").ap()

    skind = "ExternalOutput" if debug else "Internal"

    def scr(name, shape, dtype):
        return nc.dram_tensor(name, shape, dtype, kind=skind).ap()

    xq1_sc = scr("xq1_sc", [NT, ND, 128, 128], BF16)
    q_sc = scr("q_sc", [NKV, 128, NT, 4, 128], BF16)
    at_sc = scr("at_sc", [NKV, 128, NT, 512], BF16)
    s_sc = scr("s_sc", [NT, 128, D], BF16)
    h_sc = scr("h_sc", [NT, 128, DI], F32)
    xq2_sc = scr("xq2_sc", [NOC, 128, D], BF16)
    w1t_sc = scr("w1t_sc", [NOB, ND, 128, 512], BF16)
    w2t_sc = scr("w2t_sc", [ND, NOC, 128, 128], BF16)
    dbg = {}
    if debug:
        for nm, sh, dt in (("d_xnfm", [128, ND * T], BF16),
                           ("d_kfm", [128, NKV * T], BF16),
                           ("d_vkm", [128, NKV * T], BF16),
                           ("d_stats", [128, 64], F32),
                           ("d_wsc", [128, 8], F32)):
            dbg[nm] = nc.dram_tensor(nm, sh, dt, kind="ExternalOutput").ap()

    ctx = ExitStack()
    e = ctx.enter_context
    for s in ("dl0", "dl1", "dl2", "dl3", "ds0", "ds1", "dg0", "dg1",
              "dg2", "dg3", "pe", "dv", "ac", "gp", "bar"):
        b.semh[s] = e(nc.semaphore(s))

    cpool = e(nc.sbuf_tensor("cpool", [128, 4], F32))
    nc.const_aps.register(F32, SC_ATT, cpool[:, 0:1])
    nc.const_aps.register(F32, NEG_HALF, cpool[:, 1:2])
    onec = e(nc.sbuf_tensor("onec", [1, 128], F32))
    onecol = e(nc.sbuf_tensor("onecol", [128, 1], F32))
    one11 = e(nc.sbuf_tensor("one11", [1, 1], F32))
    ones_bf = e(nc.sbuf_tensor("ones_bf", [128, 128], BF16))
    identb = e(nc.sbuf_tensor("identb", [128, 128], BF16))
    identf = e(nc.sbuf_tensor("identf", [128, 128], F32))
    tri = e(nc.sbuf_tensor("tri", [128, 512], F32))
    m1c = e(nc.sbuf_tensor("m1c", [128, NT], F32))
    c1c = e(nc.sbuf_tensor("c1c", [128, NT], F32))
    c2c = e(nc.sbuf_tensor("c2c", [128, NT], F32))
    wsc = e(nc.sbuf_tensor("wsc", [128, 8], F32))
    b2cols = e(nc.sbuf_tensor("b2cols", [128, ND], F32))

    b.dma("sync", tri[:], tri_h[:, :])
    ev_idf = b.dma("sync", identf[:], id_h[:, :])

    def init_consts(v):
        v.memset(onec[:], 1.0)
        v.memset(onecol[:], 1.0)
        v.memset(one11[:], 1.0)
        v.memset(ones_bf[:], 1.0)
        v.memset(cpool[:, 0:1], SC_ATT)
        return v.memset(cpool[:, 1:2], NEG_HALF)
    ev_cst = b.do("vector", init_consts, mark="dv")
    b.wait("vector", ev_idf)
    ev_idb = b.do("vector", lambda v: v.tensor_copy(identb[:], identf[:]),
                  mark="dv")

    xn_ctx = ExitStack()
    xn_fm = xn_ctx.enter_context(nc.sbuf_tensor("xn_fm", [128, ND * T], BF16))
    rope_ctx = ExitStack()
    er = rope_ctx.enter_context
    cosT = er(nc.sbuf_tensor("cosT_sb", [128, T], F32))
    ssT = er(nc.sbuf_tensor("ssT_sb", [128, T], F32))
    b.dma("sync", cosT[:], cos_h[:, :])
    b.dma("sync", ssT[:], sin_h[:, :])

    b.phase = "A"
    if upto >= 1:
        pa = ExitStack()
        ea = pa.enter_context
        xld = [ea(nc.sbuf_tensor(f"xld{j}", [128, D], F32)) for j in range(2)]
        xn32 = ea(nc.sbuf_tensor("xn32", [128, D], F32))
        tq32 = ea(nc.sbuf_tensor("tq32", [128, D], F32))
        xnbf = ea(nc.sbuf_tensor("xnbf", [128, D], BF16))
        xqbf = ea(nc.sbuf_tensor("xqbf", [128, D], BF16))
        xqst = ea(nc.sbuf_tensor("xqst", [128, D], BF16))
        trashA = ea(nc.sbuf_tensor("trashA", [128, D], BF16))
        stc = ea(nc.sbuf_tensor("stc", [128, 20], F32))
        pA_ps = ExitStack()
        pT = pA_ps.enter_context(nc.psum_tensor("pTA", [128, 512], BF16))
        pT2 = pA_ps.enter_context(nc.psum_tensor("pTA2", [128, 512], BF16))

        (SU, SQ, MU, EX2, MU2, VAR, VEPS, LNV, INV, I2, MX2, M2E, LNM, RST,
         MX, QS, RM) = range(17)

        xld_free = [None, None]
        st_free = None
        for i in range(NT):
            buf = xld[i % 2]
            b.wait("sync", xld_free[i % 2])
            ev_x = b.dma("sync", buf[:], x_h[i * 128:(i + 1) * 128, :])
            b.wait("vector", ev_x)
            ev_ms = b.do("vector", lambda v, bu=buf: (
                v.drain(),
                v.reduce_sum(stc[:, SU:SU + 1], bu[:], AX.X),
                v.memset(stc[:, SQ:SQ + 1], 0.0))[-1], mark="dv")
            b.waits("scalar", [ev_x, ev_ms])
            ev_sq = b.do("scalar", lambda s, bu=buf: s.activation(
                trashA[:], bu[:], AF.Square, accum_out=stc[:, SQ:SQ + 1]),
                mark="ac")
            b.wait("vector", ev_sq)

            def stats1(v):
                v.tensor_scalar_mul(stc[:, MU:MU + 1], stc[:, SU:SU + 1],
                                    1.0 / D)
                v.tensor_scalar_mul(stc[:, EX2:EX2 + 1], stc[:, SQ:SQ + 1],
                                    1.0 / D)
                v.drain()
                v.tensor_mul(stc[:, MU2:MU2 + 1], stc[:, MU:MU + 1],
                             stc[:, MU:MU + 1])
                v.drain()
                v.tensor_sub(stc[:, VAR:VAR + 1], stc[:, EX2:EX2 + 1],
                             stc[:, MU2:MU2 + 1])
                v.drain()
                return v.tensor_scalar_add(stc[:, VEPS:VEPS + 1],
                                           stc[:, VAR:VAR + 1], EPS)
            ev_v1 = b.do("vector", stats1, mark="dv")
            b.wait("scalar", ev_v1)

            def act_inv(s):
                s.activation(stc[:, LNV:LNV + 1], stc[:, VEPS:VEPS + 1], AF.Ln)
                s.drain()
                return s.activation(stc[:, INV:INV + 1], stc[:, LNV:LNV + 1],
                                    AF.Exp, scale=NEG_HALF)
            ev_inv = b.do("scalar", act_inv, mark="ac")
            b.wait("vector", ev_inv)

            def xnf(v, bu=buf):
                v.drain()
                v.tensor_scalar(xn32[:], bu[:], stc[:, MU:MU + 1],
                                stc[:, INV:INV + 1], ALU.subtract, ALU.mult)
                v.tensor_mul(stc[:, I2:I2 + 1], stc[:, INV:INV + 1],
                             stc[:, INV:INV + 1])
                v.drain()
                v.tensor_copy(xnbf[:], xn32[:])
                v.tensor_mul(stc[:, MX2:MX2 + 1], stc[:, VAR:VAR + 1],
                             stc[:, I2:I2 + 1])
                v.reduce_max(stc[:, MX:MX + 1], xn32[:], AX.X,
                             apply_absolute_value=True)
                v.drain()
                return v.tensor_scalar_add(stc[:, M2E:M2E + 1],
                                           stc[:, MX2:MX2 + 1], EPS)
            ev_v2 = b.do("vector", xnf, mark="dv")
            xld_free[i % 2] = ev_v2
            b.wait("scalar", ev_v2)

            def act_rst(s):
                s.activation(stc[:, LNM:LNM + 1], stc[:, M2E:M2E + 1], AF.Ln)
                s.drain()
                return s.activation(stc[:, RST:RST + 1], stc[:, LNM:LNM + 1],
                                    AF.Exp, scale=NEG_HALF)
            ev_rst = b.do("scalar", act_rst, mark="ac")
            b.wait("vector", ev_rst)

            def quant(v, i=i):
                v.tensor_mul(m1c[:, i:i + 1], stc[:, RST:RST + 1],
                             stc[:, MX:MX + 1])
                v.drain()
                v.tensor_scalar_max(m1c[:, i:i + 1], m1c[:, i:i + 1], EPS)
                v.drain()
                v.reciprocal(stc[:, RM:RM + 1], m1c[:, i:i + 1])
                v.drain()
                v.tensor_mul(stc[:, QS:QS + 1], stc[:, RST:RST + 1],
                             stc[:, RM:RM + 1])
                v.drain()
                v.tensor_scalar_mul(stc[:, QS:QS + 1], stc[:, QS:QS + 1],
                                    127.0)
                v.drain()
                v.tensor_scalar_mul(tq32[:], xn32[:], stc[:, QS:QS + 1])
                v.drain()
                return v.tensor_scalar(xqbf[:], tq32[:], MAGIC, MAGIC,
                                       ALU.add, ALU.subtract)
            ev_q = b.do("vector", quant, mark="dv")

            b.waits("tensor", [ev_v2, ev_idb])
            for gdc in range(4):
                def tr4(t, gdc=gdc):
                    ins = None
                    for j in range(4):
                        dc = gdc * 4 + j
                        ins = t.transpose(pT[:, j * 128:(j + 1) * 128],
                                          xnbf[:, dc * 128:(dc + 1) * 128],
                                          identb[:])
                    return ins
                ev_t = b.do("tensor", tr4, mark="pe")
                b.wait("vector", ev_t)

                def evac(v, gdc=gdc, i=i):
                    dst = xn_fm[:].rearrange("p (dc t) -> p dc t", dc=ND)[
                        :, gdc * 4:(gdc + 1) * 4, i * 128:(i + 1) * 128]
                    return v.tensor_copy(
                        dst, pT[:].rearrange("p (j t) -> p j t", j=4))
                ev_ev = b.do("vector", evac, mark="dv")
                b.wait("tensor", ev_ev)
            b.waits("tensor", [ev_q, st_free])
            last_evq = None
            for gdc in range(4):
                def trq(t, gdc=gdc):
                    ins = None
                    for j in range(4):
                        dc = gdc * 4 + j
                        ins = t.transpose(pT2[:, j * 128:(j + 1) * 128],
                                          xqbf[:, dc * 128:(dc + 1) * 128],
                                          identb[:])
                    return ins
                ev_t = b.do("tensor", trq, mark="pe")
                b.wait("vector", ev_t)

                def evacq(v, gdc=gdc):
                    dst = xqst[:].rearrange("p (dc t) -> p dc t", dc=ND)[
                        :, gdc * 4:(gdc + 1) * 4, :]
                    return v.tensor_copy(
                        dst, pT2[:].rearrange("p (j t) -> p j t", j=4))
                last_evq = b.do("vector", evacq, mark="dv")
                b.wait("tensor", last_evq)
            b.wait("gpsimd", last_evq)
            st_free = b.dma("gpsimd",
                            xq1_sc[i].rearrange("dc d t -> d dc t"),
                            xqst[:].rearrange("p (dc t) -> p dc t", dc=ND))
        if debug:
            b.wait("gpsimd", ("dv", b.cnt.get("dv", 0), None))
            b.dma("gpsimd", dbg["d_xnfm"][:, :], xn_fm[:])
            b.dma("gpsimd", dbg["d_stats"][:, 0:NT], m1c[:])
        b.barrier()
        pa.close()
        pA_ps.close()

    b.phase = "W"
    if upto >= 2:
        pw = ExitStack()
        ew = pw.enter_context
        wch = [ew(nc.sbuf_tensor(f"wch{j}", [128, DI], F32)) for j in range(2)]
        acc = ew(nc.sbuf_tensor("wacc", [128, 4], F32))
        ternb = [ew(nc.sbuf_tensor(f"tern{j}", [128, DI], BF16))
                 for j in range(2)]
        negb = ew(nc.sbuf_tensor("negb", [128, DI], BF16))
        srow = ew(nc.sbuf_tensor("srow", [1, 8], F32))
        pw_ps = ExitStack()
        pW = pw_ps.enter_context(nc.psum_tensor("pW", [1, 2], F32))
        pC = pw_ps.enter_context(nc.psum_tensor("pC", [128, 8], F32))

        ch_free = [None, None]
        tern_free = [None, None]
        for wi, (w_h, nch, F, nelem) in enumerate(
                ((w1_h, 16, DI, D * DI), (w2_h, 64, D, DI * D))):
            ev_z = b.do("vector", lambda v, wi=wi: v.memset(
                acc[:, wi:wi + 1], 0.0), mark="dv")
            for c in range(nch):
                buf = wch[c % 2]
                b.wait("sync", ch_free[c % 2])
                ev_w = b.dma("sync", buf[:, 0:F],
                             w_h[c * 128:(c + 1) * 128, :])
                b.waits("vector", [ev_w, ev_z])

                def statw(v, buf=buf, F=F, wi=wi):
                    v.drain()
                    v.tensor_reduce(acc[:, 2:3], buf[:, 0:F], AX.X, ALU.add,
                                    apply_absolute_value=True)
                    v.drain()
                    return v.tensor_add(acc[:, wi:wi + 1], acc[:, wi:wi + 1],
                                        acc[:, 2:3])
                ch_free[c % 2] = b.do("vector", statw, mark="dv")
            b.waits("tensor", [ch_free[(nch - 1) % 2], ev_cst])
            ev_tot = b.do("tensor", lambda t, wi=wi: t.matmul(
                pW[0:1, wi:wi + 1], acc[:, wi:wi + 1], onecol[:, 0:1],
                start=True, stop=True), mark="pe")
            b.wait("vector", ev_tot)

            def scals(v, wi=wi, nelem=nelem):
                v.tensor_scalar_mul(srow[0:1, 4 * wi:4 * wi + 1],
                                    pW[0:1, wi:wi + 1], 1.0 / nelem)
                v.drain()
                v.tensor_scalar_max(srow[0:1, 4 * wi + 3:4 * wi + 4],
                                    srow[0:1, 4 * wi:4 * wi + 1], EPS)
                v.drain()
                v.tensor_scalar_mul(srow[0:1, 4 * wi + 1:4 * wi + 2],
                                    srow[0:1, 4 * wi + 3:4 * wi + 4], 0.5)
                return v.tensor_scalar_mul(
                    srow[0:1, 4 * wi + 2:4 * wi + 3],
                    srow[0:1, 4 * wi + 3:4 * wi + 4], -0.5)
            ev_sc = b.do("vector", scals, mark="dv")
            b.wait("tensor", ev_sc)
            ev_bc = b.do("tensor", lambda t, wi=wi: t.matmul(
                pC[:, 3 * wi:3 * wi + 3],
                onec[0:1, :],
                srow[0:1, 4 * wi + 1:4 * wi + 4],
                start=True, stop=True), mark="pe")
            b.wait("vector", ev_bc)
            ev_wsc = b.do("vector", lambda v, wi=wi: (
                v.tensor_copy(wsc[:, 3 * wi:3 * wi + 3],
                              pC[:, 3 * wi:3 * wi + 3]),
                v.drain())[0], mark="dv")
            for c in range(nch):
                buf = wch[c % 2]
                b.wait("sync", ch_free[c % 2])
                ev_w = b.dma("sync", buf[:, 0:F],
                             w_h[c * 128:(c + 1) * 128, :])
                tb = ternb[c % 2]
                b.waits("vector", [ev_w, ev_wsc, tern_free[c % 2]])

                def ternf(v, buf=buf, tb=tb, F=F, wi=wi):
                    v.drain()
                    v.tensor_single_scalar(negb[:, 0:F], buf[:, 0:F],
                                           wsc[:, 3 * wi + 1:3 * wi + 2],
                                           ALU.is_lt)
                    v.drain()
                    return v.scalar_tensor_tensor(
                        tb[:, 0:F], buf[:, 0:F],
                        wsc[:, 3 * wi:3 * wi + 1], negb[:, 0:F],
                        ALU.is_gt, ALU.subtract)
                ev_t = b.do("vector", ternf, mark="dv")
                ch_free[c % 2] = ev_t
                b.wait("gpsimd", ev_t)
                if wi == 0:
                    dst = w1t_sc[:, c, :, :].rearrange("ob d o -> d ob o")
                    src = tb[:, 0:F].rearrange("d (ob o) -> d ob o", ob=NOB)
                else:
                    dst = w2t_sc[:, c, :, :].rearrange("dpc o dp -> o dpc dp")
                    src = tb[:, 0:F].rearrange("o (dpc dp) -> o dpc dp",
                                               dpc=ND)
                tern_free[c % 2] = b.dma("gpsimd", dst, src)
            if wi == 0:
                ev_c1 = b.do("vector", lambda v: (
                    v.drain(),
                    v.tensor_scalar(c1c[:], m1c[:], wsc[:, 2:3], 1.0 / 127.0,
                                    ALU.mult, ALU.mult))[-1], mark="dv")
        if debug:
            b.wait("gpsimd", ("dv", b.cnt.get("dv", 0), None))
            b.dma("gpsimd", dbg["d_wsc"][:, 0:6], wsc[:, 0:6])
        b.barrier()
        pw.close()
        pw_ps.close()

    kv_ctx = ExitStack()
    k_fm = kv_ctx.enter_context(
        nc.sbuf_tensor("k_fm", [128, NKV * T], BF16, side="right"))
    v_km = kv_ctx.enter_context(
        nc.sbuf_tensor("v_km", [128, NKV * T], BF16, side="right"))
    b.phase = "P1kv"
    if upto >= 3:
        p1 = ExitStack()
        e1 = p1.enter_context
        wres = e1(nc.sbuf_tensor("wres1", [128, 16 * 1024], BF16))
        wstg = [e1(nc.sbuf_tensor(f"wstg{j}", [128, 1024], F32))
                for j in range(2)]
        a32 = e1(nc.sbuf_tensor("a32", [128, 1024], F32))
        b32 = e1(nc.sbuf_tensor("b32", [128, 1024], F32))
        vstg = e1(nc.sbuf_tensor("vstg", [128, 1024], BF16))
        p1_ps = ExitStack()
        pA = p1_ps.enter_context(nc.psum_tensor("pA1", [128, 1024], F32))
        pB = p1_ps.enter_context(nc.psum_tensor("pB1", [128, 1024], F32))
        pTv = p1_ps.enter_context(nc.psum_tensor("pTv", [128, 512], BF16))

        last_pe_half = None
        stg_free = [None, None]
        prev_evac = [None, None]
        for half in range(2):
            cast_evs = []
            for dc in range(ND):
                sb = wstg[dc % 2]
                b.wait("sync", stg_free[dc % 2])
                ev_w = b.dma("sync", sb[:],
                             wkvq_h[dc * 128:(dc + 1) * 128,
                                    half * 1024:(half + 1) * 1024])
                b.waits("scalar", [ev_w, last_pe_half])
                ev_c = b.do("scalar", lambda s, sb=sb, dc=dc: s.copy(
                    wres[:, dc * 1024:(dc + 1) * 1024], sb[:]), mark="ac")
                stg_free[dc % 2] = ev_c
                cast_evs.append(ev_c)
            b.waits("tensor", cast_evs)
            unit = 0
            for oc in range(8):
                for tb in range(2):
                    ps = pA if unit % 2 == 0 else pB
                    b.wait("tensor", prev_evac[unit % 2])

                    def mmkv(t, oc=oc, tb=tb, ps=ps):
                        ins = None
                        for dc in range(ND):
                            for nb in range(2):
                                ins = t.matmul(
                                    ps[:, nb * 512:(nb + 1) * 512],
                                    wres[:, dc * 1024 + oc * 128:
                                         dc * 1024 + (oc + 1) * 128],
                                    xn_fm[:, dc * T + tb * 1024 + nb * 512:
                                          dc * T + tb * 1024 +
                                          (nb + 1) * 512],
                                    start=(dc == 0), stop=(dc == ND - 1))
                        return ins
                    ev_mm = b.do("tensor", mmkv, mark="pe")
                    b.wait("vector", ev_mm)
                    if half == 0:
                        ev_cp = b.do("vector", lambda v, ps=ps:
                                     v.tensor_copy(vstg[:], ps[:]),
                                     mark="dv")
                        prev_evac[unit % 2] = ev_cp
                        b.wait("tensor", ev_cp)
                        for grp in range(2):
                            def trv(t, grp=grp):
                                ins = None
                                for j in range(4):
                                    ins = t.transpose(
                                        pTv[:, j * 128:(j + 1) * 128],
                                        vstg[:, (grp * 4 + j) * 128:
                                             (grp * 4 + j + 1) * 128],
                                        identb[:])
                                return ins
                            ev_t = b.do("tensor", trv, mark="pe")
                            b.wait("vector", ev_t)

                            def evv(v, oc=oc, tb=tb, grp=grp):
                                kt0 = oc * 16 + tb * 8 + grp * 4
                                return v.tensor_copy(
                                    v_km[:, kt0 * 128:(kt0 + 4) * 128],
                                    pTv[:])
                            ev_e = b.do("vector", evv, mark="dv")
                            b.wait("tensor", ev_e)
                    else:
                        def ropek(v, ps=ps, oc=oc, tb=tb):
                            sl = slice(tb * 1024, (tb + 1) * 1024)
                            v.drain()
                            v.tensor_mul(a32[:], ps[:], cosT[:, sl])
                            v.tensor_mul(b32[0:64, :], ps[64:128, :],
                                         ssT[0:64, sl])
                            v.tensor_mul(b32[64:128, :], ps[0:64, :],
                                         ssT[64:128, sl])
                            v.drain()
                            return v.tensor_add(
                                k_fm[:, oc * T + tb * 1024:
                                     oc * T + (tb + 1) * 1024],
                                a32[:], b32[:])
                        ev_r = b.do("vector", ropek, mark="dv")
                        prev_evac[unit % 2] = ev_r
                    unit += 1
            last_pe_half = ("pe", b.cnt.get("pe", 0), "tensor")
        if debug:
            b.wait("gpsimd", ("dv", b.cnt.get("dv", 0), None))
            b.dma("gpsimd", dbg["d_kfm"][:, :], k_fm[:])
            b.dma("gpsimd", dbg["d_vkm"][:, :], v_km[:])
        b.barrier()
        p1.close()
        p1_ps.close()

    b.phase = "P2q"
    if upto >= 4:
        p2 = ExitStack()
        e2 = p2.enter_context
        wres2 = e2(nc.sbuf_tensor("wres2", [128, 16 * 1024], BF16))
        wstg2 = e2(nc.sbuf_tensor("wstg2", [128, 1024], F32))
        a32q = e2(nc.sbuf_tensor("a32q", [128, 1024], F32))
        b32q = e2(nc.sbuf_tensor("b32q", [128, 1024], F32))
        qstg = [e2(nc.sbuf_tensor(f"qstg{j}", [128, 1024], BF16))
                for j in range(2)]
        p2_ps = ExitStack()
        pA2 = p2_ps.enter_context(nc.psum_tensor("pA2", [128, 1024], F32))
        pB2 = p2_ps.enter_context(nc.psum_tensor("pB2", [128, 1024], F32))

        last_pe_qr = None
        stg_free = None
        prev_rope = [None, None]
        st_free = [None, None]
        for qr in range(4):
            cast_evs = []
            for dc in range(ND):
                b.wait("sync", stg_free)
                ev_w = b.dma("sync", wstg2[:],
                             wkvq_h[dc * 128:(dc + 1) * 128,
                                    2048 + qr * 1024:2048 + (qr + 1) * 1024])
                b.waits("scalar", [ev_w, last_pe_qr])
                ev_c = b.do("scalar", lambda s, dc=dc: s.copy(
                    wres2[:, dc * 1024:(dc + 1) * 1024], wstg2[:]),
                    mark="ac")
                stg_free = ev_c
                cast_evs.append(ev_c)
            b.waits("tensor", cast_evs)
            unit = 0
            for oc in range(8):
                h = qr * 8 + oc
                for tb in range(2):
                    ps = pA2 if unit % 2 == 0 else pB2
                    b.wait("tensor", prev_rope[unit % 2])

                    def mmq(t, oc=oc, tb=tb, ps=ps):
                        ins = None
                        for dc in range(ND):
                            for nb in range(2):
                                ins = t.matmul(
                                    ps[:, nb * 512:(nb + 1) * 512],
                                    wres2[:, dc * 1024 + oc * 128:
                                          dc * 1024 + (oc + 1) * 128],
                                    xn_fm[:, dc * T + tb * 1024 + nb * 512:
                                          dc * T + tb * 1024 +
                                          (nb + 1) * 512],
                                    start=(dc == 0), stop=(dc == ND - 1))
                        return ins
                    ev_mm = b.do("tensor", mmq, mark="pe")
                    qs_buf = qstg[unit % 2]
                    b.waits("vector", [ev_mm, st_free[unit % 2]])

                    def ropeq(v, ps=ps, tb=tb, qs_buf=qs_buf):
                        sl = slice(tb * 1024, (tb + 1) * 1024)
                        v.drain()
                        v.tensor_mul(a32q[:], ps[:], cosT[:, sl])
                        v.tensor_mul(b32q[0:64, :], ps[64:128, :],
                                     ssT[0:64, sl])
                        v.tensor_mul(b32q[64:128, :], ps[0:64, :],
                                     ssT[64:128, sl])
                        v.drain()
                        return v.tensor_add(qs_buf[:], a32q[:], b32q[:])
                    ev_r = b.do("vector", ropeq, mark="dv")
                    prev_rope[unit % 2] = ev_r
                    b.wait("gpsimd", ev_r)
                    st_free[unit % 2] = b.dma(
                        "gpsimd",
                        q_sc[h // 4, :, tb * 8:(tb + 1) * 8, h % 4, :],
                        qs_buf[:].rearrange("p (qt t) -> p qt t", qt=8))
                    unit += 1
            last_pe_qr = ("pe", b.cnt.get("pe", 0), "tensor")
        b.barrier()
        p2.close()
        p2_ps.close()
    rope_ctx.close()
    xn_ctx.close()

    b.phase = "P3flash"
    if upto >= 5:
        p3 = ExitStack()
        e3 = p3.enter_context
        qbuf = [e3(nc.sbuf_tensor(f"qbuf{j}", [128, 4096], BF16))
                for j in range(2)]
        Pp = [e3(nc.sbuf_tensor(f"Pp{j}", [128, 512], BF16))
              for j in range(2)]
        zrec = e3(nc.sbuf_tensor("zrec", [128, 512], F32))
        atile = [e3(nc.sbuf_tensor(f"atile{j}", [128, 512], BF16))
                 for j in range(2)]
        p3_ps = ExitStack()
        pS = [p3_ps.enter_context(nc.psum_tensor(f"pS{j}", [128, 512], F32))
              for j in range(2)]
        pAV = p3_ps.enter_context(nc.psum_tensor("pAV", [128, 512], F32))
        pZ = p3_ps.enter_context(nc.psum_tensor("pZ", [128, 512], F32))

        q_free = [None, None]
        at_free = [None, None]
        ev_evac = None
        ai = 0
        for qt in range(NT):
            qb = qbuf[qt % 2]
            b.wait("sync", q_free[qt % 2])
            ev_q = b.dma("sync", qb[:].rearrange("p (kvh f) -> p kvh f",
                                                 kvh=NKV),
                         q_sc[:, :, qt, :, :].rearrange(
                             "kvh hd g t -> hd kvh (g t)"))
            last_smm = None
            exp_evs = {}
            pe_z = {}
            g = 0
            for kvh in range(NKV):

                def emit_smm(kt, g, kvh=kvh, qt=qt, qb=qb):
                    par = g % 2
                    b.waits("tensor", [ev_q, exp_evs.get(g - 2), ev_evac])

                    def smm(t, kvh=kvh, kt=kt, par=par, qb=qb):
                        return t.matmul(
                            pS[par][:],
                            k_fm[:, kvh * T + kt * 128:kvh * T +
                                 (kt + 1) * 128],
                            qb[:, kvh * 512:(kvh + 1) * 512],
                            start=True, stop=True)
                    ev_s = b.do("tensor", smm, mark="pe")
                    src_ev = ev_s
                    if kt == qt:
                        b.wait("vector", ev_s)
                        ev_m = b.do("vector", lambda v, par=par:
                                    v.tensor_tensor(pS[par][:], pS[par][:],
                                                    tri[:], ALU.add),
                                    mark="dv")
                        src_ev = ev_m
                    b.waits("scalar", [src_ev, pe_z.get(g - 2)])
                    ev_e = b.do("scalar", lambda s, par=par: s.activation(
                        Pp[par][:], pS[par][:], AF.Exp, scale=SC_ATT),
                        mark="ac")
                    exp_evs[g] = ev_e
                    return ev_s

                def emit_av(kt, g, kvh=kvh, qt=qt):
                    par = g % 2
                    b.wait("tensor", exp_evs[g])
                    b.do("tensor", lambda t, kvh=kvh, kt=kt, qt=qt, par=par:
                         t.matmul(pAV[:],
                                  v_km[:, (kvh * 16 + kt) * 128:
                                       (kvh * 16 + kt + 1) * 128],
                                  Pp[par][:], start=(kt == 0),
                                  stop=(kt == qt)))
                    pe_z[g] = b.do(
                        "tensor", lambda t, kt=kt, qt=qt, par=par:
                        t.matmul(pZ[:], ones_bf[:], Pp[par][:],
                                 start=(kt == 0), stop=(kt == qt)),
                        mark="pe")
                    return pe_z[g]

                for kt in range(qt + 1):
                    last_smm = emit_smm(kt, g + kt)
                    if kt >= 1:
                        emit_av(kt - 1, g + kt - 1)
                ev_zlast = emit_av(qt, g + qt)
                g += qt + 1
                at = atile[ai % 2]
                b.waits("vector", [ev_zlast, at_free[ai % 2]])

                def evfl(v, at=at):
                    v.drain()
                    v.reciprocal(zrec[:], pZ[:])
                    v.drain()
                    return v.tensor_tensor(at[:], pAV[:], zrec[:], ALU.mult)
                ev_evac = b.do("vector", evfl, mark="dv")
                b.wait("gpsimd", ev_evac)
                at_free[ai % 2] = b.dma("gpsimd", at_sc[kvh, :, qt, :], at[:])
                ai += 1
            q_free[qt % 2] = last_smm
        b.barrier()
        p3.close()
        p3_ps.close()
    kv_ctx.close()

    b.phase = "F1"
    if upto >= 6:
        pf = ExitStack()
        ef = pf.enter_context
        xq1b = ef(nc.sbuf_tensor("xq1b", [128, ND * T], BF16))
        w1b = [ef(nc.sbuf_tensor(f"w1b{j}", [128, DI], BF16))
               for j in range(2)]
        b1bc = ef(nc.sbuf_tensor("b1bc", [128, DI], F32))
        b1row_t = ef(nc.sbuf_tensor("b1row_t", [1, DI], F32))
        b1row = b1row_t[0:1, :]
        ev_b1l = b.dma("sync", b1row, b1_h[0:1, :])
        htmp = [ef(nc.sbuf_tensor(f"htmp{j}", [128, 512], F32))
                for j in range(2)]
        hsil = [ef(nc.sbuf_tensor(f"hsil{j}", [128, 512], F32))
                for j in range(2)]
        pf_ps = ExitStack()
        pF = [pf_ps.enter_context(nc.psum_tensor(f"pF{j}", [128, 512], F32))
              for j in range(2)]
        pB5 = pf_ps.enter_context(nc.psum_tensor("pB5", [128, 512], F32))

        ev_xq = []
        for qt in range(NT):
            dst = xq1b[:].rearrange("p (dc t) -> p dc t", dc=ND)[
                :, :, qt * 128:(qt + 1) * 128]
            ev_xq.append(b.dma("sync", dst,
                               xq1_sc[qt].rearrange("dc d t -> d dc t")))
        b.wait("tensor", ev_b1l)
        for ob in range(NOB):
            ev_b = b.do("tensor", lambda t, ob=ob: t.matmul(
                pB5[:], onec[0:1, :], b1row[0:1, ob * 512:(ob + 1) * 512],
                start=True, stop=True), mark="pe")
            b.wait("vector", ev_b)
            ev_v = b.do("vector", lambda v, ob=ob: (
                v.tensor_copy(b1bc[:, ob * 512:(ob + 1) * 512], pB5[:]),
                v.drain())[0], mark="dv")
            b.wait("tensor", ev_v)

        w_free = [None, None]
        ui = 0
        h_free = [None, None]
        ev_evF = [None, None]
        for ob in range(NOB):
            wb_ = w1b[ob % 2]
            b.wait("sync", w_free[ob % 2])
            ev_w = b.dma("sync", wb_[:].rearrange("p (dc o) -> p dc o",
                                                  dc=ND),
                         w1t_sc[ob].rearrange("dc d o -> d dc o"))
            last_mm = None
            for qt in range(NT):
                ps = pF[ui % 2]
                b.waits("tensor", ev_xq + [ev_w, ev_evF[ui % 2]])

                def mmf(t, ob=ob, qt=qt, ps=ps, wb_=wb_):
                    ins = None
                    for dc in range(ND):
                        ins = t.matmul(ps[:],
                                       xq1b[:, dc * T + qt * 128:
                                            dc * T + (qt + 1) * 128],
                                       wb_[:, dc * 512:(dc + 1) * 512],
                                       start=(dc == 0), stop=(dc == ND - 1))
                    return ins
                ev_mm = b.do("tensor", mmf, mark="pe")
                last_mm = ev_mm
                b.waits("vector", [ev_mm, h_free[ui % 2]])

                def evf(v, ps=ps, qt=qt, ob=ob, j=ui % 2):
                    return v.scalar_tensor_tensor(
                        htmp[j][:], ps[:], c1c[:, qt:qt + 1],
                        b1bc[:, ob * 512:(ob + 1) * 512],
                        ALU.mult, ALU.add)
                ev_e = b.do("vector", evf, mark="dv")
                ev_evF[ui % 2] = ev_e
                b.wait("scalar", ev_e)
                ev_s = b.do("scalar", lambda s, j=ui % 2: s.activation(
                    hsil[j][:], htmp[j][:], AF.Silu), mark="ac")
                b.wait("gpsimd", ev_s)
                h_free[ui % 2] = b.dma(
                    "gpsimd", h_sc[qt, :, ob * 512:(ob + 1) * 512],
                    hsil[ui % 2][:])
                ui += 1
            w_free[ob % 2] = last_mm
        b.barrier()
        pf.close()
        pf_ps.close()

    b.phase = "PF"
    if upto >= 7:
        pg = ExitStack()
        eg = pg.enter_context
        gbc = eg(nc.sbuf_tensor("gbc", [128, DI], F32))
        bbc = eg(nc.sbuf_tensor("bbc", [128, DI], F32))
        stq = eg(nc.sbuf_tensor("stq", [128, 16], F32))
        pg_ps = ExitStack()
        pG = pg_ps.enter_context(nc.psum_tensor("pG", [128, 512], F32))
        pT3 = pg_ps.enter_context(nc.psum_tensor("pT3", [128, 512], BF16))
        pO = [pg_ps.enter_context(nc.psum_tensor(f"pO{j}", [128, 512], F32))
              for j in range(2)]

        rows_ctx = ExitStack()
        growb = rows_ctx.enter_context(
            nc.sbuf_tensor("growb", [1, 2 * DI], F32))
        ev_gl = b.dma("sync", growb[0:1, 0:DI], g2_h[0:1, :])
        ev_bl = b.dma("sync", growb[0:1, DI:2 * DI], bb2_h[0:1, :])
        b.waits("tensor", [ev_gl, ev_bl])
        ev_gb_done = None
        for ri, dstt in ((0, gbc), (1, bbc)):
            for ob in range(NOB):
                ev_b = b.do("tensor", lambda t, ob=ob, ri=ri:
                            t.matmul(pG[:], onec[0:1, :],
                                     growb[0:1, ri * DI + ob * 512:
                                           ri * DI + (ob + 1) * 512],
                                     start=True, stop=True), mark="pe")
                b.wait("vector", ev_b)
                ev_v = b.do("vector", lambda v, ob=ob, dstt=dstt: (
                    v.tensor_copy(dstt[:, ob * 512:(ob + 1) * 512], pG[:]),
                    v.drain())[0], mark="dv")
                b.wait("tensor", ev_v)
                ev_gb_done = ev_v
        rows_ctx.close()
        hb = eg(nc.sbuf_tensor("hbP", [128, DI], F32))
        ab2 = eg(nc.sbuf_tensor("ab2", [128, DI], F32))
        xqt2 = eg(nc.sbuf_tensor("xqt2", [128, DI], BF16))
        trashP = xqt2
        wslab = eg(nc.sbuf_tensor("wslab", [128, 32 * 512], BF16))
        wst4 = eg(nc.sbuf_tensor("wst4", [128, 512], F32))
        abuf = [eg(nc.sbuf_tensor(f"abufP{j}", [128, 4096],
                                  BF16)) for j in range(2)]
        sst = [eg(nc.sbuf_tensor(f"sstP{j}", [128, 512], BF16))
               for j in range(2)]

        (ASUM, ASQ, MU2c, EX2c, VARc, VEPSc, LNc, INVc, SQ2, EXRc, LNRc,
         RST2, T2c, M2c, RM2c, QS2c) = range(16)

        f2_state = {"h_free": ev_gb_done, "st2_free": None,
                    "pool_prev": None}

        def emit_f2(qt):
            b.wait("sync", f2_state["h_free"])
            ev_h = b.dma("sync", hb[:], h_sc[qt])
            ev_ms = b.do("vector", lambda v: (
                v.memset(stq[:, ASUM:ASUM + 1], 0.0),
                v.memset(stq[:, ASQ:ASQ + 1], 0.0),
                v.memset(stq[:, SQ2:SQ2 + 1], 0.0))[-1], mark="dv")
            b.waits("scalar", [ev_h, ev_ms, f2_state["pool_prev"],
                               f2_state["st2_free"]])

            def gelu_sq(s):
                s.activation(ab2[:], hb[:], AF.Gelu,
                             accum_out=stq[:, ASUM:ASUM + 1])
                s.drain()
                return s.activation(trashP[:], ab2[:], AF.Square,
                                    accum_out=stq[:, ASQ:ASQ + 1])
            ev_g = b.do("scalar", gelu_sq, mark="ac")
            b.wait("vector", ev_g)

            def stats2(v):
                v.tensor_scalar_mul(stq[:, MU2c:MU2c + 1],
                                    stq[:, ASUM:ASUM + 1], 1.0 / DI)
                v.tensor_scalar_mul(stq[:, EX2c:EX2c + 1],
                                    stq[:, ASQ:ASQ + 1], 1.0 / DI)
                v.drain()
                v.tensor_mul(stq[:, VARc:VARc + 1], stq[:, MU2c:MU2c + 1],
                             stq[:, MU2c:MU2c + 1])
                v.drain()
                v.tensor_sub(stq[:, VARc:VARc + 1], stq[:, EX2c:EX2c + 1],
                             stq[:, VARc:VARc + 1])
                v.drain()
                return v.tensor_scalar_add(stq[:, VEPSc:VEPSc + 1],
                                           stq[:, VARc:VARc + 1], EPS)
            ev_s2 = b.do("vector", stats2, mark="dv")
            b.wait("scalar", ev_s2)
            ev_i2 = b.do("scalar", lambda s: (
                s.activation(stq[:, LNc:LNc + 1], stq[:, VEPSc:VEPSc + 1],
                             AF.Ln),
                s.drain(),
                s.activation(stq[:, INVc:INVc + 1], stq[:, LNc:LNc + 1],
                             AF.Exp, scale=NEG_HALF))[-1], mark="ac")
            b.wait("vector", ev_i2)
            ev_zn = b.do("vector", lambda v: (
                v.drain(),
                v.tensor_scalar(ab2[:], ab2[:], stq[:, MU2c:MU2c + 1],
                                stq[:, INVc:INVc + 1], ALU.subtract,
                                ALU.mult))[-1], mark="dv")
            b.wait("gpsimd", ev_zn)
            ev_gm = b.do("gpsimd", lambda g: (
                g.tensor_mul(ab2[:], ab2[:], gbc[:]),
                g.drain(),
                g.tensor_add(ab2[:], ab2[:], bbc[:]))[-1], mark="gp")
            f2_state["pool_prev"] = ev_gm
            b.wait("vector", ev_gm)
            ev_mx = b.do("vector", lambda v: v.reduce_max(
                stq[:, T2c:T2c + 1], ab2[:], AX.X,
                apply_absolute_value=True), mark="dv")
            b.waits("scalar", [ev_gm])
            ev_sq2 = b.do("scalar", lambda s: s.activation(
                trashP[:], ab2[:], AF.Square,
                accum_out=stq[:, SQ2:SQ2 + 1]), mark="ac")
            b.wait("vector", ev_sq2)

            def preR(v):
                v.tensor_scalar_mul(stq[:, EXRc:EXRc + 1],
                                    stq[:, SQ2:SQ2 + 1], 1.0 / DI)
                v.drain()
                return v.tensor_scalar_add(stq[:, EXRc:EXRc + 1],
                                           stq[:, EXRc:EXRc + 1], EPS)
            ev_pr = b.do("vector", preR, mark="dv")
            b.wait("scalar", ev_pr)
            ev_r2 = b.do("scalar", lambda s: (
                s.activation(stq[:, LNRc:LNRc + 1], stq[:, EXRc:EXRc + 1],
                             AF.Ln),
                s.drain(),
                s.activation(stq[:, RST2:RST2 + 1], stq[:, LNRc:LNRc + 1],
                             AF.Exp, scale=NEG_HALF))[-1], mark="ac")
            b.waits("vector", [ev_r2, ev_mx])

            def quant2(v, qt=qt):
                v.tensor_mul(stq[:, M2c:M2c + 1], stq[:, RST2:RST2 + 1],
                             stq[:, T2c:T2c + 1])
                v.drain()
                v.tensor_scalar_max(stq[:, M2c:M2c + 1],
                                    stq[:, M2c:M2c + 1], EPS)
                v.drain()
                v.tensor_scalar(c2c[:, qt:qt + 1], stq[:, M2c:M2c + 1],
                                wsc[:, 5:6], 1.0 / 127.0, ALU.mult, ALU.mult)
                v.reciprocal(stq[:, RM2c:RM2c + 1], stq[:, M2c:M2c + 1])
                v.drain()
                v.tensor_mul(stq[:, QS2c:QS2c + 1], stq[:, RST2:RST2 + 1],
                             stq[:, RM2c:RM2c + 1])
                v.drain()
                v.tensor_scalar_mul(stq[:, QS2c:QS2c + 1],
                                    stq[:, QS2c:QS2c + 1], 127.0)
                v.drain()
                v.tensor_scalar_mul(hb[:], ab2[:], stq[:, QS2c:QS2c + 1])
                v.drain()
                return v.tensor_scalar(xqt2[:], hb[:], MAGIC, MAGIC,
                                       ALU.add, ALU.subtract)
            ev_q2 = b.do("vector", quant2, mark="dv")
            f2_state["h_free"] = ev_q2
            b.waits("tensor", [ev_q2, f2_state["st2_free"]])
            last_ev = None
            for grp in range(NT):
                def trx(t, grp=grp):
                    ins = None
                    for j in range(4):
                        oc = grp * 4 + j
                        ins = t.transpose(pT3[:, j * 128:(j + 1) * 128],
                                          xqt2[:, oc * 128:(oc + 1) * 128],
                                          identb[:])
                    return ins
                ev_t = b.do("tensor", trx, mark="pe")
                eng_e = "vector" if grp % 2 == 0 else "scalar"
                b.wait(eng_e, ev_t)

                def evx(e, grp=grp, eng_e=eng_e):
                    dst = xqt2[:].rearrange("p (oc t) -> p oc t", oc=NOC)[
                        :, grp * 4:(grp + 1) * 4, :]
                    src = pT3[:].rearrange("p (j t) -> p j t", j=4)
                    return (e.tensor_copy(dst, src) if eng_e == "vector"
                            else e.copy(dst, src))
                last_ev = b.do(eng_e, evx, mark="dv" if eng_e == "vector"
                               else "ac")
                b.wait("tensor", last_ev)
            b.waits("gpsimd", [last_ev,
                               ("dv", b.cnt.get("dv", 0), None),
                               ("ac", b.cnt.get("ac", 0), None)])
            f2_state["st2_free"] = b.dma(
                "gpsimd",
                xq2_sc[:, :, qt * 128:(qt + 1) * 128].rearrange(
                    "oc o t -> o oc t"),
                xqt2[:].rearrange("p (oc t) -> p oc t", oc=NOC))

        a_free = [None, None]
        s_free = [None, None]
        ev_ev4 = [None, None]
        slab_last_mm = None
        f2_done = 0
        for sl in range(4):
            stg_free = None
            cast_evs = []
            for h in range(32):
                b.wait("sync", stg_free)
                ev_w = b.dma("sync", wst4[:],
                             wout_h[h * 128:(h + 1) * 128,
                                    sl * 512:(sl + 1) * 512])
                b.waits("scalar", [ev_w, slab_last_mm])
                ev_c = b.do("scalar", lambda s, h=h: s.copy(
                    wslab[:, h * 512:(h + 1) * 512], wst4[:]), mark="ac")
                stg_free = ev_c
                cast_evs.append(ev_c)
            for qt in range(NT):
                ab = abuf[qt % 2]
                b.wait("sync", a_free[qt % 2])
                ev_a = b.dma("sync",
                             ab[:].rearrange("p (kvh f) -> p kvh f",
                                             kvh=NKV),
                             at_sc[:, :, qt, :].rearrange(
                                 "kvh hd f -> hd kvh f"))
                ps = pO[qt % 2]
                b.waits("tensor", [ev_a, cast_evs[-1], ev_ev4[qt % 2]])

                def mmo(t, sl=sl, ps=ps, ab=ab):
                    ins = None
                    for h in range(32):
                        ins = t.matmul(
                            ps[:],
                            ab[:, (h // 4) * 512 + (h % 4) * 128:
                               (h // 4) * 512 + (h % 4 + 1) * 128],
                            wslab[:, h * 512:(h + 1) * 512],
                            start=(h == 0), stop=(h == 31))
                    return ins
                ev_mm = b.do("tensor", mmo, mark="pe")
                a_free[qt % 2] = ev_mm
                if qt == NT - 1:
                    slab_last_mm = ev_mm
                ss = sst[qt % 2]
                b.waits("vector", [ev_mm, s_free[qt % 2]])
                ev_e = b.do("vector", lambda v, ps=ps, ss=ss:
                            v.tensor_copy(ss[:], ps[:]), mark="dv")
                ev_ev4[qt % 2] = ev_e
                b.wait("gpsimd", ev_e)
                s_free[qt % 2] = b.dma(
                    "gpsimd", s_sc[qt, :, sl * 512:(sl + 1) * 512], ss[:])
                if f2_done < NT and (sl * NT + qt) % 3 == 1:
                    emit_f2(f2_done)
                    f2_done += 1
        while f2_done < NT:
            emit_f2(f2_done)
            f2_done += 1
        b.barrier()
        pg.close()
        pg_ps.close()

    b.phase = "M2"
    if upto >= 9:
        pm = ExitStack()
        em = pm.enter_context
        x2q = em(nc.sbuf_tensor("x2q", [128, NOC * 512], BF16))
        w2b = [em(nc.sbuf_tensor(f"w2b{j}", [128, NOC * 128], BF16))
               for j in range(2)]
        c2bc = em(nc.sbuf_tensor("c2bc", [128, 512], F32))
        c2r = em(nc.sbuf_tensor("c2r", [1, 128], F32))
        b2row_t = em(nc.sbuf_tensor("b2row_t", [1, D], F32))
        b2row = b2row_t[0:1, :]
        ev_b2l = b.dma("sync", b2row, b2_h[0:1, :])
        mtmp = em(nc.sbuf_tensor("mtmp", [128, 512], F32))
        gb_ = em(nc.sbuf_tensor("gb_", [128, 512], BF16))
        gq = em(nc.sbuf_tensor("gq", [128, 4 * D], BF16))
        xcb = em(nc.sbuf_tensor("xcb", [128, D], F32))
        scb2 = em(nc.sbuf_tensor("scb2", [128, D], BF16))
        sg32 = em(nc.sbuf_tensor("sg32", [128, D], F32))
        ocb = em(nc.sbuf_tensor("ocb", [128, D], F32))
        pm_ps = ExitStack()
        pM = [pm_ps.enter_context(nc.psum_tensor(f"pM{j}", [128, 512], F32))
              for j in range(2)]
        pTg = pm_ps.enter_context(nc.psum_tensor("pTg", [128, 512], BF16))
        pCt = pm_ps.enter_context(nc.psum_tensor("pCt", [1, 128], F32))
        pCb = pm_ps.enter_context(nc.psum_tensor("pCb", [128, 128], F32))

        b.wait("tensor", ev_b2l)
        for dpc in range(ND):
            ev_b = b.do("tensor", lambda t, dpc=dpc: t.matmul(
                pCb[:, 0:1], b2row[0:1, dpc * 128:(dpc + 1) * 128],
                one11[0:1, 0:1], start=True, stop=True), mark="pe")
            b.wait("vector", ev_b)
            ev_v = b.do("vector", lambda v, dpc=dpc: v.tensor_copy(
                b2cols[:, dpc:dpc + 1], pCb[:, 0:1]), mark="dv")
            b.wait("tensor", ev_v)

        x2_free = None
        w_free = [None, None]
        ev_evM = [None, None]
        ev_gev = None
        prev_sl = None
        prev_tg = None
        for quar in range(4):
            b.wait("sync", x2_free)
            ev_x2 = b.dma("sync",
                          x2q[:].rearrange("p (oc t) -> p oc t", oc=NOC),
                          xq2_sc[:, :, quar * 512:(quar + 1) * 512]
                          .rearrange("oc o t -> o oc t"))
            for j in range(4):
                qt = quar * 4 + j
                ev_t = b.do("tensor", lambda t, qt=qt: t.matmul(
                    pCt[0:1, :], c2c[:, qt:qt + 1], identf[:],
                    start=True, stop=True), mark="pe")
                b.wait("vector", ev_t)
                ev_v = b.do("vector", lambda v: v.tensor_copy(c2r[:],
                                                              pCt[0:1, :]),
                            mark="dv")
                b.wait("tensor", ev_v)
                ev_b2 = b.do("tensor", lambda t: t.matmul(
                    pCb[:], onec[0:1, :], c2r[0:1, :], start=True,
                    stop=True), mark="pe")
                b.wait("vector", ev_b2)
                ev_c = b.do("vector", lambda v, j=j: (
                    v.tensor_copy(c2bc[:, j * 128:(j + 1) * 128], pCb[:]),
                    v.drain())[0], mark="dv")
                b.wait("tensor", ev_c)

            for dpc in range(ND):
                wb2_ = w2b[dpc % 2]
                b.wait("sync", w_free[dpc % 2])
                ev_w = b.dma("sync",
                             wb2_[:].rearrange("p (oc dp) -> p oc dp",
                                               oc=NOC),
                             w2t_sc[dpc].rearrange("oc o dp -> o oc dp"))
                ps = pM[dpc % 2]
                b.waits("tensor", [ev_x2, ev_w, ev_evM[dpc % 2]])

                def mm2(t, ps=ps, wb2_=wb2_):
                    ins = None
                    for oc in range(NOC):
                        ins = t.matmul(ps[:],
                                       wb2_[:, oc * 128:(oc + 1) * 128],
                                       x2q[:, oc * 512:(oc + 1) * 512],
                                       start=(oc == 0), stop=(oc == NOC - 1))
                    return ins
                ev_mm = b.do("tensor", mm2, mark="pe")
                w_free[dpc % 2] = ev_mm
                if dpc == ND - 1:
                    x2_free = ev_mm
                b.waits("vector", [ev_mm, prev_sl])
                ev_e = b.do("vector", lambda v, ps=ps: v.tensor_tensor(
                    mtmp[:], ps[:], c2bc[:], ALU.mult), mark="dv")
                ev_evM[dpc % 2] = ev_e
                b.waits("scalar", [ev_e, prev_tg])
                ev_sl = b.do("scalar", lambda s, dpc=dpc: s.activation(
                    gb_[:], mtmp[:], AF.Silu,
                    bias=b2cols[:, dpc:dpc + 1]), mark="ac")
                prev_sl = ev_sl
                b.waits("tensor", [ev_sl, ev_gev])
                ev_tg = b.do("tensor", lambda t: (
                    [t.transpose(pTg[:, j * 128:(j + 1) * 128],
                                 gb_[:, j * 128:(j + 1) * 128], identb[:])
                     for j in range(4)])[-1], mark="pe")
                prev_tg = ev_tg
                b.wait("vector", ev_tg)

                def evg(v, dpc=dpc):
                    dst = gq[:].rearrange("p (tl d) -> p tl d", tl=4)[
                        :, :, dpc * 128:(dpc + 1) * 128]
                    return v.tensor_copy(
                        dst, pTg[:].rearrange("p (j t) -> p j t", j=4))
                ev_gev = b.do("vector", evg, mark="dv")
                b.wait("tensor", ev_gev)
            comb_free = None
            for tl in range(4):
                qt = quar * 4 + tl
                b.wait("sync", comb_free)
                ev_xr = b.dma("sync", xcb[:],
                              x_h[qt * 128:(qt + 1) * 128, :])
                ev_sr = b.dma("sync", scb2[:], s_sc[qt])
                b.waits("vector", [ev_xr, ev_sr, ev_gev])

                def comb(v, tl=tl):
                    v.drain()
                    v.tensor_mul(sg32[:], scb2[:],
                                 gq[:, tl * D:(tl + 1) * D])
                    v.drain()
                    return v.tensor_add(ocb[:], xcb[:], sg32[:])
                ev_c = b.do("vector", comb, mark="dv")
                comb_free = ev_c
                b.wait("gpsimd", ev_c)
                ev_y = b.dma("gpsimd", y_h[qt * 128:(qt + 1) * 128, :],
                             ocb[:])
                b.wait("vector", ev_y)
        pm.close()
        pm_ps.close()

    def fin(g):
        for s_ in ("dg0", "dg1", "dg2", "dg3"):
            g.wait_ge(b.semh[s_], b.cnt.get(s_, 0))
    b.emit("gpsimd", fin)

    phase_ranges = []

    def run_ops(eng, key):
        cur = None
        start = nc.n_instructions()
        for ph, f in b.ops[key]:
            if ph != cur:
                if cur is not None:
                    phase_ranges.append((key, cur, start,
                                         nc.n_instructions()))
                cur = ph
                start = nc.n_instructions()
            f(eng)
        if cur is not None:
            phase_ranges.append((key, cur, start, nc.n_instructions()))

    with nc.Block() as block:
        @block.sync
        def _(eng):
            run_ops(eng, "sync")

        @block.tensor
        def _(eng):
            run_ops(eng, "tensor")

        @block.vector
        def _(eng):
            run_ops(eng, "vector")

        @block.scalar
        def _(eng):
            run_ops(eng, "scalar")

        @block.gpsimd
        def _(eng):
            run_ops(eng, "gpsimd")

    ctx.close()
    nc._phase_ranges = phase_ranges
    return nc


def _rope_tables():
    e_ = np.arange(0, 128, 2, dtype=np.float32) / np.float32(128.0)
    p_ = np.power(np.float32(10000.0), e_).astype(np.float32)
    invf = (np.float32(1.0) / p_).astype(np.float32)
    pos = np.arange(T, dtype=np.float32)
    freqs = (pos[:, None] * invf[None, :]).astype(np.float32)
    emb = np.concatenate([freqs, freqs], axis=1)
    cos = np.cos(emb).astype(np.float32).T.copy()
    sin = np.sin(emb).astype(np.float32).T.copy()
    ss = sin.copy()
    ss[0:64] = -sin[0:64]
    return np.ascontiguousarray(cos), np.ascontiguousarray(ss)


def kernel(L0_attn_u, L0_attn_W_in, L0_attn_W_out, L0_ffn_w1, L0_ffn_b1,
           L0_ffn_ln_g, L0_ffn_ln_b, L0_ffn_w2, L0_ffn_b2):
    nb = L0_attn_u.shape[0]
    wkvq = np.ascontiguousarray(np.asarray(L0_attn_W_in).T[:, 4096:10240],
                                dtype=np.float32)
    woutT = np.ascontiguousarray(np.asarray(L0_attn_W_out).T,
                                 dtype=np.float32)
    w1T = np.ascontiguousarray(np.asarray(L0_ffn_w1).T, dtype=np.float32)
    w2T = np.ascontiguousarray(np.asarray(L0_ffn_w2).T, dtype=np.float32)
    b1 = np.asarray(L0_ffn_b1, dtype=np.float32).reshape(1, DI)
    lng = np.asarray(L0_ffn_ln_g, dtype=np.float32).reshape(1, DI)
    lnb = np.asarray(L0_ffn_ln_b, dtype=np.float32).reshape(1, DI)
    b2 = np.asarray(L0_ffn_b2, dtype=np.float32).reshape(1, D)
    cosT, sinT = _rope_tables()
    base = np.where(np.arange(128)[:, None] > np.arange(128)[None, :],
                    np.float32(-1e30), np.float32(0.0)).astype(np.float32)
    trimask = np.ascontiguousarray(np.tile(base, (1, 4)))
    ident = np.eye(128, dtype=np.float32)

    nc = build_nc(debug=DEBUG, upto=UPTO)
    in_maps = []
    for i in range(nb):
        in_maps.append(dict(
            x=np.ascontiguousarray(np.asarray(L0_attn_u[i],
                                              dtype=np.float32)),
            wkvq=wkvq, woutT=woutT, w1T=w1T, w2T=w2T, b1=b1, lng=lng,
            lnb=lnb, b2=b2, cosT=cosT, sinT=sinT, trimask=trimask,
            ident=ident))
    res = run_bass_kernel_spmd(nc, in_maps, core_ids=list(range(nb)))
    out = np.stack([np.asarray(r["y"], dtype=np.float32)
                    for r in res.results], axis=0)
    if DEBUG:
        kernel.last_results = res
    return out


# version 03
# speedup vs baseline: 1.0874x; 1.0874x over previous
import numpy as np
import ml_dtypes
from contextlib import ExitStack
import concourse.bass as bass
import concourse.mybir as mybir
from concourse.bass_utils import run_bass_kernel_spmd

F32 = mybir.dt.float32
BF16 = mybir.dt.bfloat16
AF = mybir.ActivationFunctionType
ALU = mybir.AluOpType
AX = mybir.AxisListType

MAGIC = 12582912.0
SC_ATT = 0.08838834764831845
NEG_HALF = -0.5
EPS = 1e-5
T, D, DI = 2048, 2048, 8192
NT, ND = 16, 16
NKV = 8
NOB, NOC = 16, 64

DEBUG = False
UPTO = 99
ENGS = ("sync", "tensor", "vector", "scalar", "gpsimd")


class B:
    def __init__(self):
        self.ops = {k: [] for k in ENGS}
        self.cnt = {}
        self.semh = {}
        self.lastw = {e: {} for e in ENGS}
        self.phase = "init"

    def emit(self, eng, fn):
        self.ops[eng].append((self.phase, fn))

    def wait(self, eng, evt):
        if evt is None:
            return
        sem, val, peng = evt
        if peng == eng or val <= 0:
            return
        if self.lastw[eng].get(sem, 0) >= val:
            return
        self.lastw[eng][sem] = val

        def f(e, sem=sem, val=val):
            e.wait_ge(self.semh[sem], val)
        self.emit(eng, f)

    def waits(self, eng, evts):
        for ev in evts:
            self.wait(eng, ev)

    def bump(self, sem, k):
        self.cnt[sem] = self.cnt.get(sem, 0) + k
        return self.cnt[sem]

    DMA_POOL = {"sync": ["dl0", "dl1", "dl2", "dl3"],
                "scalar": ["ds0", "ds1"],
                "gpsimd": ["dg0", "dg1", "dg2", "dg3"]}

    def dma(self, eng, dst, src):
        pool = self.DMA_POOL[eng]
        k = self.cnt.get("_rr_" + eng, 0)
        self.cnt["_rr_" + eng] = k + 1
        sem = pool[k % len(pool)]
        before = self.cnt.get(sem, 0)
        self.wait(eng, (sem, before, None))
        val = self.bump(sem, 16)

        def f(e, dst=dst, src=src, sem=sem):
            e.dma_start(out=dst, in_=src).then_inc(self.semh[sem], 16)
        self.emit(eng, f)
        return (sem, val, None)

    def do(self, eng, fn, mark=None):
        if mark is None:
            self.emit(eng, lambda e, fn=fn: fn(e))
            return None
        val = self.bump(mark, 1)

        def f(e, fn=fn, mark=mark):
            fn(e).then_inc(self.semh[mark], 1)
        self.emit(eng, f)
        return (mark, val, eng)

    def barrier(self):
        for eng, pool in self.DMA_POOL.items():
            for sem in pool:
                self.wait(eng, (sem, self.cnt.get(sem, 0), None))
        tgt = self.bump("bar", len(ENGS))
        for eng in ENGS:
            def f(e):
                try:
                    e.drain()
                except Exception:
                    pass
                e.nop().then_inc(self.semh["bar"], 1)
            self.emit(eng, f)
        for eng in ENGS:
            def f(e, tgt=tgt):
                e.wait_ge(self.semh["bar"], tgt)
            self.emit(eng, f)
            self.lastw[eng]["bar"] = tgt


def build_nc(debug=False, upto=99):
    nc = bass.Bass("TRN2", target_bir_lowering=False, debug=False)
    b = B()

    def inp(name, shape, dtype=F32):
        return nc.dram_tensor(name, shape, dtype, kind="ExternalInput").ap()

    x_h = inp("x", [T, D])
    wkvq_h = inp("wkvq", [D, 6144])
    wout_h = inp("woutT", [4096, D])
    w1_h = inp("w1T", [D, DI])
    w2_h = inp("w2T", [DI, D])
    b1_h = inp("b1", [1, DI])
    g2_h = inp("lng", [1, DI])
    bb2_h = inp("lnb", [1, DI])
    b2_h = inp("b2", [1, D])
    cos_h = inp("cosT", [128, T])
    sin_h = inp("sinT", [128, T])
    tri_h = inp("trimask", [128, 512])
    id_h = inp("ident", [128, 128])
    y_h = nc.dram_tensor("y", [T, D], F32, kind="ExternalOutput").ap()

    skind = "ExternalOutput" if debug else "Internal"

    def scr(name, shape, dtype):
        return nc.dram_tensor(name, shape, dtype, kind=skind).ap()

    xq1_sc = scr("xq1_sc", [NT, ND, 128, 128], BF16)
    q_sc = scr("q_sc", [NKV, 128, NT, 4, 128], BF16)
    at_sc = scr("at_sc", [NKV, 128, NT, 512], BF16)
    s_sc = scr("s_sc", [NT, 128, D], BF16)
    h_sc = scr("h_sc", [NT, 128, DI], F32)
    xq2_sc = scr("xq2_sc", [NOC, 128, D], BF16)
    w1t_sc = scr("w1t_sc", [NOB, ND, 128, 512], BF16)
    w2t_sc = scr("w2t_sc", [ND, NOC, 128, 128], BF16)
    dbg = {}
    if debug:
        for nm, sh, dt in (("d_xnfm", [128, ND * T], BF16),
                           ("d_kfm", [128, NKV * T], BF16),
                           ("d_vkm", [128, NKV * T], BF16),
                           ("d_stats", [128, 64], F32),
                           ("d_wsc", [128, 8], F32)):
            dbg[nm] = nc.dram_tensor(nm, sh, dt, kind="ExternalOutput").ap()

    ctx = ExitStack()
    e = ctx.enter_context
    for s in ("dl0", "dl1", "dl2", "dl3", "ds0", "ds1", "dg0", "dg1",
              "dg2", "dg3", "pe", "dv", "ac", "gp", "bar"):
        b.semh[s] = e(nc.semaphore(s))

    cpool = e(nc.sbuf_tensor("cpool", [128, 4], F32))
    nc.const_aps.register(F32, SC_ATT, cpool[:, 0:1])
    nc.const_aps.register(F32, NEG_HALF, cpool[:, 1:2])
    onec = e(nc.sbuf_tensor("onec", [1, 128], F32))
    onecol = e(nc.sbuf_tensor("onecol", [128, 1], F32))
    one11 = e(nc.sbuf_tensor("one11", [1, 1], F32))
    ones_bf = e(nc.sbuf_tensor("ones_bf", [128, 128], BF16))
    identb = e(nc.sbuf_tensor("identb", [128, 128], BF16))
    identf = e(nc.sbuf_tensor("identf", [128, 128], F32))
    tri = e(nc.sbuf_tensor("tri", [128, 512], F32))
    m1c = e(nc.sbuf_tensor("m1c", [128, NT], F32))
    c1c = e(nc.sbuf_tensor("c1c", [128, NT], F32))
    c2c = e(nc.sbuf_tensor("c2c", [128, NT], F32))
    wsc = e(nc.sbuf_tensor("wsc", [128, 8], F32))
    b2cols = e(nc.sbuf_tensor("b2cols", [128, ND], F32))

    b.dma("sync", tri[:], tri_h[:, :])
    ev_idf = b.dma("sync", identf[:], id_h[:, :])

    def init_consts(v):
        v.memset(onec[:], 1.0)
        v.memset(onecol[:], 1.0)
        v.memset(one11[:], 1.0)
        v.memset(ones_bf[:], 1.0)
        v.memset(cpool[:, 0:1], SC_ATT)
        return v.memset(cpool[:, 1:2], NEG_HALF)
    ev_cst = b.do("vector", init_consts, mark="dv")
    b.wait("vector", ev_idf)
    ev_idb = b.do("vector", lambda v: v.tensor_copy(identb[:], identf[:]),
                  mark="dv")

    xn_ctx = ExitStack()
    xn_fm = xn_ctx.enter_context(nc.sbuf_tensor("xn_fm", [128, ND * T], BF16))
    rope_ctx = ExitStack()
    er = rope_ctx.enter_context
    cosT = er(nc.sbuf_tensor("cosT_sb", [128, T], F32))
    ssT = er(nc.sbuf_tensor("ssT_sb", [128, T], F32))
    b.dma("sync", cosT[:], cos_h[:, :])
    b.dma("sync", ssT[:], sin_h[:, :])

    b.phase = "A"
    if upto >= 1:
        pa = ExitStack()
        ea = pa.enter_context
        xld = [ea(nc.sbuf_tensor(f"xld{j}", [128, D], F32)) for j in range(2)]
        xn32 = ea(nc.sbuf_tensor("xn32", [128, D], F32))
        tq32 = ea(nc.sbuf_tensor("tq32", [128, D], F32))
        xnbf = ea(nc.sbuf_tensor("xnbf", [128, D], BF16))
        xqbf = ea(nc.sbuf_tensor("xqbf", [128, D], BF16))
        xqst = ea(nc.sbuf_tensor("xqst", [128, D], BF16))
        trashA = ea(nc.sbuf_tensor("trashA", [128, D], BF16))
        stc = ea(nc.sbuf_tensor("stc", [128, 20], F32))
        pA_ps = ExitStack()
        pT = pA_ps.enter_context(nc.psum_tensor("pTA", [128, 512], BF16))
        pT2 = pA_ps.enter_context(nc.psum_tensor("pTA2", [128, 512], BF16))

        (SU, SQ, MU, EX2, MU2, VAR, VEPS, LNV, INV, I2, MX2, M2E, LNM, RST,
         MX, QS, RM) = range(17)

        xld_free = [None, None]
        st_free = None
        for i in range(NT):
            buf = xld[i % 2]
            b.wait("sync", xld_free[i % 2])
            ev_x = b.dma("sync", buf[:], x_h[i * 128:(i + 1) * 128, :])
            b.wait("vector", ev_x)
            ev_ms = b.do("vector", lambda v, bu=buf: (
                v.drain(),
                v.reduce_sum(stc[:, SU:SU + 1], bu[:], AX.X),
                v.memset(stc[:, SQ:SQ + 1], 0.0))[-1], mark="dv")
            b.waits("scalar", [ev_x, ev_ms])
            ev_sq = b.do("scalar", lambda s, bu=buf: s.activation(
                trashA[:], bu[:], AF.Square, accum_out=stc[:, SQ:SQ + 1]),
                mark="ac")
            b.wait("vector", ev_sq)

            def stats1(v):
                v.tensor_scalar_mul(stc[:, MU:MU + 1], stc[:, SU:SU + 1],
                                    1.0 / D)
                v.tensor_scalar_mul(stc[:, EX2:EX2 + 1], stc[:, SQ:SQ + 1],
                                    1.0 / D)
                v.drain()
                v.tensor_mul(stc[:, MU2:MU2 + 1], stc[:, MU:MU + 1],
                             stc[:, MU:MU + 1])
                v.drain()
                v.tensor_sub(stc[:, VAR:VAR + 1], stc[:, EX2:EX2 + 1],
                             stc[:, MU2:MU2 + 1])
                v.drain()
                return v.tensor_scalar_add(stc[:, VEPS:VEPS + 1],
                                           stc[:, VAR:VAR + 1], EPS)
            ev_v1 = b.do("vector", stats1, mark="dv")
            b.wait("scalar", ev_v1)

            def act_inv(s):
                s.activation(stc[:, LNV:LNV + 1], stc[:, VEPS:VEPS + 1], AF.Ln)
                s.drain()
                return s.activation(stc[:, INV:INV + 1], stc[:, LNV:LNV + 1],
                                    AF.Exp, scale=NEG_HALF)
            ev_inv = b.do("scalar", act_inv, mark="ac")
            b.wait("vector", ev_inv)

            def xnf(v, bu=buf):
                v.drain()
                v.tensor_scalar(xn32[:], bu[:], stc[:, MU:MU + 1],
                                stc[:, INV:INV + 1], ALU.subtract, ALU.mult)
                v.tensor_mul(stc[:, I2:I2 + 1], stc[:, INV:INV + 1],
                             stc[:, INV:INV + 1])
                v.drain()
                v.tensor_copy(xnbf[:], xn32[:])
                v.tensor_mul(stc[:, MX2:MX2 + 1], stc[:, VAR:VAR + 1],
                             stc[:, I2:I2 + 1])
                v.reduce_max(stc[:, MX:MX + 1], xn32[:], AX.X,
                             apply_absolute_value=True)
                v.drain()
                return v.tensor_scalar_add(stc[:, M2E:M2E + 1],
                                           stc[:, MX2:MX2 + 1], EPS)
            ev_v2 = b.do("vector", xnf, mark="dv")
            xld_free[i % 2] = ev_v2
            b.wait("scalar", ev_v2)

            def act_rst(s):
                s.activation(stc[:, LNM:LNM + 1], stc[:, M2E:M2E + 1], AF.Ln)
                s.drain()
                return s.activation(stc[:, RST:RST + 1], stc[:, LNM:LNM + 1],
                                    AF.Exp, scale=NEG_HALF)
            ev_rst = b.do("scalar", act_rst, mark="ac")
            b.wait("vector", ev_rst)

            def quant(v, i=i):
                v.tensor_mul(m1c[:, i:i + 1], stc[:, RST:RST + 1],
                             stc[:, MX:MX + 1])
                v.drain()
                v.tensor_scalar_max(m1c[:, i:i + 1], m1c[:, i:i + 1], EPS)
                v.drain()
                v.reciprocal(stc[:, RM:RM + 1], m1c[:, i:i + 1])
                v.drain()
                v.tensor_mul(stc[:, QS:QS + 1], stc[:, RST:RST + 1],
                             stc[:, RM:RM + 1])
                v.drain()
                v.tensor_scalar_mul(stc[:, QS:QS + 1], stc[:, QS:QS + 1],
                                    127.0)
                v.drain()
                v.tensor_scalar_mul(tq32[:], xn32[:], stc[:, QS:QS + 1])
                v.drain()
                return v.tensor_scalar(xqbf[:], tq32[:], MAGIC, MAGIC,
                                       ALU.add, ALU.subtract)
            ev_q = b.do("vector", quant, mark="dv")

            b.waits("tensor", [ev_v2, ev_idb])
            for gdc in range(4):
                def tr4(t, gdc=gdc):
                    ins = None
                    for j in range(4):
                        dc = gdc * 4 + j
                        ins = t.transpose(pT[:, j * 128:(j + 1) * 128],
                                          xnbf[:, dc * 128:(dc + 1) * 128],
                                          identb[:])
                    return ins
                ev_t = b.do("tensor", tr4, mark="pe")
                b.wait("vector", ev_t)

                def evac(v, gdc=gdc, i=i):
                    dst = xn_fm[:].rearrange("p (dc t) -> p dc t", dc=ND)[
                        :, gdc * 4:(gdc + 1) * 4, i * 128:(i + 1) * 128]
                    return v.tensor_copy(
                        dst, pT[:].rearrange("p (j t) -> p j t", j=4))
                ev_ev = b.do("vector", evac, mark="dv")
                b.wait("tensor", ev_ev)
            b.waits("tensor", [ev_q, st_free])
            last_evq = None
            for gdc in range(4):
                def trq(t, gdc=gdc):
                    ins = None
                    for j in range(4):
                        dc = gdc * 4 + j
                        ins = t.transpose(pT2[:, j * 128:(j + 1) * 128],
                                          xqbf[:, dc * 128:(dc + 1) * 128],
                                          identb[:])
                    return ins
                ev_t = b.do("tensor", trq, mark="pe")
                b.wait("vector", ev_t)

                def evacq(v, gdc=gdc):
                    dst = xqst[:].rearrange("p (dc t) -> p dc t", dc=ND)[
                        :, gdc * 4:(gdc + 1) * 4, :]
                    return v.tensor_copy(
                        dst, pT2[:].rearrange("p (j t) -> p j t", j=4))
                last_evq = b.do("vector", evacq, mark="dv")
                b.wait("tensor", last_evq)
            b.wait("gpsimd", last_evq)
            st_free = b.dma("gpsimd",
                            xq1_sc[i].rearrange("dc d t -> d dc t"),
                            xqst[:].rearrange("p (dc t) -> p dc t", dc=ND))
        if debug:
            b.wait("gpsimd", ("dv", b.cnt.get("dv", 0), None))
            b.dma("gpsimd", dbg["d_xnfm"][:, :], xn_fm[:])
            b.dma("gpsimd", dbg["d_stats"][:, 0:NT], m1c[:])
        b.barrier()
        pa.close()
        pA_ps.close()

    b.phase = "W"
    if upto >= 2:
        pw = ExitStack()
        ew = pw.enter_context
        wch = [ew(nc.sbuf_tensor(f"wch{j}", [128, DI], F32)) for j in range(2)]
        acc = ew(nc.sbuf_tensor("wacc", [128, 4], F32))
        ternb = [ew(nc.sbuf_tensor(f"tern{j}", [128, DI], BF16))
                 for j in range(2)]
        negb = ew(nc.sbuf_tensor("negb", [128, DI], BF16))
        srow = ew(nc.sbuf_tensor("srow", [1, 8], F32))
        pw_ps = ExitStack()
        pW = pw_ps.enter_context(nc.psum_tensor("pW", [1, 2], F32))
        pC = pw_ps.enter_context(nc.psum_tensor("pC", [128, 8], F32))

        ch_free = [None, None]
        tern_free = [None, None]
        for wi, (w_h, nch, F, nelem) in enumerate(
                ((w1_h, 16, DI, D * DI), (w2_h, 64, D, DI * D))):
            ev_z = b.do("vector", lambda v, wi=wi: v.memset(
                acc[:, wi:wi + 1], 0.0), mark="dv")
            for c in range(nch):
                buf = wch[c % 2]
                b.wait("sync", ch_free[c % 2])
                ev_w = b.dma("sync", buf[:, 0:F],
                             w_h[c * 128:(c + 1) * 128, :])
                b.waits("vector", [ev_w, ev_z])

                def statw(v, buf=buf, F=F, wi=wi):
                    v.drain()
                    v.tensor_reduce(acc[:, 2:3], buf[:, 0:F], AX.X, ALU.add,
                                    apply_absolute_value=True)
                    v.drain()
                    return v.tensor_add(acc[:, wi:wi + 1], acc[:, wi:wi + 1],
                                        acc[:, 2:3])
                ch_free[c % 2] = b.do("vector", statw, mark="dv")
            b.waits("tensor", [ch_free[(nch - 1) % 2], ev_cst])
            ev_tot = b.do("tensor", lambda t, wi=wi: t.matmul(
                pW[0:1, wi:wi + 1], acc[:, wi:wi + 1], onecol[:, 0:1],
                start=True, stop=True), mark="pe")
            b.wait("vector", ev_tot)

            def scals(v, wi=wi, nelem=nelem):
                v.tensor_scalar_mul(srow[0:1, 4 * wi:4 * wi + 1],
                                    pW[0:1, wi:wi + 1], 1.0 / nelem)
                v.drain()
                v.tensor_scalar_max(srow[0:1, 4 * wi + 3:4 * wi + 4],
                                    srow[0:1, 4 * wi:4 * wi + 1], EPS)
                v.drain()
                v.tensor_scalar_mul(srow[0:1, 4 * wi + 1:4 * wi + 2],
                                    srow[0:1, 4 * wi + 3:4 * wi + 4], 0.5)
                return v.tensor_scalar_mul(
                    srow[0:1, 4 * wi + 2:4 * wi + 3],
                    srow[0:1, 4 * wi + 3:4 * wi + 4], -0.5)
            ev_sc = b.do("vector", scals, mark="dv")
            b.wait("tensor", ev_sc)
            ev_bc = b.do("tensor", lambda t, wi=wi: t.matmul(
                pC[:, 3 * wi:3 * wi + 3],
                onec[0:1, :],
                srow[0:1, 4 * wi + 1:4 * wi + 4],
                start=True, stop=True), mark="pe")
            b.wait("vector", ev_bc)
            ev_wsc = b.do("vector", lambda v, wi=wi: (
                v.tensor_copy(wsc[:, 3 * wi:3 * wi + 3],
                              pC[:, 3 * wi:3 * wi + 3]),
                v.drain())[0], mark="dv")
            for c in range(nch):
                buf = wch[c % 2]
                b.wait("sync", ch_free[c % 2])
                ev_w = b.dma("sync", buf[:, 0:F],
                             w_h[c * 128:(c + 1) * 128, :])
                tb = ternb[c % 2]
                b.waits("vector", [ev_w, ev_wsc, tern_free[c % 2]])

                def ternf(v, buf=buf, tb=tb, F=F, wi=wi):
                    v.drain()
                    v.tensor_single_scalar(negb[:, 0:F], buf[:, 0:F],
                                           wsc[:, 3 * wi + 1:3 * wi + 2],
                                           ALU.is_lt)
                    v.drain()
                    return v.scalar_tensor_tensor(
                        tb[:, 0:F], buf[:, 0:F],
                        wsc[:, 3 * wi:3 * wi + 1], negb[:, 0:F],
                        ALU.is_gt, ALU.subtract)
                ev_t = b.do("vector", ternf, mark="dv")
                ch_free[c % 2] = ev_t
                b.wait("gpsimd", ev_t)
                if wi == 0:
                    dst = w1t_sc[:, c, :, :].rearrange("ob d o -> d ob o")
                    src = tb[:, 0:F].rearrange("d (ob o) -> d ob o", ob=NOB)
                else:
                    dst = w2t_sc[:, c, :, :].rearrange("dpc o dp -> o dpc dp")
                    src = tb[:, 0:F].rearrange("o (dpc dp) -> o dpc dp",
                                               dpc=ND)
                tern_free[c % 2] = b.dma("gpsimd", dst, src)
            if wi == 0:
                ev_c1 = b.do("vector", lambda v: (
                    v.drain(),
                    v.tensor_scalar(c1c[:], m1c[:], wsc[:, 2:3], 1.0 / 127.0,
                                    ALU.mult, ALU.mult))[-1], mark="dv")
        if debug:
            b.wait("gpsimd", ("dv", b.cnt.get("dv", 0), None))
            b.dma("gpsimd", dbg["d_wsc"][:, 0:6], wsc[:, 0:6])
        b.barrier()
        pw.close()
        pw_ps.close()

    kv_ctx = ExitStack()
    k_fm = kv_ctx.enter_context(
        nc.sbuf_tensor("k_fm", [128, NKV * T], BF16, side="right"))
    v_km = kv_ctx.enter_context(
        nc.sbuf_tensor("v_km", [128, NKV * T], BF16, side="right"))
    b.phase = "P1kv"
    if upto >= 3:
        p1 = ExitStack()
        e1 = p1.enter_context
        wres = e1(nc.sbuf_tensor("wres1", [128, 16 * 1024], BF16))
        wstg = [e1(nc.sbuf_tensor(f"wstg{j}", [128, 1024], F32))
                for j in range(2)]
        a32 = e1(nc.sbuf_tensor("a32", [128, 1024], F32))
        b32 = e1(nc.sbuf_tensor("b32", [128, 1024], F32))
        vstg = e1(nc.sbuf_tensor("vstg", [128, 1024], BF16))
        p1_ps = ExitStack()
        pA = p1_ps.enter_context(nc.psum_tensor("pA1", [128, 1024], F32))
        pB = p1_ps.enter_context(nc.psum_tensor("pB1", [128, 1024], F32))
        pTv = p1_ps.enter_context(nc.psum_tensor("pTv", [128, 512], BF16))

        last_pe_half = None
        stg_free = [None, None]
        prev_evac = [None, None]
        for half in range(2):
            cast_evs = []
            for dc in range(ND):
                sb = wstg[dc % 2]
                b.wait("sync", stg_free[dc % 2])
                ev_w = b.dma("sync", sb[:],
                             wkvq_h[dc * 128:(dc + 1) * 128,
                                    half * 1024:(half + 1) * 1024])
                b.waits("scalar", [ev_w, last_pe_half])
                ev_c = b.do("scalar", lambda s, sb=sb, dc=dc: s.copy(
                    wres[:, dc * 1024:(dc + 1) * 1024], sb[:]), mark="ac")
                stg_free[dc % 2] = ev_c
                cast_evs.append(ev_c)
            b.waits("tensor", cast_evs)
            unit = 0
            for oc in range(8):
                for tb in range(2):
                    ps = pA if unit % 2 == 0 else pB
                    b.wait("tensor", prev_evac[unit % 2])

                    def mmkv(t, oc=oc, tb=tb, ps=ps):
                        ins = None
                        for dc in range(ND):
                            for nb in range(2):
                                ins = t.matmul(
                                    ps[:, nb * 512:(nb + 1) * 512],
                                    wres[:, dc * 1024 + oc * 128:
                                         dc * 1024 + (oc + 1) * 128],
                                    xn_fm[:, dc * T + tb * 1024 + nb * 512:
                                          dc * T + tb * 1024 +
                                          (nb + 1) * 512],
                                    start=(dc == 0), stop=(dc == ND - 1))
                        return ins
                    ev_mm = b.do("tensor", mmkv, mark="pe")
                    b.wait("vector", ev_mm)
                    if half == 0:
                        ev_cp = b.do("vector", lambda v, ps=ps:
                                     v.tensor_copy(vstg[:], ps[:]),
                                     mark="dv")
                        prev_evac[unit % 2] = ev_cp
                        b.wait("tensor", ev_cp)
                        for grp in range(2):
                            def trv(t, grp=grp):
                                ins = None
                                for j in range(4):
                                    ins = t.transpose(
                                        pTv[:, j * 128:(j + 1) * 128],
                                        vstg[:, (grp * 4 + j) * 128:
                                             (grp * 4 + j + 1) * 128],
                                        identb[:])
                                return ins
                            ev_t = b.do("tensor", trv, mark="pe")
                            b.wait("vector", ev_t)

                            def evv(v, oc=oc, tb=tb, grp=grp):
                                kt0 = oc * 16 + tb * 8 + grp * 4
                                return v.tensor_copy(
                                    v_km[:, kt0 * 128:(kt0 + 4) * 128],
                                    pTv[:])
                            ev_e = b.do("vector", evv, mark="dv")
                            b.wait("tensor", ev_e)
                    else:
                        def ropek(v, ps=ps, oc=oc, tb=tb):
                            sl = slice(tb * 1024, (tb + 1) * 1024)
                            v.drain()
                            v.tensor_mul(a32[:], ps[:], cosT[:, sl])
                            v.tensor_mul(b32[0:64, :], ps[64:128, :],
                                         ssT[0:64, sl])
                            v.tensor_mul(b32[64:128, :], ps[0:64, :],
                                         ssT[64:128, sl])
                            v.drain()
                            return v.tensor_add(
                                k_fm[:, oc * T + tb * 1024:
                                     oc * T + (tb + 1) * 1024],
                                a32[:], b32[:])
                        ev_r = b.do("vector", ropek, mark="dv")
                        prev_evac[unit % 2] = ev_r
                    unit += 1
            last_pe_half = ("pe", b.cnt.get("pe", 0), "tensor")
        if debug:
            b.wait("gpsimd", ("dv", b.cnt.get("dv", 0), None))
            b.dma("gpsimd", dbg["d_kfm"][:, :], k_fm[:])
            b.dma("gpsimd", dbg["d_vkm"][:, :], v_km[:])
        b.barrier()
        p1.close()
        p1_ps.close()

    b.phase = "P2q"
    if upto >= 4:
        p2 = ExitStack()
        e2 = p2.enter_context
        wres2 = e2(nc.sbuf_tensor("wres2", [128, 16 * 1024], BF16))
        wstg2 = e2(nc.sbuf_tensor("wstg2", [128, 1024], F32))
        a32q = e2(nc.sbuf_tensor("a32q", [128, 1024], F32))
        b32q = e2(nc.sbuf_tensor("b32q", [128, 1024], F32))
        qstg = [e2(nc.sbuf_tensor(f"qstg{j}", [128, 1024], BF16))
                for j in range(2)]
        p2_ps = ExitStack()
        pA2 = p2_ps.enter_context(nc.psum_tensor("pA2", [128, 1024], F32))
        pB2 = p2_ps.enter_context(nc.psum_tensor("pB2", [128, 1024], F32))

        last_pe_qr = None
        stg_free = None
        prev_rope = [None, None]
        st_free = [None, None]
        for qr in range(4):
            cast_evs = []
            for dc in range(ND):
                b.wait("sync", stg_free)
                ev_w = b.dma("sync", wstg2[:],
                             wkvq_h[dc * 128:(dc + 1) * 128,
                                    2048 + qr * 1024:2048 + (qr + 1) * 1024])
                b.waits("scalar", [ev_w, last_pe_qr])
                ev_c = b.do("scalar", lambda s, dc=dc: s.copy(
                    wres2[:, dc * 1024:(dc + 1) * 1024], wstg2[:]),
                    mark="ac")
                stg_free = ev_c
                cast_evs.append(ev_c)
            b.waits("tensor", cast_evs)
            unit = 0
            for oc in range(8):
                h = qr * 8 + oc
                for tb in range(2):
                    ps = pA2 if unit % 2 == 0 else pB2
                    b.wait("tensor", prev_rope[unit % 2])

                    def mmq(t, oc=oc, tb=tb, ps=ps):
                        ins = None
                        for dc in range(ND):
                            for nb in range(2):
                                ins = t.matmul(
                                    ps[:, nb * 512:(nb + 1) * 512],
                                    wres2[:, dc * 1024 + oc * 128:
                                          dc * 1024 + (oc + 1) * 128],
                                    xn_fm[:, dc * T + tb * 1024 + nb * 512:
                                          dc * T + tb * 1024 +
                                          (nb + 1) * 512],
                                    start=(dc == 0), stop=(dc == ND - 1))
                        return ins
                    ev_mm = b.do("tensor", mmq, mark="pe")
                    qs_buf = qstg[unit % 2]
                    b.waits("vector", [ev_mm, st_free[unit % 2]])

                    def ropeq(v, ps=ps, tb=tb, qs_buf=qs_buf):
                        sl = slice(tb * 1024, (tb + 1) * 1024)
                        v.drain()
                        v.tensor_mul(a32q[:], ps[:], cosT[:, sl])
                        v.tensor_mul(b32q[0:64, :], ps[64:128, :],
                                     ssT[0:64, sl])
                        v.tensor_mul(b32q[64:128, :], ps[0:64, :],
                                     ssT[64:128, sl])
                        v.drain()
                        return v.tensor_add(qs_buf[:], a32q[:], b32q[:])
                    ev_r = b.do("vector", ropeq, mark="dv")
                    prev_rope[unit % 2] = ev_r
                    b.wait("gpsimd", ev_r)
                    st_free[unit % 2] = b.dma(
                        "gpsimd",
                        q_sc[h // 4, :, tb * 8:(tb + 1) * 8, h % 4, :],
                        qs_buf[:].rearrange("p (qt t) -> p qt t", qt=8))
                    unit += 1
            last_pe_qr = ("pe", b.cnt.get("pe", 0), "tensor")
        b.barrier()
        p2.close()
        p2_ps.close()
    rope_ctx.close()
    xn_ctx.close()

    b.phase = "P3flash"
    if upto >= 5:
        p3 = ExitStack()
        e3 = p3.enter_context
        qbuf = [e3(nc.sbuf_tensor(f"qbuf{j}", [128, 4096], BF16))
                for j in range(2)]
        Pp = [e3(nc.sbuf_tensor(f"Pp{j}", [128, 512], BF16))
              for j in range(2)]
        zrec = e3(nc.sbuf_tensor("zrec", [128, 512], F32))
        atile = [e3(nc.sbuf_tensor(f"atile{j}", [128, 512], BF16))
                 for j in range(2)]
        p3_ps = ExitStack()
        pS = [p3_ps.enter_context(nc.psum_tensor(f"pS{j}", [128, 512], F32))
              for j in range(2)]
        pAV = p3_ps.enter_context(nc.psum_tensor("pAV", [128, 512], F32))
        pZ = p3_ps.enter_context(nc.psum_tensor("pZ", [128, 512], F32))

        q_free = [None, None]
        at_free = [None, None]
        ev_evac = None
        ai = 0
        for qt in range(NT):
            qb = qbuf[qt % 2]
            b.wait("sync", q_free[qt % 2])
            ev_q = b.dma("sync", qb[:].rearrange("p (kvh f) -> p kvh f",
                                                 kvh=NKV),
                         q_sc[:, :, qt, :, :].rearrange(
                             "kvh hd g t -> hd kvh (g t)"))
            last_smm = None
            exp_evs = {}
            pe_z = {}
            g = 0
            for kvh in range(NKV):

                def emit_smm(kt, g, kvh=kvh, qt=qt, qb=qb):
                    par = g % 2
                    b.waits("tensor", [ev_q, exp_evs.get(g - 2), ev_evac])

                    def smm(t, kvh=kvh, kt=kt, par=par, qb=qb):
                        return t.matmul(
                            pS[par][:],
                            k_fm[:, kvh * T + kt * 128:kvh * T +
                                 (kt + 1) * 128],
                            qb[:, kvh * 512:(kvh + 1) * 512],
                            start=True, stop=True)
                    ev_s = b.do("tensor", smm, mark="pe")
                    src_ev = ev_s
                    if kt == qt:
                        b.wait("vector", ev_s)
                        ev_m = b.do("vector", lambda v, par=par:
                                    v.tensor_tensor(pS[par][:], pS[par][:],
                                                    tri[:], ALU.add),
                                    mark="dv")
                        src_ev = ev_m
                    b.waits("scalar", [src_ev, pe_z.get(g - 2)])
                    ev_e = b.do("scalar", lambda s, par=par: s.activation(
                        Pp[par][:], pS[par][:], AF.Exp, scale=SC_ATT),
                        mark="ac")
                    exp_evs[g] = ev_e
                    return ev_s

                def emit_av(kt, g, kvh=kvh, qt=qt):
                    par = g % 2
                    b.wait("tensor", exp_evs[g])
                    b.do("tensor", lambda t, kvh=kvh, kt=kt, qt=qt, par=par:
                         t.matmul(pAV[:],
                                  v_km[:, (kvh * 16 + kt) * 128:
                                       (kvh * 16 + kt + 1) * 128],
                                  Pp[par][:], start=(kt == 0),
                                  stop=(kt == qt)))
                    pe_z[g] = b.do(
                        "tensor", lambda t, kt=kt, qt=qt, par=par:
                        t.matmul(pZ[:], ones_bf[:], Pp[par][:],
                                 start=(kt == 0), stop=(kt == qt)),
                        mark="pe")
                    return pe_z[g]

                for kt in range(qt + 1):
                    last_smm = emit_smm(kt, g + kt)
                    if kt >= 1:
                        emit_av(kt - 1, g + kt - 1)
                ev_zlast = emit_av(qt, g + qt)
                g += qt + 1
                at = atile[ai % 2]
                b.waits("vector", [ev_zlast, at_free[ai % 2]])

                def evfl(v, at=at):
                    v.drain()
                    v.reciprocal(zrec[:], pZ[:])
                    v.drain()
                    return v.tensor_tensor(at[:], pAV[:], zrec[:], ALU.mult)
                ev_evac = b.do("vector", evfl, mark="dv")
                b.wait("gpsimd", ev_evac)
                at_free[ai % 2] = b.dma("gpsimd", at_sc[kvh, :, qt, :], at[:])
                ai += 1
            q_free[qt % 2] = last_smm
        b.barrier()
        p3.close()
        p3_ps.close()
    kv_ctx.close()

    b.phase = "F1"
    if upto >= 6:
        pf = ExitStack()
        ef = pf.enter_context
        xq1b = ef(nc.sbuf_tensor("xq1b", [128, ND * T], BF16))
        w1b = [ef(nc.sbuf_tensor(f"w1b{j}", [128, DI], BF16))
               for j in range(2)]
        b1bc = ef(nc.sbuf_tensor("b1bc", [128, DI], F32))
        b1row_t = ef(nc.sbuf_tensor("b1row_t", [1, DI], F32))
        b1row = b1row_t[0:1, :]
        ev_b1l = b.dma("sync", b1row, b1_h[0:1, :])
        htmp = [ef(nc.sbuf_tensor(f"htmp{j}", [128, 512], F32))
                for j in range(2)]
        hsil = [ef(nc.sbuf_tensor(f"hsil{j}", [128, 512], F32))
                for j in range(2)]
        pf_ps = ExitStack()
        pF = [pf_ps.enter_context(nc.psum_tensor(f"pF{j}", [128, 512], F32))
              for j in range(2)]
        pB5 = pf_ps.enter_context(nc.psum_tensor("pB5", [128, 512], F32))

        ev_xq = []
        for qt in range(NT):
            dst = xq1b[:].rearrange("p (dc t) -> p dc t", dc=ND)[
                :, :, qt * 128:(qt + 1) * 128]
            ev_xq.append(b.dma("sync", dst,
                               xq1_sc[qt].rearrange("dc d t -> d dc t")))
        b.wait("tensor", ev_b1l)
        for ob in range(NOB):
            ev_b = b.do("tensor", lambda t, ob=ob: t.matmul(
                pB5[:], onec[0:1, :], b1row[0:1, ob * 512:(ob + 1) * 512],
                start=True, stop=True), mark="pe")
            b.wait("vector", ev_b)
            ev_v = b.do("vector", lambda v, ob=ob: (
                v.tensor_copy(b1bc[:, ob * 512:(ob + 1) * 512], pB5[:]),
                v.drain())[0], mark="dv")
            b.wait("tensor", ev_v)

        w_free = [None, None]
        ui = 0
        h_free = [None, None]
        ev_evF = [None, None]
        for ob in range(NOB):
            wb_ = w1b[ob % 2]
            b.wait("sync", w_free[ob % 2])
            ev_w = b.dma("sync", wb_[:].rearrange("p (dc o) -> p dc o",
                                                  dc=ND),
                         w1t_sc[ob].rearrange("dc d o -> d dc o"))
            last_mm = None
            for qt in range(NT):
                ps = pF[ui % 2]
                b.waits("tensor", ev_xq + [ev_w, ev_evF[ui % 2]])

                def mmf(t, ob=ob, qt=qt, ps=ps, wb_=wb_):
                    ins = None
                    for dc in range(ND):
                        ins = t.matmul(ps[:],
                                       xq1b[:, dc * T + qt * 128:
                                            dc * T + (qt + 1) * 128],
                                       wb_[:, dc * 512:(dc + 1) * 512],
                                       start=(dc == 0), stop=(dc == ND - 1))
                    return ins
                ev_mm = b.do("tensor", mmf, mark="pe")
                last_mm = ev_mm
                b.waits("vector", [ev_mm, h_free[ui % 2]])

                def evf(v, ps=ps, qt=qt, ob=ob, j=ui % 2):
                    return v.scalar_tensor_tensor(
                        htmp[j][:], ps[:], c1c[:, qt:qt + 1],
                        b1bc[:, ob * 512:(ob + 1) * 512],
                        ALU.mult, ALU.add)
                ev_e = b.do("vector", evf, mark="dv")
                ev_evF[ui % 2] = ev_e
                b.wait("scalar", ev_e)
                ev_s = b.do("scalar", lambda s, j=ui % 2: s.activation(
                    hsil[j][:], htmp[j][:], AF.Silu), mark="ac")
                b.wait("gpsimd", ev_s)
                h_free[ui % 2] = b.dma(
                    "gpsimd", h_sc[qt, :, ob * 512:(ob + 1) * 512],
                    hsil[ui % 2][:])
                ui += 1
            w_free[ob % 2] = last_mm
        b.barrier()
        pf.close()
        pf_ps.close()

    b.phase = "P4out"
    if upto >= 7:
        p4 = ExitStack()
        e4 = p4.enter_context
        woutb = e4(nc.sbuf_tensor("woutb", [128, 32 * D], BF16))
        wst4 = [e4(nc.sbuf_tensor(f"wst4{j}", [128, D], F32))
                for j in range(2)]
        abuf = [e4(nc.sbuf_tensor(f"abuf{j}", [128, 4096], BF16))
                for j in range(2)]
        sst = [e4(nc.sbuf_tensor(f"sst{j}", [128, D], BF16))
               for j in range(2)]
        p4_ps = ExitStack()
        pO = [p4_ps.enter_context(nc.psum_tensor(f"pO{j}", [128, 512], F32))
              for j in range(2)]

        stg_free = [None, None]
        for h in range(32):
            sb = wst4[h % 2]
            b.wait("sync", stg_free[h % 2])
            ev_w = b.dma("sync", sb[:], wout_h[h * 128:(h + 1) * 128, :])
            b.wait("scalar", ev_w)
            ev_c = b.do("scalar", lambda s, sb=sb, h=h: s.copy(
                woutb[:, h * D:(h + 1) * D], sb[:]), mark="ac")
            stg_free[h % 2] = ev_c
        cast_done = stg_free[31 % 2]
        a_free = [None, None]
        s_free = [None, None]
        ev_ev4 = [None, None]
        for qt in range(NT):
            ab = abuf[qt % 2]
            b.wait("sync", a_free[qt % 2])
            ev_a = b.dma("sync", ab[:].rearrange("p (kvh f) -> p kvh f",
                                                 kvh=NKV),
                         at_sc[:, :, qt, :].rearrange("kvh hd f -> hd kvh f"))
            ss = sst[qt % 2]
            b.wait("vector", s_free[qt % 2])
            last_mm = None
            for dpb in range(4):
                ps = pO[dpb % 2]
                b.waits("tensor", [ev_a, cast_done, ev_ev4[dpb % 2]])

                def mmo(t, dpb=dpb, ps=ps, ab=ab):
                    ins = None
                    for h in range(32):
                        ins = t.matmul(
                            ps[:],
                            ab[:, (h // 4) * 512 + (h % 4) * 128:
                               (h // 4) * 512 + (h % 4 + 1) * 128],
                            woutb[:, h * D + dpb * 512:
                                  h * D + (dpb + 1) * 512],
                            start=(h == 0), stop=(h == 31))
                    return ins
                ev_mm = b.do("tensor", mmo, mark="pe")
                last_mm = ev_mm
                b.wait("vector", ev_mm)
                ev_e = b.do("vector", lambda v, ps=ps, dpb=dpb, ss=ss:
                            v.tensor_copy(ss[:, dpb * 512:(dpb + 1) * 512],
                                          ps[:]), mark="dv")
                ev_ev4[dpb % 2] = ev_e
            a_free[qt % 2] = last_mm
            b.wait("gpsimd", ev_ev4[1])
            b.wait("gpsimd", ev_ev4[0])
            s_free[qt % 2] = b.dma("gpsimd", s_sc[qt], ss[:])
        b.barrier()
        p4.close()
        p4_ps.close()

    b.phase = "F2"
    if upto >= 8:
        pg = ExitStack()
        eg = pg.enter_context
        gbc = eg(nc.sbuf_tensor("gbc", [128, DI], F32))
        bbc = eg(nc.sbuf_tensor("bbc", [128, DI], F32))
        stq = eg(nc.sbuf_tensor("stq", [128, 34], F32))
        pg_ps = ExitStack()
        pG = pg_ps.enter_context(nc.psum_tensor("pG", [128, 512], F32))
        pT3 = [pg_ps.enter_context(nc.psum_tensor(f"pT3{j}", [128, 512],
                                                  BF16)) for j in range(2)]

        rows_ctx = ExitStack()
        growb = rows_ctx.enter_context(
            nc.sbuf_tensor("growb", [1, 2 * DI], F32))
        ev_gl = b.dma("sync", growb[0:1, 0:DI], g2_h[0:1, :])
        ev_bl = b.dma("sync", growb[0:1, DI:2 * DI], bb2_h[0:1, :])
        b.waits("tensor", [ev_gl, ev_bl])
        ev_gb_done = None
        for ri, dstt in ((0, gbc), (1, bbc)):
            for ob in range(NOB):
                ev_b = b.do("tensor", lambda t, ob=ob, ri=ri:
                            t.matmul(pG[:], onec[0:1, :],
                                     growb[0:1, ri * DI + ob * 512:
                                           ri * DI + (ob + 1) * 512],
                                     start=True, stop=True), mark="pe")
                b.wait("vector", ev_b)
                ev_v = b.do("vector", lambda v, ob=ob, dstt=dstt: (
                    v.tensor_copy(dstt[:, ob * 512:(ob + 1) * 512], pG[:]),
                    v.drain())[0], mark="dv")
                b.wait("tensor", ev_v)
                ev_gb_done = ev_v
        rows_ctx.close()
        hb = eg(nc.sbuf_tensor("hbF", [128, DI], F32))
        ab2 = [eg(nc.sbuf_tensor(f"ab2{j}", [128, DI], F32))
               for j in range(2)]
        xqt2 = [eg(nc.sbuf_tensor(f"xqt2{j}", [128, DI], BF16))
                for j in range(2)]

        (ASUM, ASQ, MU2c, EX2c, VARc, VEPSc, LNc, INVc, SQ2, EXRc, LNRc,
         RST2, T2c, M2c, RM2c, QS2c, NMIc) = range(17)

        h_free = [ev_gb_done, ev_gb_done]
        st2_free = [None, None]
        pool_prev = None
        sq2_prev = None
        hb_free = [None]
        for qt in range(NT):
            j = qt % 2
            B0 = 17 * j
            ab = ab2[j]
            xq = xqt2[j]

            def C(i):
                return stq[:, B0 + i:B0 + i + 1]
            b.waits("sync", [h_free[j], hb_free[0]])
            ev_h = b.dma("sync", hb[:], h_sc[qt])
            ev_ms = b.do("vector", lambda v, C=C: (
                v.memset(C(ASUM), 0.0),
                v.memset(C(ASQ), 0.0),
                v.memset(C(SQ2), 0.0))[-1], mark="dv")
            b.waits("scalar", [ev_h, ev_ms, h_free[j], st2_free[j]])

            def gelu_sq(s, C=C, ab=ab, xq=xq):
                s.activation(ab[:], hb[:], AF.Gelu, accum_out=C(ASUM))
                s.drain()
                return s.activation(xq[:], ab[:], AF.Square,
                                    accum_out=C(ASQ))
            ev_g = b.do("scalar", gelu_sq, mark="ac")
            hb_free[0] = ev_g
            b.wait("vector", ev_g)

            def stats2(v, C=C):
                v.tensor_scalar_mul(C(MU2c), C(ASUM), 1.0 / DI)
                v.tensor_scalar_mul(C(EX2c), C(ASQ), 1.0 / DI)
                v.drain()
                v.tensor_mul(C(VARc), C(MU2c), C(MU2c))
                v.drain()
                v.tensor_sub(C(VARc), C(EX2c), C(VARc))
                v.drain()
                return v.tensor_scalar_add(C(VEPSc), C(VARc), EPS)
            ev_s2 = b.do("vector", stats2, mark="dv")
            b.wait("scalar", ev_s2)
            ev_i2 = b.do("scalar", lambda s, C=C: (
                s.activation(C(LNc), C(VEPSc), AF.Ln),
                s.drain(),
                s.activation(C(INVc), C(LNc), AF.Exp,
                             scale=NEG_HALF))[-1], mark="ac")
            b.wait("vector", ev_i2)
            ev_nm = b.do("vector", lambda v, C=C: (
                v.drain(),
                v.tensor_mul(C(NMIc), C(MU2c), C(INVc)),
                v.drain(),
                v.tensor_scalar_mul(C(NMIc), C(NMIc), -1.0))[-1], mark="dv")
            b.wait("scalar", ev_nm)
            ev_zn = b.do("scalar", lambda s, C=C, ab=ab: (
                s.drain(),
                s.activation(ab[:], ab[:], AF.Identity, bias=C(NMIc),
                             scale=C(INVc)))[-1], mark="ac")
            b.waits("gpsimd", [ev_zn, pool_prev])
            ev_gm = b.do("gpsimd", lambda g, ab=ab: (
                g.tensor_mul(ab[:], ab[:], gbc[:]),
                g.drain(),
                g.tensor_add(ab[:], ab[:], bbc[:]))[-1], mark="gp")
            pool_prev = ev_gm
            b.wait("vector", ev_gm)
            ev_mx = b.do("vector", lambda v, C=C, ab=ab: v.reduce_max(
                C(T2c), ab[:], AX.X, apply_absolute_value=True), mark="dv")
            b.waits("scalar", [ev_gm, sq2_prev])
            ev_sq2 = b.do("scalar", lambda s, C=C, ab=ab, xq=xq:
                          s.activation(xq[:], ab[:], AF.Square,
                                       accum_out=C(SQ2)), mark="ac")
            sq2_prev = ev_sq2
            b.wait("vector", ev_sq2)

            def preR(v, C=C):
                v.tensor_scalar_mul(C(EXRc), C(SQ2), 1.0 / DI)
                v.drain()
                return v.tensor_scalar_add(C(EXRc), C(EXRc), EPS)
            ev_pr = b.do("vector", preR, mark="dv")
            b.wait("scalar", ev_pr)
            ev_r2 = b.do("scalar", lambda s, C=C: (
                s.activation(C(LNRc), C(EXRc), AF.Ln),
                s.drain(),
                s.activation(C(RST2), C(LNRc), AF.Exp,
                             scale=NEG_HALF))[-1], mark="ac")
            b.waits("vector", [ev_r2, ev_mx])

            def quant2w(v, qt=qt, C=C, ab=ab, xq=xq):
                v.tensor_mul(C(M2c), C(RST2), C(T2c))
                v.drain()
                v.tensor_scalar_max(C(M2c), C(M2c), EPS)
                v.drain()
                v.tensor_scalar(c2c[:, qt:qt + 1], C(M2c), wsc[:, 5:6],
                                1.0 / 127.0, ALU.mult, ALU.mult)
                v.reciprocal(C(RM2c), C(M2c))
                v.drain()
                v.tensor_mul(C(QS2c), C(RST2), C(RM2c))
                v.drain()
                v.tensor_scalar_mul(C(QS2c), C(QS2c), 127.0)
                v.drain()
                v.tensor_scalar(ab[:], ab[:], C(QS2c), MAGIC, ALU.mult,
                                ALU.add)
                v.drain()
                return v.tensor_single_scalar(xq[:], ab[:], MAGIC,
                                              ALU.subtract)
            ev_q2 = b.do("vector", quant2w, mark="dv")
            h_free[j] = ev_q2
            b.waits("tensor", [ev_q2, st2_free[j]])
            last_ev = None
            evx_prev = [None, None]
            for grp in range(NT):
                pt = pT3[grp % 2]

                def trx(t, grp=grp, pt=pt, xq=xq):
                    ins = None
                    for jj in range(4):
                        oc = grp * 4 + jj
                        ins = t.transpose(pt[:, jj * 128:(jj + 1) * 128],
                                          xq[:, oc * 128:(oc + 1) * 128],
                                          identb[:])
                    return ins
                b.wait("tensor", evx_prev[grp % 2])
                ev_t = b.do("tensor", trx, mark="pe")
                b.wait("vector", ev_t)

                def evx(v, grp=grp, pt=pt, xq=xq):
                    dst = xq[:].rearrange("p (oc t) -> p oc t", oc=NOC)[
                        :, grp * 4:(grp + 1) * 4, :]
                    return v.tensor_copy(dst, pt[:].rearrange(
                        "p (j t) -> p j t", j=4))
                last_ev = b.do("vector", evx, mark="dv")
                evx_prev[grp % 2] = last_ev
            b.wait("gpsimd", last_ev)
            st2_free[j] = b.dma(
                "gpsimd",
                xq2_sc[:, :, qt * 128:(qt + 1) * 128].rearrange(
                    "oc o t -> o oc t"),
                xqt2[j][:].rearrange("p (oc t) -> p oc t", oc=NOC))
        b.barrier()
        pg.close()
        pg_ps.close()

    b.phase = "M2"
    if upto >= 9:
        pm = ExitStack()
        em = pm.enter_context
        x2q = em(nc.sbuf_tensor("x2q", [128, NOC * 512], BF16))
        w2b = [em(nc.sbuf_tensor(f"w2b{j}", [128, NOC * 128], BF16))
               for j in range(2)]
        c2bc = em(nc.sbuf_tensor("c2bc", [128, 512], F32))
        c2r = em(nc.sbuf_tensor("c2r", [1, 128], F32))
        b2row_t = em(nc.sbuf_tensor("b2row_t", [1, D], F32))
        b2row = b2row_t[0:1, :]
        ev_b2l = b.dma("sync", b2row, b2_h[0:1, :])
        mtmp = em(nc.sbuf_tensor("mtmp", [128, 512], F32))
        gb_ = em(nc.sbuf_tensor("gb_", [128, 512], BF16))
        gq = em(nc.sbuf_tensor("gq", [128, 4 * D], BF16))
        xcb = em(nc.sbuf_tensor("xcb", [128, D], F32))
        scb2 = em(nc.sbuf_tensor("scb2", [128, D], BF16))
        sg32 = em(nc.sbuf_tensor("sg32", [128, D], F32))
        ocb = em(nc.sbuf_tensor("ocb", [128, D], F32))
        pm_ps = ExitStack()
        pM = [pm_ps.enter_context(nc.psum_tensor(f"pM{j}", [128, 512], F32))
              for j in range(2)]
        pTg = pm_ps.enter_context(nc.psum_tensor("pTg", [128, 512], BF16))
        pCt = pm_ps.enter_context(nc.psum_tensor("pCt", [1, 128], F32))
        pCb = pm_ps.enter_context(nc.psum_tensor("pCb", [128, 128], F32))

        b.wait("tensor", ev_b2l)
        for dpc in range(ND):
            ev_b = b.do("tensor", lambda t, dpc=dpc: t.matmul(
                pCb[:, 0:1], b2row[0:1, dpc * 128:(dpc + 1) * 128],
                one11[0:1, 0:1], start=True, stop=True), mark="pe")
            b.wait("vector", ev_b)
            ev_v = b.do("vector", lambda v, dpc=dpc: v.tensor_copy(
                b2cols[:, dpc:dpc + 1], pCb[:, 0:1]), mark="dv")
            b.wait("tensor", ev_v)

        x2_free = None
        w_free = [None, None]
        ev_evM = [None, None]
        ev_gev = None
        prev_sl = None
        prev_tg = None
        for quar in range(4):
            b.wait("sync", x2_free)
            ev_x2 = b.dma("sync",
                          x2q[:].rearrange("p (oc t) -> p oc t", oc=NOC),
                          xq2_sc[:, :, quar * 512:(quar + 1) * 512]
                          .rearrange("oc o t -> o oc t"))
            for j in range(4):
                qt = quar * 4 + j
                ev_t = b.do("tensor", lambda t, qt=qt: t.matmul(
                    pCt[0:1, :], c2c[:, qt:qt + 1], identf[:],
                    start=True, stop=True), mark="pe")
                b.wait("vector", ev_t)
                ev_v = b.do("vector", lambda v: v.tensor_copy(c2r[:],
                                                              pCt[0:1, :]),
                            mark="dv")
                b.wait("tensor", ev_v)
                ev_b2 = b.do("tensor", lambda t: t.matmul(
                    pCb[:], onec[0:1, :], c2r[0:1, :], start=True,
                    stop=True), mark="pe")
                b.wait("vector", ev_b2)
                ev_c = b.do("vector", lambda v, j=j: (
                    v.tensor_copy(c2bc[:, j * 128:(j + 1) * 128], pCb[:]),
                    v.drain())[0], mark="dv")
                b.wait("tensor", ev_c)

            for dpc in range(ND):
                wb2_ = w2b[dpc % 2]
                b.wait("sync", w_free[dpc % 2])
                ev_w = b.dma("sync",
                             wb2_[:].rearrange("p (oc dp) -> p oc dp",
                                               oc=NOC),
                             w2t_sc[dpc].rearrange("oc o dp -> o oc dp"))
                ps = pM[dpc % 2]
                b.waits("tensor", [ev_x2, ev_w, ev_evM[dpc % 2]])

                def mm2(t, ps=ps, wb2_=wb2_):
                    ins = None
                    for oc in range(NOC):
                        ins = t.matmul(ps[:],
                                       wb2_[:, oc * 128:(oc + 1) * 128],
                                       x2q[:, oc * 512:(oc + 1) * 512],
                                       start=(oc == 0), stop=(oc == NOC - 1))
                    return ins
                ev_mm = b.do("tensor", mm2, mark="pe")
                w_free[dpc % 2] = ev_mm
                if dpc == ND - 1:
                    x2_free = ev_mm
                b.waits("vector", [ev_mm, prev_sl])
                ev_e = b.do("vector", lambda v, ps=ps: v.tensor_tensor(
                    mtmp[:], ps[:], c2bc[:], ALU.mult), mark="dv")
                ev_evM[dpc % 2] = ev_e
                b.waits("scalar", [ev_e, prev_tg])
                ev_sl = b.do("scalar", lambda s, dpc=dpc: s.activation(
                    gb_[:], mtmp[:], AF.Silu,
                    bias=b2cols[:, dpc:dpc + 1]), mark="ac")
                prev_sl = ev_sl
                b.waits("tensor", [ev_sl, ev_gev])
                ev_tg = b.do("tensor", lambda t: (
                    [t.transpose(pTg[:, j * 128:(j + 1) * 128],
                                 gb_[:, j * 128:(j + 1) * 128], identb[:])
                     for j in range(4)])[-1], mark="pe")
                prev_tg = ev_tg
                b.wait("vector", ev_tg)

                def evg(v, dpc=dpc):
                    dst = gq[:].rearrange("p (tl d) -> p tl d", tl=4)[
                        :, :, dpc * 128:(dpc + 1) * 128]
                    return v.tensor_copy(
                        dst, pTg[:].rearrange("p (j t) -> p j t", j=4))
                ev_gev = b.do("vector", evg, mark="dv")
                b.wait("tensor", ev_gev)
            comb_free = None
            for tl in range(4):
                qt = quar * 4 + tl
                b.wait("sync", comb_free)
                ev_xr = b.dma("sync", xcb[:],
                              x_h[qt * 128:(qt + 1) * 128, :])
                ev_sr = b.dma("sync", scb2[:], s_sc[qt])
                b.waits("vector", [ev_xr, ev_sr, ev_gev])

                def comb(v, tl=tl):
                    v.drain()
                    v.tensor_mul(sg32[:], scb2[:],
                                 gq[:, tl * D:(tl + 1) * D])
                    v.drain()
                    return v.tensor_add(ocb[:], xcb[:], sg32[:])
                ev_c = b.do("vector", comb, mark="dv")
                comb_free = ev_c
                b.wait("gpsimd", ev_c)
                ev_y = b.dma("gpsimd", y_h[qt * 128:(qt + 1) * 128, :],
                             ocb[:])
                b.wait("vector", ev_y)
        pm.close()
        pm_ps.close()

    def fin(g):
        for s_ in ("dg0", "dg1", "dg2", "dg3"):
            g.wait_ge(b.semh[s_], b.cnt.get(s_, 0))
    b.emit("gpsimd", fin)

    phase_ranges = []

    def run_ops(eng, key):
        cur = None
        start = nc.n_instructions()
        for ph, f in b.ops[key]:
            if ph != cur:
                if cur is not None:
                    phase_ranges.append((key, cur, start,
                                         nc.n_instructions()))
                cur = ph
                start = nc.n_instructions()
            f(eng)
        if cur is not None:
            phase_ranges.append((key, cur, start, nc.n_instructions()))

    with nc.Block() as block:
        @block.sync
        def _(eng):
            run_ops(eng, "sync")

        @block.tensor
        def _(eng):
            run_ops(eng, "tensor")

        @block.vector
        def _(eng):
            run_ops(eng, "vector")

        @block.scalar
        def _(eng):
            run_ops(eng, "scalar")

        @block.gpsimd
        def _(eng):
            run_ops(eng, "gpsimd")

    ctx.close()
    nc._phase_ranges = phase_ranges
    return nc


def _rope_tables():
    e_ = np.arange(0, 128, 2, dtype=np.float32) / np.float32(128.0)
    p_ = np.power(np.float32(10000.0), e_).astype(np.float32)
    invf = (np.float32(1.0) / p_).astype(np.float32)
    pos = np.arange(T, dtype=np.float32)
    freqs = (pos[:, None] * invf[None, :]).astype(np.float32)
    emb = np.concatenate([freqs, freqs], axis=1)
    cos = np.cos(emb).astype(np.float32).T.copy()
    sin = np.sin(emb).astype(np.float32).T.copy()
    ss = sin.copy()
    ss[0:64] = -sin[0:64]
    return np.ascontiguousarray(cos), np.ascontiguousarray(ss)


def kernel(L0_attn_u, L0_attn_W_in, L0_attn_W_out, L0_ffn_w1, L0_ffn_b1,
           L0_ffn_ln_g, L0_ffn_ln_b, L0_ffn_w2, L0_ffn_b2):
    nb = L0_attn_u.shape[0]
    wkvq = np.ascontiguousarray(np.asarray(L0_attn_W_in).T[:, 4096:10240],
                                dtype=np.float32)
    woutT = np.ascontiguousarray(np.asarray(L0_attn_W_out).T,
                                 dtype=np.float32)
    w1T = np.ascontiguousarray(np.asarray(L0_ffn_w1).T, dtype=np.float32)
    w2T = np.ascontiguousarray(np.asarray(L0_ffn_w2).T, dtype=np.float32)
    b1 = np.asarray(L0_ffn_b1, dtype=np.float32).reshape(1, DI)
    lng = np.asarray(L0_ffn_ln_g, dtype=np.float32).reshape(1, DI)
    lnb = np.asarray(L0_ffn_ln_b, dtype=np.float32).reshape(1, DI)
    b2 = np.asarray(L0_ffn_b2, dtype=np.float32).reshape(1, D)
    cosT, sinT = _rope_tables()
    base = np.where(np.arange(128)[:, None] > np.arange(128)[None, :],
                    np.float32(-1e30), np.float32(0.0)).astype(np.float32)
    trimask = np.ascontiguousarray(np.tile(base, (1, 4)))
    ident = np.eye(128, dtype=np.float32)

    nc = build_nc(debug=DEBUG, upto=UPTO)
    in_maps = []
    for i in range(nb):
        in_maps.append(dict(
            x=np.ascontiguousarray(np.asarray(L0_attn_u[i],
                                              dtype=np.float32)),
            wkvq=wkvq, woutT=woutT, w1T=w1T, w2T=w2T, b1=b1, lng=lng,
            lnb=lnb, b2=b2, cosT=cosT, sinT=sinT, trimask=trimask,
            ident=ident))
    res = run_bass_kernel_spmd(nc, in_maps, core_ids=list(range(nb)))
    out = np.stack([np.asarray(r["y"], dtype=np.float32)
                    for r in res.results], axis=0)
    if DEBUG:
        kernel.last_results = res
    return out


# version 05
# speedup vs baseline: 1.1851x; 1.0898x over previous
import numpy as np
import ml_dtypes
from contextlib import ExitStack
import concourse.bass as bass
import concourse.mybir as mybir
from concourse.bass_utils import run_bass_kernel_spmd

F32 = mybir.dt.float32
BF16 = mybir.dt.bfloat16
AF = mybir.ActivationFunctionType
ALU = mybir.AluOpType
AX = mybir.AxisListType

MAGIC = 12582912.0
SC_ATT = 0.08838834764831845
NEG_HALF = -0.5
EPS = 1e-5
T, D, DI = 2048, 2048, 8192
NT, ND = 16, 16
NKV = 8
NOB, NOC = 16, 64

DEBUG = False
UPTO = 99
ENGS = ("sync", "tensor", "vector", "scalar", "gpsimd")


class B:
    def __init__(self):
        self.ops = {k: [] for k in ENGS}
        self.cnt = {}
        self.semh = {}
        self.lastw = {e: {} for e in ENGS}
        self.phase = "init"

    def emit(self, eng, fn):
        self.ops[eng].append((self.phase, fn))

    def wait(self, eng, evt):
        if evt is None:
            return
        sem, val, peng = evt
        if peng == eng or val <= 0:
            return
        if self.lastw[eng].get(sem, 0) >= val:
            return
        self.lastw[eng][sem] = val

        def f(e, sem=sem, val=val):
            e.wait_ge(self.semh[sem], val)
        self.emit(eng, f)

    def waits(self, eng, evts):
        for ev in evts:
            self.wait(eng, ev)

    def bump(self, sem, k):
        self.cnt[sem] = self.cnt.get(sem, 0) + k
        return self.cnt[sem]

    DMA_POOL = {"sync": ["dl0", "dl1", "dl2", "dl3"],
                "scalar": ["ds0", "ds1"],
                "gpsimd": ["dg0", "dg1", "dg2", "dg3"]}

    def dma(self, eng, dst, src):
        pool = self.DMA_POOL[eng]
        k = self.cnt.get("_rr_" + eng, 0)
        self.cnt["_rr_" + eng] = k + 1
        sem = pool[k % len(pool)]
        before = self.cnt.get(sem, 0)
        self.wait(eng, (sem, before, None))
        val = self.bump(sem, 16)

        def f(e, dst=dst, src=src, sem=sem):
            e.dma_start(out=dst, in_=src).then_inc(self.semh[sem], 16)
        self.emit(eng, f)
        return (sem, val, None)

    def do(self, eng, fn, mark=None):
        if mark is None:
            self.emit(eng, lambda e, fn=fn: fn(e))
            return None
        val = self.bump(mark, 1)

        def f(e, fn=fn, mark=mark):
            fn(e).then_inc(self.semh[mark], 1)
        self.emit(eng, f)
        return (mark, val, eng)

    def barrier(self):
        for eng, pool in self.DMA_POOL.items():
            for sem in pool:
                self.wait(eng, (sem, self.cnt.get(sem, 0), None))
        tgt = self.bump("bar", len(ENGS))
        for eng in ENGS:
            def f(e):
                try:
                    e.drain()
                except Exception:
                    pass
                e.nop().then_inc(self.semh["bar"], 1)
            self.emit(eng, f)
        for eng in ENGS:
            def f(e, tgt=tgt):
                e.wait_ge(self.semh["bar"], tgt)
            self.emit(eng, f)
            self.lastw[eng]["bar"] = tgt


def build_nc(debug=False, upto=99):
    nc = bass.Bass("TRN2", target_bir_lowering=False, debug=False)
    b = B()

    def inp(name, shape, dtype=F32):
        return nc.dram_tensor(name, shape, dtype, kind="ExternalInput").ap()

    x_h = inp("x", [T, D])
    wkvq_h = inp("wkvq", [D, 6144])
    wout_h = inp("woutT", [4096, D])
    w1_h = inp("w1T", [D, DI])
    w2_h = inp("w2T", [DI, D])
    b1_h = inp("b1", [1, DI])
    g2_h = inp("lng", [1, DI])
    bb2_h = inp("lnb", [1, DI])
    b2_h = inp("b2", [1, D])
    cos_h = inp("cosT", [128, T])
    sin_h = inp("sinT", [128, T])
    tri_h = inp("trimask", [128, 1024])
    id_h = inp("ident", [128, 128])
    y_h = nc.dram_tensor("y", [T, D], F32, kind="ExternalOutput").ap()

    skind = "ExternalOutput" if debug else "Internal"

    def scr(name, shape, dtype):
        return nc.dram_tensor(name, shape, dtype, kind=skind).ap()

    xq1_sc = scr("xq1_sc", [NT, ND, 128, 128], BF16)
    q_sc = scr("q_sc", [NKV, 128, NT, 4, 128], BF16)
    at_sc = scr("at_sc", [NKV, 128, NT, 512], BF16)
    s_sc = scr("s_sc", [NT, 128, D], BF16)
    h_sc = scr("h_sc", [NT, 128, DI], F32)
    xq2_sc = scr("xq2_sc", [NOC, 128, D], BF16)
    w1t_sc = scr("w1t_sc", [NOB, ND, 128, 512], BF16)
    w2t_sc = scr("w2t_sc", [ND, NOC, 128, 128], BF16)
    dbg = {}
    if debug:
        for nm, sh, dt in (("d_xnfm", [128, ND * T], BF16),
                           ("d_kfm", [128, NKV * T], BF16),
                           ("d_vkm", [128, NKV * T], BF16),
                           ("d_stats", [128, 64], F32),
                           ("d_wsc", [128, 8], F32)):
            dbg[nm] = nc.dram_tensor(nm, sh, dt, kind="ExternalOutput").ap()

    ctx = ExitStack()
    e = ctx.enter_context
    for s in ("dl0", "dl1", "dl2", "dl3", "ds0", "ds1", "dg0", "dg1",
              "dg2", "dg3", "pe", "dv", "ac", "gp", "bar"):
        b.semh[s] = e(nc.semaphore(s))

    cpool = e(nc.sbuf_tensor("cpool", [128, 4], F32))
    nc.const_aps.register(F32, SC_ATT, cpool[:, 0:1])
    nc.const_aps.register(F32, NEG_HALF, cpool[:, 1:2])
    nc.const_aps.register(F32, -1.0, cpool[:, 2:3])
    onec = e(nc.sbuf_tensor("onec", [1, 128], F32))
    onecol = e(nc.sbuf_tensor("onecol", [128, 1], F32))
    one11 = e(nc.sbuf_tensor("one11", [1, 1], F32))
    ones_bf = e(nc.sbuf_tensor("ones_bf", [128, 128], BF16))
    identb = e(nc.sbuf_tensor("identb", [128, 128], BF16))
    identf = e(nc.sbuf_tensor("identf", [128, 128], F32))
    tri2 = e(nc.sbuf_tensor("tri2", [128, 1024], F32))
    m1c = e(nc.sbuf_tensor("m1c", [128, NT], F32))
    c1c = e(nc.sbuf_tensor("c1c", [128, NT], F32))
    c2c = e(nc.sbuf_tensor("c2c", [128, NT], F32))
    wsc = e(nc.sbuf_tensor("wsc", [128, 8], F32))
    b2cols = e(nc.sbuf_tensor("b2cols", [128, ND], F32))

    b.dma("sync", tri2[:], tri_h[:, :])
    ev_idf = b.dma("sync", identf[:], id_h[:, :])

    def init_consts(v):
        v.memset(onec[:], 1.0)
        v.memset(onecol[:], 1.0)
        v.memset(one11[:], 1.0)
        v.memset(ones_bf[:], 1.0)
        v.memset(cpool[:, 0:1], SC_ATT)
        v.memset(cpool[:, 2:3], -1.0)
        return v.memset(cpool[:, 1:2], NEG_HALF)
    ev_cst = b.do("vector", init_consts, mark="dv")
    b.wait("vector", ev_idf)
    ev_idb = b.do("vector", lambda v: v.tensor_copy(identb[:], identf[:]),
                  mark="dv")

    xn_ctx = ExitStack()
    xn_fm = xn_ctx.enter_context(nc.sbuf_tensor("xn_fm", [128, ND * T], BF16))
    rope_ctx = ExitStack()
    er = rope_ctx.enter_context
    cosT = er(nc.sbuf_tensor("cosT_sb", [128, T], F32))
    ssT = er(nc.sbuf_tensor("ssT_sb", [128, T], F32))
    b.dma("sync", cosT[:], cos_h[:, :])
    b.dma("sync", ssT[:], sin_h[:, :])

    b.phase = "A"
    if upto >= 1:
        pa = ExitStack()
        ea = pa.enter_context
        xld = [ea(nc.sbuf_tensor(f"xld{j}", [128, D], F32)) for j in range(2)]
        xn32 = ea(nc.sbuf_tensor("xn32", [128, D], F32))
        tq32 = ea(nc.sbuf_tensor("tq32", [128, D], F32))
        xnbf = ea(nc.sbuf_tensor("xnbf", [128, D], BF16))
        xqbf = ea(nc.sbuf_tensor("xqbf", [128, D], BF16))
        xqst = ea(nc.sbuf_tensor("xqst", [128, D], BF16))
        trashA = ea(nc.sbuf_tensor("trashA", [128, D], BF16))
        stc = ea(nc.sbuf_tensor("stc", [128, 20], F32))
        pA_ps = ExitStack()
        pT = pA_ps.enter_context(nc.psum_tensor("pTA", [128, 512], BF16))
        pT2 = pA_ps.enter_context(nc.psum_tensor("pTA2", [128, 512], BF16))

        (SU, SQ, MU, EX2, MU2, VAR, VEPS, LNV, INV, I2, MX2, M2E, LNM, RST,
         MX, QS, RM) = range(17)

        xld_free = [None, None]
        st_free = None
        for i in range(NT):
            buf = xld[i % 2]
            b.wait("sync", xld_free[i % 2])
            ev_x = b.dma("sync", buf[:], x_h[i * 128:(i + 1) * 128, :])
            b.wait("vector", ev_x)
            ev_ms = b.do("vector", lambda v, bu=buf: (
                v.drain(),
                v.reduce_sum(stc[:, SU:SU + 1], bu[:], AX.X),
                v.memset(stc[:, SQ:SQ + 1], 0.0))[-1], mark="dv")
            b.waits("scalar", [ev_x, ev_ms])
            ev_sq = b.do("scalar", lambda s, bu=buf: s.activation(
                trashA[:], bu[:], AF.Square, accum_out=stc[:, SQ:SQ + 1]),
                mark="ac")
            b.wait("vector", ev_sq)

            def stats1(v):
                v.tensor_scalar_mul(stc[:, MU:MU + 1], stc[:, SU:SU + 1],
                                    1.0 / D)
                v.tensor_scalar_mul(stc[:, EX2:EX2 + 1], stc[:, SQ:SQ + 1],
                                    1.0 / D)
                v.drain()
                v.tensor_mul(stc[:, MU2:MU2 + 1], stc[:, MU:MU + 1],
                             stc[:, MU:MU + 1])
                v.drain()
                v.tensor_sub(stc[:, VAR:VAR + 1], stc[:, EX2:EX2 + 1],
                             stc[:, MU2:MU2 + 1])
                v.drain()
                return v.tensor_scalar_add(stc[:, VEPS:VEPS + 1],
                                           stc[:, VAR:VAR + 1], EPS)
            ev_v1 = b.do("vector", stats1, mark="dv")
            b.wait("scalar", ev_v1)

            def act_inv(s):
                s.activation(stc[:, LNV:LNV + 1], stc[:, VEPS:VEPS + 1], AF.Ln)
                s.drain()
                return s.activation(stc[:, INV:INV + 1], stc[:, LNV:LNV + 1],
                                    AF.Exp, scale=NEG_HALF)
            ev_inv = b.do("scalar", act_inv, mark="ac")
            b.wait("vector", ev_inv)

            def xnf(v, bu=buf):
                v.drain()
                v.tensor_scalar(xn32[:], bu[:], stc[:, MU:MU + 1],
                                stc[:, INV:INV + 1], ALU.subtract, ALU.mult)
                v.tensor_mul(stc[:, I2:I2 + 1], stc[:, INV:INV + 1],
                             stc[:, INV:INV + 1])
                v.drain()
                v.tensor_copy(xnbf[:], xn32[:])
                v.tensor_mul(stc[:, MX2:MX2 + 1], stc[:, VAR:VAR + 1],
                             stc[:, I2:I2 + 1])
                v.reduce_max(stc[:, MX:MX + 1], xn32[:], AX.X,
                             apply_absolute_value=True)
                v.drain()
                return v.tensor_scalar_add(stc[:, M2E:M2E + 1],
                                           stc[:, MX2:MX2 + 1], EPS)
            ev_v2 = b.do("vector", xnf, mark="dv")
            xld_free[i % 2] = ev_v2
            b.wait("scalar", ev_v2)

            def act_rst(s):
                s.activation(stc[:, LNM:LNM + 1], stc[:, M2E:M2E + 1], AF.Ln)
                s.drain()
                return s.activation(stc[:, RST:RST + 1], stc[:, LNM:LNM + 1],
                                    AF.Exp, scale=NEG_HALF)
            ev_rst = b.do("scalar", act_rst, mark="ac")
            b.wait("vector", ev_rst)

            def quant(v, i=i):
                v.tensor_mul(m1c[:, i:i + 1], stc[:, RST:RST + 1],
                             stc[:, MX:MX + 1])
                v.drain()
                v.tensor_scalar_max(m1c[:, i:i + 1], m1c[:, i:i + 1], EPS)
                v.drain()
                v.reciprocal(stc[:, RM:RM + 1], m1c[:, i:i + 1])
                v.drain()
                v.tensor_mul(stc[:, QS:QS + 1], stc[:, RST:RST + 1],
                             stc[:, RM:RM + 1])
                v.drain()
                v.tensor_scalar_mul(stc[:, QS:QS + 1], stc[:, QS:QS + 1],
                                    127.0)
                v.drain()
                v.tensor_scalar_mul(tq32[:], xn32[:], stc[:, QS:QS + 1])
                v.drain()
                return v.tensor_scalar(xqbf[:], tq32[:], MAGIC, MAGIC,
                                       ALU.add, ALU.subtract)
            ev_q = b.do("vector", quant, mark="dv")

            b.waits("tensor", [ev_v2, ev_idb])
            for gdc in range(4):
                def tr4(t, gdc=gdc):
                    ins = None
                    for j in range(4):
                        dc = gdc * 4 + j
                        ins = t.transpose(pT[:, j * 128:(j + 1) * 128],
                                          xnbf[:, dc * 128:(dc + 1) * 128],
                                          identb[:])
                    return ins
                ev_t = b.do("tensor", tr4, mark="pe")
                b.wait("vector", ev_t)

                def evac(v, gdc=gdc, i=i):
                    dst = xn_fm[:].rearrange("p (dc t) -> p dc t", dc=ND)[
                        :, gdc * 4:(gdc + 1) * 4, i * 128:(i + 1) * 128]
                    return v.tensor_copy(
                        dst, pT[:].rearrange("p (j t) -> p j t", j=4))
                ev_ev = b.do("vector", evac, mark="dv")
                b.wait("tensor", ev_ev)
            b.waits("tensor", [ev_q, st_free])
            last_evq = None
            for gdc in range(4):
                def trq(t, gdc=gdc):
                    ins = None
                    for j in range(4):
                        dc = gdc * 4 + j
                        ins = t.transpose(pT2[:, j * 128:(j + 1) * 128],
                                          xqbf[:, dc * 128:(dc + 1) * 128],
                                          identb[:])
                    return ins
                ev_t = b.do("tensor", trq, mark="pe")
                b.wait("vector", ev_t)

                def evacq(v, gdc=gdc):
                    dst = xqst[:].rearrange("p (dc t) -> p dc t", dc=ND)[
                        :, gdc * 4:(gdc + 1) * 4, :]
                    return v.tensor_copy(
                        dst, pT2[:].rearrange("p (j t) -> p j t", j=4))
                last_evq = b.do("vector", evacq, mark="dv")
                b.wait("tensor", last_evq)
            b.wait("gpsimd", last_evq)
            st_free = b.dma("gpsimd",
                            xq1_sc[i].rearrange("dc d t -> d dc t"),
                            xqst[:].rearrange("p (dc t) -> p dc t", dc=ND))
        if debug:
            b.wait("gpsimd", ("dv", b.cnt.get("dv", 0), None))
            b.dma("gpsimd", dbg["d_xnfm"][:, :], xn_fm[:])
            b.dma("gpsimd", dbg["d_stats"][:, 0:NT], m1c[:])
        b.barrier()
        pa.close()
        pA_ps.close()

    b.phase = "W"
    if upto >= 2:
        pw = ExitStack()
        ew = pw.enter_context
        wch = [ew(nc.sbuf_tensor(f"wch{j}", [128, DI], F32)) for j in range(2)]
        acc = ew(nc.sbuf_tensor("wacc", [128, 4], F32))
        ternb = [ew(nc.sbuf_tensor(f"tern{j}", [128, DI], BF16))
                 for j in range(2)]
        negb = ew(nc.sbuf_tensor("negb", [128, DI], BF16))
        srow = ew(nc.sbuf_tensor("srow", [1, 8], F32))
        pw_ps = ExitStack()
        pW = pw_ps.enter_context(nc.psum_tensor("pW", [1, 2], F32))
        pC = pw_ps.enter_context(nc.psum_tensor("pC", [128, 8], F32))

        ch_free = [None, None]
        tern_free = [None, None]
        for wi, (w_h, nch, F, nelem) in enumerate(
                ((w1_h, 16, DI, D * DI), (w2_h, 64, D, DI * D))):
            ev_z = b.do("vector", lambda v, wi=wi: v.memset(
                acc[:, wi:wi + 1], 0.0), mark="dv")
            for c in range(nch):
                buf = wch[c % 2]
                b.wait("sync", ch_free[c % 2])
                ev_w = b.dma("sync", buf[:, 0:F],
                             w_h[c * 128:(c + 1) * 128, :])
                b.waits("vector", [ev_w, ev_z])

                def statw(v, buf=buf, F=F, wi=wi):
                    v.drain()
                    v.tensor_reduce(acc[:, 2:3], buf[:, 0:F], AX.X, ALU.add,
                                    apply_absolute_value=True)
                    v.drain()
                    return v.tensor_add(acc[:, wi:wi + 1], acc[:, wi:wi + 1],
                                        acc[:, 2:3])
                ch_free[c % 2] = b.do("vector", statw, mark="dv")
            b.waits("tensor", [ch_free[(nch - 1) % 2], ev_cst])
            ev_tot = b.do("tensor", lambda t, wi=wi: t.matmul(
                pW[0:1, wi:wi + 1], acc[:, wi:wi + 1], onecol[:, 0:1],
                start=True, stop=True), mark="pe")
            b.wait("vector", ev_tot)

            def scals(v, wi=wi, nelem=nelem):
                v.tensor_scalar_mul(srow[0:1, 4 * wi:4 * wi + 1],
                                    pW[0:1, wi:wi + 1], 1.0 / nelem)
                v.drain()
                v.tensor_scalar_max(srow[0:1, 4 * wi + 3:4 * wi + 4],
                                    srow[0:1, 4 * wi:4 * wi + 1], EPS)
                v.drain()
                v.tensor_scalar_mul(srow[0:1, 4 * wi + 1:4 * wi + 2],
                                    srow[0:1, 4 * wi + 3:4 * wi + 4], 0.5)
                return v.tensor_scalar_mul(
                    srow[0:1, 4 * wi + 2:4 * wi + 3],
                    srow[0:1, 4 * wi + 3:4 * wi + 4], -0.5)
            ev_sc = b.do("vector", scals, mark="dv")
            b.wait("tensor", ev_sc)
            ev_bc = b.do("tensor", lambda t, wi=wi: t.matmul(
                pC[:, 3 * wi:3 * wi + 3],
                onec[0:1, :],
                srow[0:1, 4 * wi + 1:4 * wi + 4],
                start=True, stop=True), mark="pe")
            b.wait("vector", ev_bc)
            ev_wsc = b.do("vector", lambda v, wi=wi: (
                v.tensor_copy(wsc[:, 3 * wi:3 * wi + 3],
                              pC[:, 3 * wi:3 * wi + 3]),
                v.drain())[0], mark="dv")
            for c in range(nch):
                buf = wch[c % 2]
                b.wait("sync", ch_free[c % 2])
                ev_w = b.dma("sync", buf[:, 0:F],
                             w_h[c * 128:(c + 1) * 128, :])
                tb = ternb[c % 2]
                b.waits("vector", [ev_w, ev_wsc, tern_free[c % 2]])

                def ternf(v, buf=buf, tb=tb, F=F, wi=wi):
                    v.drain()
                    v.tensor_single_scalar(negb[:, 0:F], buf[:, 0:F],
                                           wsc[:, 3 * wi + 1:3 * wi + 2],
                                           ALU.is_lt)
                    v.drain()
                    return v.scalar_tensor_tensor(
                        tb[:, 0:F], buf[:, 0:F],
                        wsc[:, 3 * wi:3 * wi + 1], negb[:, 0:F],
                        ALU.is_gt, ALU.subtract)
                ev_t = b.do("vector", ternf, mark="dv")
                ch_free[c % 2] = ev_t
                b.wait("gpsimd", ev_t)
                if wi == 0:
                    dst = w1t_sc[:, c, :, :].rearrange("ob d o -> d ob o")
                    src = tb[:, 0:F].rearrange("d (ob o) -> d ob o", ob=NOB)
                else:
                    dst = w2t_sc[:, c, :, :].rearrange("dpc o dp -> o dpc dp")
                    src = tb[:, 0:F].rearrange("o (dpc dp) -> o dpc dp",
                                               dpc=ND)
                tern_free[c % 2] = b.dma("gpsimd", dst, src)
            if wi == 0:
                ev_c1 = b.do("vector", lambda v: (
                    v.drain(),
                    v.tensor_scalar(c1c[:], m1c[:], wsc[:, 2:3], 1.0 / 127.0,
                                    ALU.mult, ALU.mult))[-1], mark="dv")
        if debug:
            b.wait("gpsimd", ("dv", b.cnt.get("dv", 0), None))
            b.dma("gpsimd", dbg["d_wsc"][:, 0:6], wsc[:, 0:6])
        b.barrier()
        pw.close()
        pw_ps.close()

    kv_ctx = ExitStack()
    k_fm = kv_ctx.enter_context(
        nc.sbuf_tensor("k_fm", [128, NKV * T], BF16, side="right"))
    v_km = kv_ctx.enter_context(
        nc.sbuf_tensor("v_km", [128, NKV * T], BF16, side="right"))
    b.phase = "P1kv"
    if upto >= 3:
        p1 = ExitStack()
        e1 = p1.enter_context
        wres = e1(nc.sbuf_tensor("wres1", [128, 16 * 1024], BF16))
        wstg = [e1(nc.sbuf_tensor(f"wstg{j}", [128, 1024], F32))
                for j in range(2)]
        a32 = e1(nc.sbuf_tensor("a32", [128, 1024], F32))
        b32 = e1(nc.sbuf_tensor("b32", [128, 1024], F32))
        vstg = e1(nc.sbuf_tensor("vstg", [128, 1024], BF16))
        p1_ps = ExitStack()
        pA = p1_ps.enter_context(nc.psum_tensor("pA1", [128, 1024], F32))
        pB = p1_ps.enter_context(nc.psum_tensor("pB1", [128, 1024], F32))
        pTv = p1_ps.enter_context(nc.psum_tensor("pTv", [128, 512], BF16))

        last_pe_half = None
        stg_free = [None, None]
        prev_evac = [None, None]
        for half in range(2):
            cast_evs = []
            for dc in range(ND):
                sb = wstg[dc % 2]
                b.wait("sync", stg_free[dc % 2])
                ev_w = b.dma("sync", sb[:],
                             wkvq_h[dc * 128:(dc + 1) * 128,
                                    half * 1024:(half + 1) * 1024])
                b.waits("scalar", [ev_w, last_pe_half])
                ev_c = b.do("scalar", lambda s, sb=sb, dc=dc: s.copy(
                    wres[:, dc * 1024:(dc + 1) * 1024], sb[:]), mark="ac")
                stg_free[dc % 2] = ev_c
                cast_evs.append(ev_c)
            b.waits("tensor", cast_evs)
            unit = 0
            for oc in range(8):
                for tb in range(2):
                    ps = pA if unit % 2 == 0 else pB
                    b.wait("tensor", prev_evac[unit % 2])

                    def mmkv(t, oc=oc, tb=tb, ps=ps):
                        ins = None
                        for dc in range(ND):
                            for nb in range(2):
                                ins = t.matmul(
                                    ps[:, nb * 512:(nb + 1) * 512],
                                    wres[:, dc * 1024 + oc * 128:
                                         dc * 1024 + (oc + 1) * 128],
                                    xn_fm[:, dc * T + tb * 1024 + nb * 512:
                                          dc * T + tb * 1024 +
                                          (nb + 1) * 512],
                                    start=(dc == 0), stop=(dc == ND - 1))
                        return ins
                    ev_mm = b.do("tensor", mmkv, mark="pe")
                    b.wait("vector", ev_mm)
                    if half == 0:
                        ev_cp = b.do("vector", lambda v, ps=ps:
                                     v.tensor_copy(vstg[:], ps[:]),
                                     mark="dv")
                        prev_evac[unit % 2] = ev_cp
                        b.wait("tensor", ev_cp)
                        for grp in range(2):
                            def trv(t, grp=grp):
                                ins = None
                                for j in range(4):
                                    ins = t.transpose(
                                        pTv[:, j * 128:(j + 1) * 128],
                                        vstg[:, (grp * 4 + j) * 128:
                                             (grp * 4 + j + 1) * 128],
                                        identb[:])
                                return ins
                            ev_t = b.do("tensor", trv, mark="pe")
                            b.wait("vector", ev_t)

                            def evv(v, oc=oc, tb=tb, grp=grp):
                                kt0 = oc * 16 + tb * 8 + grp * 4
                                return v.tensor_copy(
                                    v_km[:, kt0 * 128:(kt0 + 4) * 128],
                                    pTv[:])
                            ev_e = b.do("vector", evv, mark="dv")
                            b.wait("tensor", ev_e)
                    else:
                        def ropek(v, ps=ps, oc=oc, tb=tb):
                            sl = slice(tb * 1024, (tb + 1) * 1024)
                            v.drain()
                            v.tensor_mul(a32[:], ps[:], cosT[:, sl])
                            v.tensor_mul(b32[0:64, :], ps[64:128, :],
                                         ssT[0:64, sl])
                            v.tensor_mul(b32[64:128, :], ps[0:64, :],
                                         ssT[64:128, sl])
                            v.drain()
                            return v.tensor_add(
                                k_fm[:, oc * T + tb * 1024:
                                     oc * T + (tb + 1) * 1024],
                                a32[:], b32[:])
                        ev_r = b.do("vector", ropek, mark="dv")
                        prev_evac[unit % 2] = ev_r
                    unit += 1
            last_pe_half = ("pe", b.cnt.get("pe", 0), "tensor")
        if debug:
            b.wait("gpsimd", ("dv", b.cnt.get("dv", 0), None))
            b.dma("gpsimd", dbg["d_kfm"][:, :], k_fm[:])
            b.dma("gpsimd", dbg["d_vkm"][:, :], v_km[:])
        b.barrier()
        p1.close()
        p1_ps.close()

    b.phase = "P2q"
    if upto >= 4:
        p2 = ExitStack()
        e2 = p2.enter_context
        wres2 = e2(nc.sbuf_tensor("wres2", [128, 16 * 1024], BF16))
        wstg2 = e2(nc.sbuf_tensor("wstg2", [128, 1024], F32))
        a32q = e2(nc.sbuf_tensor("a32q", [128, 1024], F32))
        b32q = e2(nc.sbuf_tensor("b32q", [128, 1024], F32))
        qraw = e2(nc.sbuf_tensor("qraw", [128, 1024], F32))
        qstg = [e2(nc.sbuf_tensor(f"qstg{j}", [128, 1024], BF16))
                for j in range(2)]
        p2_ps = ExitStack()
        pA2 = p2_ps.enter_context(nc.psum_tensor("pA2", [128, 1024], F32))
        pB2 = p2_ps.enter_context(nc.psum_tensor("pB2", [128, 1024], F32))

        last_pe_qr = None
        stg_free = None
        prev_rope = [None, None]
        prev_evq = [None, None]
        prev_pq = None
        prev_r_any = None
        st_free = [None, None]
        for qr in range(4):
            cast_evs = []
            for dc in range(ND):
                b.wait("sync", stg_free)
                ev_w = b.dma("sync", wstg2[:],
                             wkvq_h[dc * 128:(dc + 1) * 128,
                                    2048 + qr * 1024:2048 + (qr + 1) * 1024])
                b.waits("scalar", [ev_w, last_pe_qr])
                ev_c = b.do("scalar", lambda s, dc=dc: s.copy(
                    wres2[:, dc * 1024:(dc + 1) * 1024], wstg2[:]),
                    mark="ac")
                stg_free = ev_c
                cast_evs.append(ev_c)
            b.waits("tensor", cast_evs)
            unit = 0
            for oc in range(8):
                h = qr * 8 + oc
                for tb in range(2):
                    ps = pA2 if unit % 2 == 0 else pB2
                    b.wait("tensor", prev_rope[unit % 2])

                    def mmq(t, oc=oc, tb=tb, ps=ps):
                        ins = None
                        for dc in range(ND):
                            for nb in range(2):
                                ins = t.matmul(
                                    ps[:, nb * 512:(nb + 1) * 512],
                                    wres2[:, dc * 1024 + oc * 128:
                                          dc * 1024 + (oc + 1) * 128],
                                    xn_fm[:, dc * T + tb * 1024 + nb * 512:
                                          dc * T + tb * 1024 +
                                          (nb + 1) * 512],
                                    start=(dc == 0), stop=(dc == ND - 1))
                        return ins
                    ev_mm = b.do("tensor", mmq, mark="pe")
                    qs_buf = qstg[unit % 2]
                    b.waits("vector", [ev_mm, st_free[unit % 2]])

                    def ropeq(v, ps=ps, tb=tb, qs_buf=qs_buf):
                        sl = slice(tb * 1024, (tb + 1) * 1024)
                        v.drain()
                        v.tensor_mul(a32q[:], ps[:], cosT[:, sl])
                        v.tensor_mul(b32q[0:64, :], ps[64:128, :],
                                     ssT[0:64, sl])
                        v.tensor_mul(b32q[64:128, :], ps[0:64, :],
                                     ssT[64:128, sl])
                        v.drain()
                        return v.tensor_add(qs_buf[:], a32q[:], b32q[:])
                    ev_r = b.do("vector", ropeq, mark="dv")
                    prev_rope[unit % 2] = ev_r
                    b.wait("gpsimd", ev_r)
                    st_free[unit % 2] = b.dma(
                        "gpsimd",
                        q_sc[h // 4, :, tb * 8:(tb + 1) * 8, h % 4, :],
                        qs_buf[:].rearrange("p (qt t) -> p qt t", qt=8))
                    unit += 1
            last_pe_qr = ("pe", b.cnt.get("pe", 0), "tensor")
        b.barrier()
        p2.close()
        p2_ps.close()
    rope_ctx.close()
    xn_ctx.close()

    b.phase = "P3flash"
    if upto >= 5:
        p3 = ExitStack()
        e3 = p3.enter_context
        qbuf = [e3(nc.sbuf_tensor(f"qbuf{j}", [128, 4096], BF16))
                for j in range(2)]
        Pp = [e3(nc.sbuf_tensor(f"Pp{j}", [128, 1024], BF16))
              for j in range(2)]
        zl = e3(nc.sbuf_tensor("zl", [128, 1024], F32))
        zr = e3(nc.sbuf_tensor("zr", [128, 1024], F32))
        atile = [e3(nc.sbuf_tensor(f"atile{j}", [128, 1024], BF16))
                 for j in range(2)]
        p3_ps = ExitStack()
        pS = [p3_ps.enter_context(nc.psum_tensor(f"pS{j}", [128, 1024], F32))
              for j in range(2)]
        pAV = p3_ps.enter_context(nc.psum_tensor("pAV", [128, 1024], F32))
        pZ = p3_ps.enter_context(nc.psum_tensor("pZ", [128, 1024], F32))

        q_free = [None, None]
        at_free = [None, None]
        ev_evac = None
        ev_zr = None
        ai = 0
        for qt in range(NT):
            qb = qbuf[qt % 2]
            b.wait("sync", q_free[qt % 2])
            ev_q = b.dma("sync", qb[:].rearrange("p (kvh f) -> p kvh f",
                                                 kvh=NKV),
                         q_sc[:, :, qt, :, :].rearrange(
                             "kvh hd g t -> hd kvh (g t)"))
            last_smm = None
            exp_evs = {}
            pe_z = {}
            g = 0
            for kvp in range(4):
                kvh0, kvh1 = 2 * kvp, 2 * kvp + 1

                def emit_smm(kt, g, kvh0=kvh0, kvh1=kvh1, qt=qt, qb=qb):
                    par = g % 2
                    b.waits("tensor", [ev_q, exp_evs.get(g - 2), ev_evac])

                    def smm(t, kt=kt, par=par):
                        t.matmul(pS[par][:, 0:512],
                                 k_fm[:, kvh0 * T + kt * 128:
                                      kvh0 * T + (kt + 1) * 128],
                                 qb[:, kvh0 * 512:(kvh0 + 1) * 512],
                                 start=True, stop=True)
                        return t.matmul(pS[par][:, 512:1024],
                                        k_fm[:, kvh1 * T + kt * 128:
                                             kvh1 * T + (kt + 1) * 128],
                                        qb[:, kvh1 * 512:(kvh1 + 1) * 512],
                                        start=True, stop=True)
                    ev_s = b.do("tensor", smm, mark="pe")
                    src_ev = ev_s
                    if kt == qt:
                        b.wait("vector", ev_s)
                        ev_m = b.do("vector", lambda v, par=par:
                                    v.tensor_tensor(pS[par][:], pS[par][:],
                                                    tri2[:], ALU.add),
                                    mark="dv")
                        src_ev = ev_m
                    b.waits("scalar", [src_ev, pe_z.get(g - 2)])
                    ev_e = b.do("scalar", lambda s, par=par: s.activation(
                        Pp[par][:], pS[par][:], AF.Exp, scale=SC_ATT),
                        mark="ac")
                    exp_evs[g] = ev_e
                    return ev_s

                def emit_av(kt, g, kvh0=kvh0, kvh1=kvh1, qt=qt):
                    par = g % 2
                    b.wait("tensor", exp_evs[g])
                    b.do("tensor", lambda t, kt=kt, par=par:
                         t.matmul(pAV[:, 0:512],
                                  v_km[:, (kvh0 * 16 + kt) * 128:
                                       (kvh0 * 16 + kt + 1) * 128],
                                  Pp[par][:, 0:512], start=(kt == 0),
                                  stop=(kt == qt)))
                    b.do("tensor", lambda t, kt=kt, par=par:
                         t.matmul(pAV[:, 512:1024],
                                  v_km[:, (kvh1 * 16 + kt) * 128:
                                       (kvh1 * 16 + kt + 1) * 128],
                                  Pp[par][:, 512:1024], start=(kt == 0),
                                  stop=(kt == qt)))
                    b.do("tensor", lambda t, kt=kt, par=par:
                         t.matmul(pZ[:, 0:512], ones_bf[:],
                                  Pp[par][:, 0:512], start=(kt == 0),
                                  stop=(kt == qt)))
                    pe_z[g] = b.do(
                        "tensor", lambda t, kt=kt, par=par:
                        t.matmul(pZ[:, 512:1024], ones_bf[:],
                                 Pp[par][:, 512:1024], start=(kt == 0),
                                 stop=(kt == qt)), mark="pe")
                    return pe_z[g]

                for kt in range(qt + 1):
                    last_smm = emit_smm(kt, g + kt)
                    if kt >= 1:
                        emit_av(kt - 1, g + kt - 1)
                ev_zlast = emit_av(qt, g + qt)
                g += qt + 1
                at = atile[ai % 2]
                b.waits("scalar", [ev_zlast])

                def evz(s):
                    s.activation(zl[:], pZ[:], AF.Ln)
                    s.drain()
                    return s.activation(zr[:], zl[:], AF.Exp, scale=-1.0)
                ev_zr = b.do("scalar", evz, mark="ac")
                b.waits("vector", [ev_zr, at_free[ai % 2]])
                ev_evac = b.do("vector", lambda v, at=at: v.tensor_tensor(
                    at[:], pAV[:], zr[:], ALU.mult), mark="dv")
                b.wait("gpsimd", ev_evac)
                at_free[ai % 2] = b.dma(
                    "gpsimd",
                    at_sc[kvh0:kvh0 + 2, :, qt, :].rearrange(
                        "kv hd f -> hd kv f"),
                    at[:].rearrange("p (kv f) -> p kv f", kv=2))
                ai += 1
            q_free[qt % 2] = last_smm
        b.barrier()
        p3.close()
        p3_ps.close()
    kv_ctx.close()

    b.phase = "F1"
    if upto >= 6:
        pf = ExitStack()
        ef = pf.enter_context
        xq1b = ef(nc.sbuf_tensor("xq1b", [128, ND * T], BF16))
        w1b = [ef(nc.sbuf_tensor(f"w1b{j}", [128, DI], BF16))
               for j in range(2)]
        b1bc = ef(nc.sbuf_tensor("b1bc", [128, DI], F32))
        b1row_t = ef(nc.sbuf_tensor("b1row_t", [1, DI], F32))
        b1row = b1row_t[0:1, :]
        ev_b1l = b.dma("sync", b1row, b1_h[0:1, :])
        htmp = [ef(nc.sbuf_tensor(f"htmp{j}", [128, 512], F32))
                for j in range(2)]
        hsil = [ef(nc.sbuf_tensor(f"hsil{j}", [128, 512], F32))
                for j in range(2)]
        pf_ps = ExitStack()
        pF = [pf_ps.enter_context(nc.psum_tensor(f"pF{j}", [128, 512], F32))
              for j in range(2)]
        pB5 = pf_ps.enter_context(nc.psum_tensor("pB5", [128, 512], F32))

        ev_xq = []
        for qt in range(NT):
            dst = xq1b[:].rearrange("p (dc t) -> p dc t", dc=ND)[
                :, :, qt * 128:(qt + 1) * 128]
            ev_xq.append(b.dma("sync", dst,
                               xq1_sc[qt].rearrange("dc d t -> d dc t")))
        b.wait("tensor", ev_b1l)
        for ob in range(NOB):
            ev_b = b.do("tensor", lambda t, ob=ob: t.matmul(
                pB5[:], onec[0:1, :], b1row[0:1, ob * 512:(ob + 1) * 512],
                start=True, stop=True), mark="pe")
            b.wait("vector", ev_b)
            ev_v = b.do("vector", lambda v, ob=ob: (
                v.tensor_copy(b1bc[:, ob * 512:(ob + 1) * 512], pB5[:]),
                v.drain())[0], mark="dv")
            b.wait("tensor", ev_v)

        w_free = [None, None]
        ui = 0
        h_free = [None, None]
        ev_evF = [None, None]
        for ob in range(NOB):
            wb_ = w1b[ob % 2]
            b.wait("sync", w_free[ob % 2])
            ev_w = b.dma("sync", wb_[:].rearrange("p (dc o) -> p dc o",
                                                  dc=ND),
                         w1t_sc[ob].rearrange("dc d o -> d dc o"))
            last_mm = None
            for qt in range(NT):
                ps = pF[ui % 2]
                b.waits("tensor", ev_xq + [ev_w, ev_evF[ui % 2]])

                def mmf(t, ob=ob, qt=qt, ps=ps, wb_=wb_):
                    ins = None
                    for dc in range(ND):
                        ins = t.matmul(ps[:],
                                       xq1b[:, dc * T + qt * 128:
                                            dc * T + (qt + 1) * 128],
                                       wb_[:, dc * 512:(dc + 1) * 512],
                                       start=(dc == 0), stop=(dc == ND - 1))
                    return ins
                ev_mm = b.do("tensor", mmf, mark="pe")
                last_mm = ev_mm
                b.waits("vector", [ev_mm, h_free[ui % 2]])

                def evf(v, ps=ps, qt=qt, ob=ob, j=ui % 2):
                    return v.scalar_tensor_tensor(
                        htmp[j][:], ps[:], c1c[:, qt:qt + 1],
                        b1bc[:, ob * 512:(ob + 1) * 512],
                        ALU.mult, ALU.add)
                ev_e = b.do("vector", evf, mark="dv")
                ev_evF[ui % 2] = ev_e
                b.wait("scalar", ev_e)
                ev_s = b.do("scalar", lambda s, j=ui % 2: s.activation(
                    hsil[j][:], htmp[j][:], AF.Silu), mark="ac")
                b.wait("gpsimd", ev_s)
                h_free[ui % 2] = b.dma(
                    "gpsimd", h_sc[qt, :, ob * 512:(ob + 1) * 512],
                    hsil[ui % 2][:])
                ui += 1
            w_free[ob % 2] = last_mm
        b.barrier()
        pf.close()
        pf_ps.close()

    b.phase = "P4out"
    if upto >= 7:
        p4 = ExitStack()
        e4 = p4.enter_context
        woutb = e4(nc.sbuf_tensor("woutb", [128, 32 * D], BF16))
        wst4 = [e4(nc.sbuf_tensor(f"wst4{j}", [128, D], F32))
                for j in range(2)]
        abuf = [e4(nc.sbuf_tensor(f"abuf{j}", [128, 4096], BF16))
                for j in range(2)]
        sst = [e4(nc.sbuf_tensor(f"sst{j}", [128, D], BF16))
               for j in range(2)]
        p4_ps = ExitStack()
        pO = [p4_ps.enter_context(nc.psum_tensor(f"pO{j}", [128, 512], F32))
              for j in range(2)]

        stg_free = [None, None]
        for h in range(32):
            sb = wst4[h % 2]
            b.wait("sync", stg_free[h % 2])
            ev_w = b.dma("sync", sb[:], wout_h[h * 128:(h + 1) * 128, :])
            b.wait("scalar", ev_w)
            ev_c = b.do("scalar", lambda s, sb=sb, h=h: s.copy(
                woutb[:, h * D:(h + 1) * D], sb[:]), mark="ac")
            stg_free[h % 2] = ev_c
        cast_done = stg_free[31 % 2]
        a_free = [None, None]
        s_free = [None, None]
        ev_ev4 = [None, None]
        for qt in range(NT):
            ab = abuf[qt % 2]
            b.wait("sync", a_free[qt % 2])
            ev_a = b.dma("sync", ab[:].rearrange("p (kvh f) -> p kvh f",
                                                 kvh=NKV),
                         at_sc[:, :, qt, :].rearrange("kvh hd f -> hd kvh f"))
            ss = sst[qt % 2]
            b.wait("vector", s_free[qt % 2])
            last_mm = None
            for dpb in range(4):
                ps = pO[dpb % 2]
                b.waits("tensor", [ev_a, cast_done, ev_ev4[dpb % 2]])

                def mmo(t, dpb=dpb, ps=ps, ab=ab):
                    ins = None
                    for h in range(32):
                        ins = t.matmul(
                            ps[:],
                            ab[:, (h // 4) * 512 + (h % 4) * 128:
                               (h // 4) * 512 + (h % 4 + 1) * 128],
                            woutb[:, h * D + dpb * 512:
                                  h * D + (dpb + 1) * 512],
                            start=(h == 0), stop=(h == 31))
                    return ins
                ev_mm = b.do("tensor", mmo, mark="pe")
                last_mm = ev_mm
                b.wait("vector", ev_mm)
                ev_e = b.do("vector", lambda v, ps=ps, dpb=dpb, ss=ss:
                            v.tensor_copy(ss[:, dpb * 512:(dpb + 1) * 512],
                                          ps[:]), mark="dv")
                ev_ev4[dpb % 2] = ev_e
            a_free[qt % 2] = last_mm
            b.wait("gpsimd", ev_ev4[1])
            b.wait("gpsimd", ev_ev4[0])
            s_free[qt % 2] = b.dma("gpsimd", s_sc[qt], ss[:])
        b.barrier()
        p4.close()
        p4_ps.close()

    b.phase = "F2"
    if upto >= 8:
        pg = ExitStack()
        eg = pg.enter_context
        gbc = eg(nc.sbuf_tensor("gbc", [128, DI], F32))
        bbc = eg(nc.sbuf_tensor("bbc", [128, DI], F32))
        stq = eg(nc.sbuf_tensor("stq", [128, 34], F32))
        pg_ps = ExitStack()
        pG = pg_ps.enter_context(nc.psum_tensor("pG", [128, 512], F32))
        pT3 = [pg_ps.enter_context(nc.psum_tensor(f"pT3{j}", [128, 512],
                                                  BF16)) for j in range(2)]

        rows_ctx = ExitStack()
        growb = rows_ctx.enter_context(
            nc.sbuf_tensor("growb", [1, 2 * DI], F32))
        ev_gl = b.dma("sync", growb[0:1, 0:DI], g2_h[0:1, :])
        ev_bl = b.dma("sync", growb[0:1, DI:2 * DI], bb2_h[0:1, :])
        b.waits("tensor", [ev_gl, ev_bl])
        ev_gb_done = None
        for ri, dstt in ((0, gbc), (1, bbc)):
            for ob in range(NOB):
                ev_b = b.do("tensor", lambda t, ob=ob, ri=ri:
                            t.matmul(pG[:], onec[0:1, :],
                                     growb[0:1, ri * DI + ob * 512:
                                           ri * DI + (ob + 1) * 512],
                                     start=True, stop=True), mark="pe")
                b.wait("vector", ev_b)
                ev_v = b.do("vector", lambda v, ob=ob, dstt=dstt: (
                    v.tensor_copy(dstt[:, ob * 512:(ob + 1) * 512], pG[:]),
                    v.drain())[0], mark="dv")
                b.wait("tensor", ev_v)
                ev_gb_done = ev_v
        rows_ctx.close()
        hb = eg(nc.sbuf_tensor("hbF", [128, DI], F32))
        ab2 = [eg(nc.sbuf_tensor(f"ab2{j}", [128, DI], F32))
               for j in range(2)]
        xqt2 = [eg(nc.sbuf_tensor(f"xqt2{j}", [128, DI], BF16))
                for j in range(2)]

        (ASUM, ASQ, MU2c, EX2c, VARc, VEPSc, LNc, INVc, SQ2, EXRc, LNRc,
         RST2, T2c, M2c, RM2c, QS2c, NMIc) = range(17)

        h_free = [ev_gb_done, ev_gb_done]
        st2_free = [None, None]
        pool_prev = None
        sq2_prev = None
        hb_free = [None]
        for qt in range(NT):
            j = qt % 2
            B0 = 17 * j
            ab = ab2[j]
            xq = xqt2[j]

            def C(i):
                return stq[:, B0 + i:B0 + i + 1]
            b.waits("sync", [h_free[j], hb_free[0]])
            ev_h = b.dma("sync", hb[:], h_sc[qt])
            ev_ms = b.do("vector", lambda v, C=C: (
                v.memset(C(ASUM), 0.0),
                v.memset(C(ASQ), 0.0),
                v.memset(C(SQ2), 0.0))[-1], mark="dv")
            b.waits("scalar", [ev_h, ev_ms, h_free[j], st2_free[j]])

            def gelu_sq(s, C=C, ab=ab, xq=xq):
                s.activation(ab[:], hb[:], AF.Gelu, accum_out=C(ASUM))
                s.drain()
                return s.activation(xq[:], ab[:], AF.Square,
                                    accum_out=C(ASQ))
            ev_g = b.do("scalar", gelu_sq, mark="ac")
            hb_free[0] = ev_g
            b.wait("vector", ev_g)

            def stats2(v, C=C):
                v.tensor_scalar_mul(C(MU2c), C(ASUM), 1.0 / DI)
                v.tensor_scalar_mul(C(EX2c), C(ASQ), 1.0 / DI)
                v.drain()
                v.tensor_mul(C(VARc), C(MU2c), C(MU2c))
                v.drain()
                v.tensor_sub(C(VARc), C(EX2c), C(VARc))
                v.drain()
                return v.tensor_scalar_add(C(VEPSc), C(VARc), EPS)
            ev_s2 = b.do("vector", stats2, mark="dv")
            b.wait("scalar", ev_s2)
            ev_i2 = b.do("scalar", lambda s, C=C: (
                s.activation(C(LNc), C(VEPSc), AF.Ln),
                s.drain(),
                s.activation(C(INVc), C(LNc), AF.Exp,
                             scale=NEG_HALF))[-1], mark="ac")
            b.wait("vector", ev_i2)
            ev_nm = b.do("vector", lambda v, C=C: (
                v.drain(),
                v.tensor_mul(C(NMIc), C(MU2c), C(INVc)),
                v.drain(),
                v.tensor_scalar_mul(C(NMIc), C(NMIc), -1.0))[-1], mark="dv")
            b.wait("scalar", ev_nm)
            ev_zn = b.do("scalar", lambda s, C=C, ab=ab: (
                s.drain(),
                s.activation(ab[:], ab[:], AF.Identity, bias=C(NMIc),
                             scale=C(INVc)))[-1], mark="ac")
            b.waits("gpsimd", [ev_zn, pool_prev])
            ev_gm = b.do("gpsimd", lambda g, ab=ab: (
                g.tensor_mul(ab[:], ab[:], gbc[:]),
                g.drain(),
                g.tensor_add(ab[:], ab[:], bbc[:]))[-1], mark="gp")
            pool_prev = ev_gm
            b.wait("vector", ev_gm)
            ev_mx = b.do("vector", lambda v, C=C, ab=ab: v.reduce_max(
                C(T2c), ab[:], AX.X, apply_absolute_value=True), mark="dv")
            b.waits("scalar", [ev_gm, sq2_prev])
            ev_sq2 = b.do("scalar", lambda s, C=C, ab=ab, xq=xq:
                          s.activation(xq[:], ab[:], AF.Square,
                                       accum_out=C(SQ2)), mark="ac")
            sq2_prev = ev_sq2
            b.wait("vector", ev_sq2)

            def preR(v, C=C):
                v.tensor_scalar_mul(C(EXRc), C(SQ2), 1.0 / DI)
                v.drain()
                return v.tensor_scalar_add(C(EXRc), C(EXRc), EPS)
            ev_pr = b.do("vector", preR, mark="dv")
            b.wait("scalar", ev_pr)
            ev_r2 = b.do("scalar", lambda s, C=C: (
                s.activation(C(LNRc), C(EXRc), AF.Ln),
                s.drain(),
                s.activation(C(RST2), C(LNRc), AF.Exp,
                             scale=NEG_HALF))[-1], mark="ac")
            b.waits("vector", [ev_r2, ev_mx])

            def quant2w(v, qt=qt, C=C, ab=ab, xq=xq):
                v.tensor_mul(C(M2c), C(RST2), C(T2c))
                v.drain()
                v.tensor_scalar_max(C(M2c), C(M2c), EPS)
                v.drain()
                v.tensor_scalar(c2c[:, qt:qt + 1], C(M2c), wsc[:, 5:6],
                                1.0 / 127.0, ALU.mult, ALU.mult)
                v.reciprocal(C(RM2c), C(M2c))
                v.drain()
                v.tensor_mul(C(QS2c), C(RST2), C(RM2c))
                v.drain()
                v.tensor_scalar_mul(C(QS2c), C(QS2c), 127.0)
                v.drain()
                v.tensor_scalar(ab[:], ab[:], C(QS2c), MAGIC, ALU.mult,
                                ALU.add)
                v.drain()
                return v.tensor_single_scalar(xq[:], ab[:], MAGIC,
                                              ALU.subtract)
            ev_q2 = b.do("vector", quant2w, mark="dv")
            h_free[j] = ev_q2
            b.waits("tensor", [ev_q2, st2_free[j]])
            last_ev = None
            evx_prev = [None, None]
            for grp in range(NT):
                pt = pT3[grp % 2]

                def trx(t, grp=grp, pt=pt, xq=xq):
                    ins = None
                    for jj in range(4):
                        oc = grp * 4 + jj
                        ins = t.transpose(pt[:, jj * 128:(jj + 1) * 128],
                                          xq[:, oc * 128:(oc + 1) * 128],
                                          identb[:])
                    return ins
                b.wait("tensor", evx_prev[grp % 2])
                ev_t = b.do("tensor", trx, mark="pe")
                b.wait("vector", ev_t)

                def evx(v, grp=grp, pt=pt, xq=xq):
                    dst = xq[:].rearrange("p (oc t) -> p oc t", oc=NOC)[
                        :, grp * 4:(grp + 1) * 4, :]
                    return v.tensor_copy(dst, pt[:].rearrange(
                        "p (j t) -> p j t", j=4))
                last_ev = b.do("vector", evx, mark="dv")
                evx_prev[grp % 2] = last_ev
            b.wait("gpsimd", last_ev)
            st2_free[j] = b.dma(
                "gpsimd",
                xq2_sc[:, :, qt * 128:(qt + 1) * 128].rearrange(
                    "oc o t -> o oc t"),
                xqt2[j][:].rearrange("p (oc t) -> p oc t", oc=NOC))
        b.barrier()
        pg.close()
        pg_ps.close()

    b.phase = "M2"
    if upto >= 9:
        pm = ExitStack()
        em = pm.enter_context
        x2q = em(nc.sbuf_tensor("x2q", [128, NOC * 512], BF16))
        w2b = [em(nc.sbuf_tensor(f"w2b{j}", [128, NOC * 128], BF16))
               for j in range(2)]
        c2bc = em(nc.sbuf_tensor("c2bc", [128, 512], F32))
        c2r = em(nc.sbuf_tensor("c2r", [1, 128], F32))
        b2row_t = em(nc.sbuf_tensor("b2row_t", [1, D], F32))
        b2row = b2row_t[0:1, :]
        ev_b2l = b.dma("sync", b2row, b2_h[0:1, :])
        mtmp = [em(nc.sbuf_tensor(f"mtmp{j}", [128, 512],
                                  F32)) for j in range(2)]
        gb_ = [em(nc.sbuf_tensor(f"gb_{j}", [128, 512],
                                 BF16)) for j in range(2)]
        gq = em(nc.sbuf_tensor("gq", [128, 4 * D], BF16))
        xcb = em(nc.sbuf_tensor("xcb", [128, D], F32))
        scb2 = em(nc.sbuf_tensor("scb2", [128, D], BF16))
        sg32 = em(nc.sbuf_tensor("sg32", [128, D], F32))
        ocb = em(nc.sbuf_tensor("ocb", [128, D], F32))
        pm_ps = ExitStack()
        pM = [pm_ps.enter_context(nc.psum_tensor(f"pM{j}", [128, 512], F32))
              for j in range(2)]
        pTg = [pm_ps.enter_context(nc.psum_tensor(f"pTg{j}",
               [128, 512], BF16)) for j in range(2)]
        pCt = pm_ps.enter_context(nc.psum_tensor("pCt", [1, 128], F32))
        pCb = pm_ps.enter_context(nc.psum_tensor("pCb", [128, 128], F32))

        b.wait("tensor", ev_b2l)
        for dpc in range(ND):
            ev_b = b.do("tensor", lambda t, dpc=dpc: t.matmul(
                pCb[:, 0:1], b2row[0:1, dpc * 128:(dpc + 1) * 128],
                one11[0:1, 0:1], start=True, stop=True), mark="pe")
            b.wait("vector", ev_b)
            ev_v = b.do("vector", lambda v, dpc=dpc: v.tensor_copy(
                b2cols[:, dpc:dpc + 1], pCb[:, 0:1]), mark="dv")
            b.wait("tensor", ev_v)

        x2_free = None
        w_free = [None, None]
        ev_evM = [None, None]
        ev_gev = None
        prev_sl = None
        prev_tg = None
        for quar in range(4):
            b.wait("sync", x2_free)
            ev_x2 = b.dma("sync",
                          x2q[:].rearrange("p (oc t) -> p oc t", oc=NOC),
                          xq2_sc[:, :, quar * 512:(quar + 1) * 512]
                          .rearrange("oc o t -> o oc t"))
            for j in range(4):
                qt = quar * 4 + j
                ev_t = b.do("tensor", lambda t, qt=qt: t.matmul(
                    pCt[0:1, :], c2c[:, qt:qt + 1], identf[:],
                    start=True, stop=True), mark="pe")
                b.wait("vector", ev_t)
                ev_v = b.do("vector", lambda v: v.tensor_copy(c2r[:],
                                                              pCt[0:1, :]),
                            mark="dv")
                b.wait("tensor", ev_v)
                ev_b2 = b.do("tensor", lambda t: t.matmul(
                    pCb[:], onec[0:1, :], c2r[0:1, :], start=True,
                    stop=True), mark="pe")
                b.wait("vector", ev_b2)
                ev_c = b.do("vector", lambda v, j=j: (
                    v.tensor_copy(c2bc[:, j * 128:(j + 1) * 128], pCb[:]),
                    v.drain())[0], mark="dv")
                b.wait("tensor", ev_c)

            mm_evs = {}

            def emit_post(dpc):
                ps = pM[dpc % 2]
                b.waits("vector", [mm_evs[dpc], prev_sl])
                ev_e = b.do("vector", lambda v, ps=ps, dpc=dpc:
                            v.tensor_tensor(mtmp[dpc % 2][:], ps[:],
                                            c2bc[:], ALU.mult), mark="dv")
                ev_evM[dpc % 2] = ev_e
                b.waits("scalar", [ev_e, prev_tg])
                ev_sl = b.do("scalar", lambda s, dpc=dpc: s.activation(
                    gb_[dpc % 2][:], mtmp[dpc % 2][:], AF.Silu,
                    bias=b2cols[:, dpc:dpc + 1]), mark="ac")
                b.waits("tensor", [ev_sl, ev_gev])
                ev_tg = b.do("tensor", lambda t, dpc=dpc: (
                    [t.transpose(pTg[dpc % 2][:, j * 128:(j + 1) * 128],
                                 gb_[dpc % 2][:, j * 128:(j + 1) * 128],
                                 identb[:])
                     for j in range(4)])[-1], mark="pe")
                b.wait("vector", ev_tg)

                def evg(v, dpc=dpc):
                    dst = gq[:].rearrange("p (tl d) -> p tl d", tl=4)[
                        :, :, dpc * 128:(dpc + 1) * 128]
                    return v.tensor_copy(
                        dst, pTg[dpc % 2][:].rearrange("p (j t) -> p j t",
                                                       j=4))
                ev_g2 = b.do("vector", evg, mark="dv")
                return ev_sl, ev_tg, ev_g2

            for dpc in range(ND):
                wb2_ = w2b[dpc % 2]
                b.wait("sync", w_free[dpc % 2])
                ev_w = b.dma("sync",
                             wb2_[:].rearrange("p (oc dp) -> p oc dp",
                                               oc=NOC),
                             w2t_sc[dpc].rearrange("oc o dp -> o oc dp"))
                ps = pM[dpc % 2]
                b.waits("tensor", [ev_x2, ev_w, ev_evM[dpc % 2]])

                def mm2(t, ps=ps, wb2_=wb2_):
                    ins = None
                    for oc in range(NOC):
                        ins = t.matmul(ps[:],
                                       wb2_[:, oc * 128:(oc + 1) * 128],
                                       x2q[:, oc * 512:(oc + 1) * 512],
                                       start=(oc == 0), stop=(oc == NOC - 1))
                    return ins
                ev_mm = b.do("tensor", mm2, mark="pe")
                mm_evs[dpc] = ev_mm
                w_free[dpc % 2] = ev_mm
                if dpc == ND - 1:
                    x2_free = ev_mm
                if dpc >= 1:
                    prev_sl, prev_tg, ev_gev = emit_post(dpc - 1)
            prev_sl, prev_tg, ev_gev = emit_post(ND - 1)
            comb_free = None
            for tl in range(4):
                qt = quar * 4 + tl
                b.wait("sync", comb_free)
                ev_xr = b.dma("sync", xcb[:],
                              x_h[qt * 128:(qt + 1) * 128, :])
                ev_sr = b.dma("sync", scb2[:], s_sc[qt])
                b.waits("vector", [ev_xr, ev_sr, ev_gev])

                def comb(v, tl=tl):
                    v.drain()
                    v.tensor_mul(sg32[:], scb2[:],
                                 gq[:, tl * D:(tl + 1) * D])
                    v.drain()
                    return v.tensor_add(ocb[:], xcb[:], sg32[:])
                ev_c = b.do("vector", comb, mark="dv")
                comb_free = ev_c
                b.wait("gpsimd", ev_c)
                ev_y = b.dma("gpsimd", y_h[qt * 128:(qt + 1) * 128, :],
                             ocb[:])
                b.wait("vector", ev_y)
        pm.close()
        pm_ps.close()

    def fin(g):
        for s_ in ("dg0", "dg1", "dg2", "dg3"):
            g.wait_ge(b.semh[s_], b.cnt.get(s_, 0))
    b.emit("gpsimd", fin)

    phase_ranges = []

    def run_ops(eng, key):
        cur = None
        start = nc.n_instructions()
        for ph, f in b.ops[key]:
            if ph != cur:
                if cur is not None:
                    phase_ranges.append((key, cur, start,
                                         nc.n_instructions()))
                cur = ph
                start = nc.n_instructions()
            f(eng)
        if cur is not None:
            phase_ranges.append((key, cur, start, nc.n_instructions()))

    with nc.Block() as block:
        @block.sync
        def _(eng):
            run_ops(eng, "sync")

        @block.tensor
        def _(eng):
            run_ops(eng, "tensor")

        @block.vector
        def _(eng):
            run_ops(eng, "vector")

        @block.scalar
        def _(eng):
            run_ops(eng, "scalar")

        @block.gpsimd
        def _(eng):
            run_ops(eng, "gpsimd")

    ctx.close()
    nc._phase_ranges = phase_ranges
    return nc


def _rope_tables():
    e_ = np.arange(0, 128, 2, dtype=np.float32) / np.float32(128.0)
    p_ = np.power(np.float32(10000.0), e_).astype(np.float32)
    invf = (np.float32(1.0) / p_).astype(np.float32)
    pos = np.arange(T, dtype=np.float32)
    freqs = (pos[:, None] * invf[None, :]).astype(np.float32)
    emb = np.concatenate([freqs, freqs], axis=1)
    cos = np.cos(emb).astype(np.float32).T.copy()
    sin = np.sin(emb).astype(np.float32).T.copy()
    ss = sin.copy()
    ss[0:64] = -sin[0:64]
    return np.ascontiguousarray(cos), np.ascontiguousarray(ss)


def kernel(L0_attn_u, L0_attn_W_in, L0_attn_W_out, L0_ffn_w1, L0_ffn_b1,
           L0_ffn_ln_g, L0_ffn_ln_b, L0_ffn_w2, L0_ffn_b2):
    nb = L0_attn_u.shape[0]
    wkvq = np.ascontiguousarray(np.asarray(L0_attn_W_in).T[:, 4096:10240],
                                dtype=np.float32)
    woutT = np.ascontiguousarray(np.asarray(L0_attn_W_out).T,
                                 dtype=np.float32)
    w1T = np.ascontiguousarray(np.asarray(L0_ffn_w1).T, dtype=np.float32)
    w2T = np.ascontiguousarray(np.asarray(L0_ffn_w2).T, dtype=np.float32)
    b1 = np.asarray(L0_ffn_b1, dtype=np.float32).reshape(1, DI)
    lng = np.asarray(L0_ffn_ln_g, dtype=np.float32).reshape(1, DI)
    lnb = np.asarray(L0_ffn_ln_b, dtype=np.float32).reshape(1, DI)
    b2 = np.asarray(L0_ffn_b2, dtype=np.float32).reshape(1, D)
    cosT, sinT = _rope_tables()
    base = np.where(np.arange(128)[:, None] > np.arange(128)[None, :],
                    np.float32(-1e30), np.float32(0.0)).astype(np.float32)
    trimask = np.ascontiguousarray(np.tile(base, (1, 8)))
    ident = np.eye(128, dtype=np.float32)

    nc = build_nc(debug=DEBUG, upto=UPTO)
    in_maps = []
    for i in range(nb):
        in_maps.append(dict(
            x=np.ascontiguousarray(np.asarray(L0_attn_u[i],
                                              dtype=np.float32)),
            wkvq=wkvq, woutT=woutT, w1T=w1T, w2T=w2T, b1=b1, lng=lng,
            lnb=lnb, b2=b2, cosT=cosT, sinT=sinT, trimask=trimask,
            ident=ident))
    res = run_bass_kernel_spmd(nc, in_maps, core_ids=list(range(nb)))
    out = np.stack([np.asarray(r["y"], dtype=np.float32)
                    for r in res.results], axis=0)
    if DEBUG:
        kernel.last_results = res
    return out


# version 06
# speedup vs baseline: 1.3062x; 1.1022x over previous
import numpy as np
import ml_dtypes
from contextlib import ExitStack
import concourse.bass as bass
import concourse.mybir as mybir
from concourse.bass_utils import run_bass_kernel_spmd

F32 = mybir.dt.float32
BF16 = mybir.dt.bfloat16
AF = mybir.ActivationFunctionType
ALU = mybir.AluOpType
AX = mybir.AxisListType

MAGIC = 12582912.0
SC_ATT = 0.08838834764831845
NEG_HALF = -0.5
EPS = 1e-5
T, D, DI = 2048, 2048, 8192
NT, ND = 16, 16
NKV = 8
NOB, NOC = 16, 64

DEBUG = False
UPTO = 99
ENGS = ("sync", "tensor", "vector", "scalar", "gpsimd")


class B:
    def __init__(self):
        self.ops = {k: [] for k in ENGS}
        self.cnt = {}
        self.semh = {}
        self.lastw = {e: {} for e in ENGS}
        self.phase = "init"

    def emit(self, eng, fn):
        self.ops[eng].append((self.phase, fn))

    def wait(self, eng, evt):
        if evt is None:
            return
        sem, val, peng = evt
        if peng == eng or val <= 0:
            return
        if self.lastw[eng].get(sem, 0) >= val:
            return
        self.lastw[eng][sem] = val

        def f(e, sem=sem, val=val):
            e.wait_ge(self.semh[sem], val)
        self.emit(eng, f)

    def waits(self, eng, evts):
        for ev in evts:
            self.wait(eng, ev)

    def bump(self, sem, k):
        self.cnt[sem] = self.cnt.get(sem, 0) + k
        return self.cnt[sem]

    DMA_POOL = {"sync": ["dl0", "dl1", "dl2", "dl3"],
                "scalar": ["ds0", "ds1"],
                "gpsimd": ["dg0", "dg1", "dg2", "dg3"]}

    def dma(self, eng, dst, src):
        pool = self.DMA_POOL[eng]
        k = self.cnt.get("_rr_" + eng, 0)
        self.cnt["_rr_" + eng] = k + 1
        sem = pool[k % len(pool)]
        before = self.cnt.get(sem, 0)
        self.wait(eng, (sem, before, None))
        val = self.bump(sem, 16)

        def f(e, dst=dst, src=src, sem=sem):
            e.dma_start(out=dst, in_=src).then_inc(self.semh[sem], 16)
        self.emit(eng, f)
        return (sem, val, None)

    def do(self, eng, fn, mark=None):
        if mark is None:
            self.emit(eng, lambda e, fn=fn: fn(e))
            return None
        val = self.bump(mark, 1)

        def f(e, fn=fn, mark=mark):
            fn(e).then_inc(self.semh[mark], 1)
        self.emit(eng, f)
        return (mark, val, eng)

    def barrier(self):
        for eng, pool in self.DMA_POOL.items():
            for sem in pool:
                self.wait(eng, (sem, self.cnt.get(sem, 0), None))
        tgt = self.bump("bar", len(ENGS))
        for eng in ENGS:
            def f(e):
                try:
                    e.drain()
                except Exception:
                    pass
                e.nop().then_inc(self.semh["bar"], 1)
            self.emit(eng, f)
        for eng in ENGS:
            def f(e, tgt=tgt):
                e.wait_ge(self.semh["bar"], tgt)
            self.emit(eng, f)
            self.lastw[eng]["bar"] = tgt


def build_nc(debug=False, upto=99):
    nc = bass.Bass("TRN2", target_bir_lowering=False, debug=False)
    b = B()

    def inp(name, shape, dtype=F32):
        return nc.dram_tensor(name, shape, dtype, kind="ExternalInput").ap()

    x_h = inp("x", [T, D])
    wkvq_h = inp("wkvq", [D, 6144])
    wout_h = inp("woutT", [4096, D])
    w1_h = inp("w1T", [D, DI])
    w2_h = inp("w2T", [DI, D])
    b1_h = inp("b1", [1, DI])
    g2_h = inp("lng", [1, DI])
    bb2_h = inp("lnb", [1, DI])
    b2_h = inp("b2", [1, D])
    cos_h = inp("cosT", [128, T])
    sin_h = inp("sinT", [128, T])
    tri_h = inp("trimask", [128, 1024])
    id_h = inp("ident", [128, 128])
    y_h = nc.dram_tensor("y", [T, D], F32, kind="ExternalOutput").ap()

    skind = "ExternalOutput" if debug else "Internal"

    def scr(name, shape, dtype):
        return nc.dram_tensor(name, shape, dtype, kind=skind).ap()

    xq1_sc = scr("xq1_sc", [NT, ND, 128, 128], BF16)
    q_sc = scr("q_sc", [NKV, 128, NT, 4, 128], BF16)
    at_sc = scr("at_sc", [NKV, 128, NT, 512], BF16)
    s_sc = scr("s_sc", [NT, 128, D], BF16)
    h_sc = scr("h_sc", [NT, 128, DI], F32)
    xq2_sc = scr("xq2_sc", [NOC, 128, D], BF16)
    w1t_sc = scr("w1t_sc", [NOB, ND, 128, 512], BF16)
    wfold_sc = nc.dram_tensor("wfold_sc", [2, 32], F32, kind="Internal").ap()
    wrep_sc = nc.dram_tensor("wrep_sc", [2, 128, 3], F32,
                             kind="Internal").ap()
    w2t_sc = scr("w2t_sc", [ND, NOC, 128, 128], BF16)
    dbg = {}
    if debug:
        for nm, sh, dt in (("d_xnfm", [128, ND * T], BF16),
                           ("d_kfm", [128, NKV * T], BF16),
                           ("d_vkm", [128, NKV * T], BF16),
                           ("d_stats", [128, 64], F32),
                           ("d_wsc", [128, 8], F32)):
            dbg[nm] = nc.dram_tensor(nm, sh, dt, kind="ExternalOutput").ap()

    ctx = ExitStack()
    e = ctx.enter_context
    for s in ("dl0", "dl1", "dl2", "dl3", "ds0", "ds1", "dg0", "dg1",
              "dg2", "dg3", "pe", "dv", "ac", "gp", "bar"):
        b.semh[s] = e(nc.semaphore(s))

    cpool = e(nc.sbuf_tensor("cpool", [128, 4], F32))
    nc.const_aps.register(F32, SC_ATT, cpool[:, 0:1])
    nc.const_aps.register(F32, NEG_HALF, cpool[:, 1:2])
    nc.const_aps.register(F32, -1.0, cpool[:, 2:3])
    onec = e(nc.sbuf_tensor("onec", [1, 128], F32))
    onecol = e(nc.sbuf_tensor("onecol", [128, 1], F32))
    one11 = e(nc.sbuf_tensor("one11", [1, 1], F32))
    ones_bf = e(nc.sbuf_tensor("ones_bf", [128, 128], BF16))
    identb = e(nc.sbuf_tensor("identb", [128, 128], BF16))
    identf = e(nc.sbuf_tensor("identf", [128, 128], F32))
    tri2 = e(nc.sbuf_tensor("tri2", [128, 1024], F32))
    m1c = e(nc.sbuf_tensor("m1c", [128, NT], F32))
    c1c = e(nc.sbuf_tensor("c1c", [128, NT], F32))
    c2c = e(nc.sbuf_tensor("c2c", [128, NT], F32))
    wsc = e(nc.sbuf_tensor("wsc", [128, 8], F32))
    b2cols = e(nc.sbuf_tensor("b2cols", [128, ND], F32))
    srow = e(nc.sbuf_tensor("srow_w", [1, 96], F32))
    wacc = e(nc.sbuf_tensor("wacc", [128, 8], F32))

    b.dma("sync", tri2[:], tri_h[:, :])
    ev_idf = b.dma("sync", identf[:], id_h[:, :])

    def init_consts(v):
        v.memset(onec[:], 1.0)
        v.memset(onecol[:], 1.0)
        v.memset(one11[:], 1.0)
        v.memset(ones_bf[:], 1.0)
        v.memset(cpool[:, 0:1], SC_ATT)
        v.memset(cpool[:, 2:3], -1.0)
        return v.memset(cpool[:, 1:2], NEG_HALF)
    ev_cst = b.do("vector", init_consts, mark="dv")
    b.wait("vector", ev_idf)
    ev_idb = b.do("vector", lambda v: v.tensor_copy(identb[:], identf[:]),
                  mark="dv")

    xn_ctx = ExitStack()
    xn_fm = xn_ctx.enter_context(nc.sbuf_tensor("xn_fm", [128, ND * T], BF16))
    rope_ctx = ExitStack()
    er = rope_ctx.enter_context
    cosT = er(nc.sbuf_tensor("cosT_sb", [128, T], F32))
    ssT = er(nc.sbuf_tensor("ssT_sb", [128, T], F32))
    b.dma("sync", cosT[:], cos_h[:, :])
    b.dma("sync", ssT[:], sin_h[:, :])

    b.phase = "A"
    if upto >= 1:
        pa = ExitStack()
        ea = pa.enter_context
        xld = [ea(nc.sbuf_tensor(f"xld{j}", [128, D], F32)) for j in range(2)]
        xn32 = ea(nc.sbuf_tensor("xn32", [128, D], F32))
        tq32 = ea(nc.sbuf_tensor("tq32", [128, D], F32))
        xnbf = ea(nc.sbuf_tensor("xnbf", [128, D], BF16))
        xqbf = ea(nc.sbuf_tensor("xqbf", [128, D], BF16))
        xqst = ea(nc.sbuf_tensor("xqst", [128, D], BF16))
        trashA = ea(nc.sbuf_tensor("trashA", [128, D], BF16))
        stc = ea(nc.sbuf_tensor("stc", [128, 20], F32))
        pA_ps = ExitStack()
        pT = pA_ps.enter_context(nc.psum_tensor("pTA", [128, 512], BF16))
        pT2 = pA_ps.enter_context(nc.psum_tensor("pTA2", [128, 512], BF16))

        (SU, SQ, MU, EX2, MU2, VAR, VEPS, LNV, INV, I2, MX2, M2E, LNM, RST,
         MX, QS, RM) = range(17)

        xld_free = [None, None]
        st_free = None
        for i in range(NT):
            buf = xld[i % 2]
            b.wait("sync", xld_free[i % 2])
            ev_x = b.dma("sync", buf[:], x_h[i * 128:(i + 1) * 128, :])
            b.wait("vector", ev_x)
            ev_ms = b.do("vector", lambda v, bu=buf: (
                v.drain(),
                v.reduce_sum(stc[:, SU:SU + 1], bu[:], AX.X),
                v.memset(stc[:, SQ:SQ + 1], 0.0))[-1], mark="dv")
            b.waits("scalar", [ev_x, ev_ms])
            ev_sq = b.do("scalar", lambda s, bu=buf: s.activation(
                trashA[:], bu[:], AF.Square, accum_out=stc[:, SQ:SQ + 1]),
                mark="ac")
            b.wait("vector", ev_sq)

            def stats1(v):
                v.tensor_scalar_mul(stc[:, MU:MU + 1], stc[:, SU:SU + 1],
                                    1.0 / D)
                v.tensor_scalar_mul(stc[:, EX2:EX2 + 1], stc[:, SQ:SQ + 1],
                                    1.0 / D)
                v.drain()
                v.tensor_mul(stc[:, MU2:MU2 + 1], stc[:, MU:MU + 1],
                             stc[:, MU:MU + 1])
                v.drain()
                v.tensor_sub(stc[:, VAR:VAR + 1], stc[:, EX2:EX2 + 1],
                             stc[:, MU2:MU2 + 1])
                v.drain()
                return v.tensor_scalar_add(stc[:, VEPS:VEPS + 1],
                                           stc[:, VAR:VAR + 1], EPS)
            ev_v1 = b.do("vector", stats1, mark="dv")
            b.wait("scalar", ev_v1)

            def act_inv(s):
                s.activation(stc[:, LNV:LNV + 1], stc[:, VEPS:VEPS + 1], AF.Ln)
                s.drain()
                return s.activation(stc[:, INV:INV + 1], stc[:, LNV:LNV + 1],
                                    AF.Exp, scale=NEG_HALF)
            ev_inv = b.do("scalar", act_inv, mark="ac")
            b.wait("vector", ev_inv)

            def xnf(v, bu=buf):
                v.drain()
                v.tensor_scalar(xn32[:], bu[:], stc[:, MU:MU + 1],
                                stc[:, INV:INV + 1], ALU.subtract, ALU.mult)
                v.tensor_mul(stc[:, I2:I2 + 1], stc[:, INV:INV + 1],
                             stc[:, INV:INV + 1])
                v.drain()
                v.tensor_copy(xnbf[:], xn32[:])
                v.tensor_mul(stc[:, MX2:MX2 + 1], stc[:, VAR:VAR + 1],
                             stc[:, I2:I2 + 1])
                v.reduce_max(stc[:, MX:MX + 1], xn32[:], AX.X,
                             apply_absolute_value=True)
                v.drain()
                return v.tensor_scalar_add(stc[:, M2E:M2E + 1],
                                           stc[:, MX2:MX2 + 1], EPS)
            ev_v2 = b.do("vector", xnf, mark="dv")
            xld_free[i % 2] = ev_v2
            b.wait("scalar", ev_v2)

            def act_rst(s):
                s.activation(stc[:, LNM:LNM + 1], stc[:, M2E:M2E + 1], AF.Ln)
                s.drain()
                return s.activation(stc[:, RST:RST + 1], stc[:, LNM:LNM + 1],
                                    AF.Exp, scale=NEG_HALF)
            ev_rst = b.do("scalar", act_rst, mark="ac")
            b.wait("vector", ev_rst)

            def quant(v, i=i):
                v.tensor_mul(m1c[:, i:i + 1], stc[:, RST:RST + 1],
                             stc[:, MX:MX + 1])
                v.drain()
                v.tensor_scalar_max(m1c[:, i:i + 1], m1c[:, i:i + 1], EPS)
                v.drain()
                v.reciprocal(stc[:, RM:RM + 1], m1c[:, i:i + 1])
                v.drain()
                v.tensor_mul(stc[:, QS:QS + 1], stc[:, RST:RST + 1],
                             stc[:, RM:RM + 1])
                v.drain()
                v.tensor_scalar_mul(stc[:, QS:QS + 1], stc[:, QS:QS + 1],
                                    127.0)
                v.drain()
                v.tensor_scalar_mul(tq32[:], xn32[:], stc[:, QS:QS + 1])
                v.drain()
                return v.tensor_scalar(xqbf[:], tq32[:], MAGIC, MAGIC,
                                       ALU.add, ALU.subtract)
            ev_q = b.do("vector", quant, mark="dv")

            b.waits("tensor", [ev_v2, ev_idb])
            for gdc in range(4):
                def tr4(t, gdc=gdc):
                    ins = None
                    for j in range(4):
                        dc = gdc * 4 + j
                        ins = t.transpose(pT[:, j * 128:(j + 1) * 128],
                                          xnbf[:, dc * 128:(dc + 1) * 128],
                                          identb[:])
                    return ins
                ev_t = b.do("tensor", tr4, mark="pe")
                b.wait("vector", ev_t)

                def evac(v, gdc=gdc, i=i):
                    dst = xn_fm[:].rearrange("p (dc t) -> p dc t", dc=ND)[
                        :, gdc * 4:(gdc + 1) * 4, i * 128:(i + 1) * 128]
                    return v.tensor_copy(
                        dst, pT[:].rearrange("p (j t) -> p j t", j=4))
                ev_ev = b.do("vector", evac, mark="dv")
                b.wait("tensor", ev_ev)
            b.waits("tensor", [ev_q, st_free])
            last_evq = None
            for gdc in range(4):
                def trq(t, gdc=gdc):
                    ins = None
                    for j in range(4):
                        dc = gdc * 4 + j
                        ins = t.transpose(pT2[:, j * 128:(j + 1) * 128],
                                          xqbf[:, dc * 128:(dc + 1) * 128],
                                          identb[:])
                    return ins
                ev_t = b.do("tensor", trq, mark="pe")
                b.wait("vector", ev_t)

                def evacq(v, gdc=gdc):
                    dst = xqst[:].rearrange("p (dc t) -> p dc t", dc=ND)[
                        :, gdc * 4:(gdc + 1) * 4, :]
                    return v.tensor_copy(
                        dst, pT2[:].rearrange("p (j t) -> p j t", j=4))
                last_evq = b.do("vector", evacq, mark="dv")
                b.wait("tensor", last_evq)
            b.wait("gpsimd", last_evq)
            st_free = b.dma("gpsimd",
                            xq1_sc[i].rearrange("dc d t -> d dc t"),
                            xqst[:].rearrange("p (dc t) -> p dc t", dc=ND))
        if debug:
            b.wait("gpsimd", ("dv", b.cnt.get("dv", 0), None))
            b.dma("gpsimd", dbg["d_xnfm"][:, :], xn_fm[:])
            b.dma("gpsimd", dbg["d_stats"][:, 0:NT], m1c[:])
        b.barrier()
        pa.close()
        pA_ps.close()

    def make_w_thunks(wi, w_h, nch, F, nelem, wch, ternb):
        st = {"ch_free": [None, None], "tern_free": [None, None],
              "z": None, "wsc_done": None}
        thunks = []
        SB = 48 * wi

        def t_zero():
            st["z"] = b.do("vector", lambda v, wi=wi: v.memset(
                wacc[:, wi:wi + 1], 0.0), mark="dv")
        thunks.append(t_zero)

        st["loads"] = {}

        def t_load(c):
            buf = wch[c % 2]
            b.wait("sync", st["ch_free"][c % 2])
            st["loads"][c] = b.dma("sync", buf[:, 0:F],
                                   w_h[c * 128:(c + 1) * 128, :])

        def t_stat(c):
            buf = wch[c % 2]
            b.waits("vector", [st["loads"][c], st["z"]])

            def statw(v, buf=buf):
                v.drain()
                v.tensor_reduce(wacc[:, 2 + wi:3 + wi], buf[:, 0:F], AX.X,
                                ALU.add, apply_absolute_value=True)
                v.drain()
                return v.tensor_add(wacc[:, wi:wi + 1], wacc[:, wi:wi + 1],
                                    wacc[:, 2 + wi:3 + wi])
            st["ch_free"][c % 2] = b.do("vector", statw, mark="dv")
        for c in range(nch):
            if c == 0:
                thunks.append(lambda: t_load(0))
            thunks.append(lambda c=c: (t_load(c + 1) if c + 1 < nch
                                       else None, t_stat(c)))

        def t_scales():
            col = 4 + 2 * wi
            b.wait("vector", st["ch_free"][(nch - 1) % 2])
            prev = b.do("vector", lambda v: (
                v.drain(),
                v.tensor_copy(wacc[:, col:col + 1],
                              wacc[:, wi:wi + 1]))[-1], mark="dv")
            n = 128
            while n > 32:
                h = n // 2
                b.wait("gpsimd", prev)
                ev_d = b.dma("gpsimd", wacc[0:h, col + 1:col + 2],
                             wacc[h:n, col:col + 1])
                b.waits("vector", [ev_d])

                def fold(v, h=h, col=col):
                    v.drain()
                    return v.tensor_tensor(
                        wacc[0:h, col:col + 1], wacc[0:h, col:col + 1],
                        wacc[0:h, col + 1:col + 2], ALU.add)
                prev = b.do("vector", fold, mark="dv")
                n = h
            b.wait("gpsimd", prev)
            ev_g1 = b.dma("gpsimd", wfold_sc[wi, :],
                          wacc[0:32, col:col + 1])
            b.wait("gpsimd", ev_g1)
            ev_g = b.dma("gpsimd", srow[0:1, SB:SB + 32], wfold_sc[wi, :])
            b.wait("vector", ev_g)

            def scals(v, nelem=nelem):
                v.drain()
                v.reduce_sum(srow[0:1, SB + 32:SB + 33],
                             srow[0:1, SB:SB + 32], AX.X)
                v.drain()
                v.tensor_scalar_mul(srow[0:1, SB + 33:SB + 34],
                                    srow[0:1, SB + 32:SB + 33], 1.0 / nelem)
                v.drain()
                v.tensor_scalar_max(srow[0:1, SB + 36:SB + 37],
                                    srow[0:1, SB + 33:SB + 34], EPS)
                v.drain()
                v.tensor_scalar_mul(srow[0:1, SB + 34:SB + 35],
                                    srow[0:1, SB + 36:SB + 37], 0.5)
                v.drain()
                return v.tensor_scalar_mul(
                    srow[0:1, SB + 35:SB + 36],
                    srow[0:1, SB + 36:SB + 37], -0.5)
            prev = b.do("vector", scals, mark="dv")
            b.wait("gpsimd", prev)
            prev = b.dma("gpsimd", wrep_sc[wi, 0:1, :],
                         srow[0:1, SB + 34:SB + 37])
            n = 1
            while n < 128:
                b.wait("gpsimd", prev)
                prev = b.dma("gpsimd", wrep_sc[wi, n:2 * n, :],
                             wrep_sc[wi, 0:n, :])
                n *= 2
            b.wait("gpsimd", prev)
            prev = b.dma("gpsimd", wsc[:, 3 * wi:3 * wi + 3],
                         wrep_sc[wi, :, :])
            st["wsc_done"] = prev
            if wi == 0:
                b.waits("vector", [prev])
                b.do("vector", lambda v: (
                    v.drain(),
                    v.tensor_scalar(c1c[:], m1c[:], wsc[:, 2:3],
                                    1.0 / 127.0, ALU.mult,
                                    ALU.mult))[-1], mark="dv")
        thunks.append(t_scales)

        def t_tern(c):
            buf = wch[c % 2]
            tb = ternb[c % 2]
            b.waits("vector", [st["loads"][c], st["wsc_done"],
                               st["tern_free"][c % 2]])

            def ternf(v, buf=buf, tb=tb, wi=wi):
                v.drain()
                v.tensor_single_scalar(tb[:, 0:F], buf[:, 0:F],
                                       wsc[:, 3 * wi + 1:3 * wi + 2],
                                       ALU.is_lt)
                v.drain()
                return v.scalar_tensor_tensor(
                    tb[:, 0:F], buf[:, 0:F],
                    wsc[:, 3 * wi:3 * wi + 1], tb[:, 0:F],
                    ALU.is_gt, ALU.subtract)
            ev_t = b.do("vector", ternf, mark="dv")
            st["ch_free"][c % 2] = ev_t
            b.wait("gpsimd", ev_t)
            if wi == 0:
                dst = w1t_sc[:, c, :, :].rearrange("ob d o -> d ob o")
                src = tb[:, 0:F].rearrange("d (ob o) -> d ob o", ob=NOB)
            else:
                dst = w2t_sc[:, c, :, :].rearrange("dpc o dp -> o dpc dp")
                src = tb[:, 0:F].rearrange("o (dpc dp) -> o dpc dp",
                                           dpc=ND)
            st["tern_free"][c % 2] = b.dma("gpsimd", dst, src)
        for c in range(nch):
            if c == 0:
                thunks.append(lambda: t_load(0))
            thunks.append(lambda c=c: (t_load(c + 1) if c + 1 < nch
                                       else None, t_tern(c)))
        return thunks

    kv_ctx = ExitStack()
    k_fm = kv_ctx.enter_context(
        nc.sbuf_tensor("k_fm", [128, NKV * T], BF16, side="right"))
    v_km = kv_ctx.enter_context(
        nc.sbuf_tensor("v_km", [128, NKV * T], BF16, side="right"))
    b.phase = "P1kv"
    if upto >= 3:
        p1 = ExitStack()
        e1 = p1.enter_context
        wres = e1(nc.sbuf_tensor("wres1", [128, 16 * 1024], BF16))
        wstg = [e1(nc.sbuf_tensor(f"wstg{j}", [128, 1024], F32))
                for j in range(2)]
        a32 = e1(nc.sbuf_tensor("a32", [128, 1024], F32))
        b32 = e1(nc.sbuf_tensor("b32", [128, 1024], F32))
        vstg = e1(nc.sbuf_tensor("vstg", [128, 1024], BF16))
        p1_ps = ExitStack()
        pA = p1_ps.enter_context(nc.psum_tensor("pA1", [128, 1024], F32))
        pB = p1_ps.enter_context(nc.psum_tensor("pB1", [128, 1024], F32))
        pTv = p1_ps.enter_context(nc.psum_tensor("pTv", [128, 512], BF16))

        last_pe_half = None
        stg_free = [None, None]
        prev_evac = [None, None]
        for half in range(2):
            cast_evs = []
            for dc in range(ND):
                sb = wstg[dc % 2]
                b.wait("sync", stg_free[dc % 2])
                ev_w = b.dma("sync", sb[:],
                             wkvq_h[dc * 128:(dc + 1) * 128,
                                    half * 1024:(half + 1) * 1024])
                b.waits("scalar", [ev_w, last_pe_half])
                ev_c = b.do("scalar", lambda s, sb=sb, dc=dc: s.copy(
                    wres[:, dc * 1024:(dc + 1) * 1024], sb[:]), mark="ac")
                stg_free[dc % 2] = ev_c
                cast_evs.append(ev_c)
            b.waits("tensor", cast_evs)
            unit = 0
            for oc in range(8):
                for tb in range(2):
                    ps = pA if unit % 2 == 0 else pB
                    b.wait("tensor", prev_evac[unit % 2])

                    def mmkv(t, oc=oc, tb=tb, ps=ps):
                        ins = None
                        for dc in range(ND):
                            for nb in range(2):
                                ins = t.matmul(
                                    ps[:, nb * 512:(nb + 1) * 512],
                                    wres[:, dc * 1024 + oc * 128:
                                         dc * 1024 + (oc + 1) * 128],
                                    xn_fm[:, dc * T + tb * 1024 + nb * 512:
                                          dc * T + tb * 1024 +
                                          (nb + 1) * 512],
                                    start=(dc == 0), stop=(dc == ND - 1))
                        return ins
                    ev_mm = b.do("tensor", mmkv, mark="pe")
                    b.wait("vector", ev_mm)
                    if half == 0:
                        ev_cp = b.do("vector", lambda v, ps=ps:
                                     v.tensor_copy(vstg[:], ps[:]),
                                     mark="dv")
                        prev_evac[unit % 2] = ev_cp
                        b.wait("tensor", ev_cp)
                        for grp in range(2):
                            def trv(t, grp=grp):
                                ins = None
                                for j in range(4):
                                    ins = t.transpose(
                                        pTv[:, j * 128:(j + 1) * 128],
                                        vstg[:, (grp * 4 + j) * 128:
                                             (grp * 4 + j + 1) * 128],
                                        identb[:])
                                return ins
                            ev_t = b.do("tensor", trv, mark="pe")
                            b.wait("vector", ev_t)

                            def evv(v, oc=oc, tb=tb, grp=grp):
                                kt0 = oc * 16 + tb * 8 + grp * 4
                                return v.tensor_copy(
                                    v_km[:, kt0 * 128:(kt0 + 4) * 128],
                                    pTv[:])
                            ev_e = b.do("vector", evv, mark="dv")
                            b.wait("tensor", ev_e)
                    else:
                        def ropek(v, ps=ps, oc=oc, tb=tb):
                            sl = slice(tb * 1024, (tb + 1) * 1024)
                            v.drain()
                            v.tensor_mul(a32[:], ps[:], cosT[:, sl])
                            v.tensor_mul(b32[0:64, :], ps[64:128, :],
                                         ssT[0:64, sl])
                            v.tensor_mul(b32[64:128, :], ps[0:64, :],
                                         ssT[64:128, sl])
                            v.drain()
                            return v.tensor_add(
                                k_fm[:, oc * T + tb * 1024:
                                     oc * T + (tb + 1) * 1024],
                                a32[:], b32[:])
                        ev_r = b.do("vector", ropek, mark="dv")
                        prev_evac[unit % 2] = ev_r
                    unit += 1
            last_pe_half = ("pe", b.cnt.get("pe", 0), "tensor")
        if debug:
            b.wait("gpsimd", ("dv", b.cnt.get("dv", 0), None))
            b.dma("gpsimd", dbg["d_kfm"][:, :], k_fm[:])
            b.dma("gpsimd", dbg["d_vkm"][:, :], v_km[:])
        b.barrier()
        p1.close()
        p1_ps.close()

    b.phase = "P2q"
    if upto >= 4:
        p2 = ExitStack()
        e2 = p2.enter_context
        wres2 = e2(nc.sbuf_tensor("wres2", [128, 16 * 1024], BF16))
        wstg2 = e2(nc.sbuf_tensor("wstg2", [128, 1024], F32))
        a32q = e2(nc.sbuf_tensor("a32q", [128, 1024], F32))
        b32q = e2(nc.sbuf_tensor("b32q", [128, 1024], F32))
        qraw = e2(nc.sbuf_tensor("qraw", [128, 1024], F32))
        qstg = [e2(nc.sbuf_tensor(f"qstg{j}", [128, 1024], BF16))
                for j in range(2)]
        p2_ps = ExitStack()
        pA2 = p2_ps.enter_context(nc.psum_tensor("pA2", [128, 1024], F32))
        pB2 = p2_ps.enter_context(nc.psum_tensor("pB2", [128, 1024], F32))

        last_pe_qr = None
        stg_free = None
        prev_rope = [None, None]
        prev_evq = [None, None]
        prev_pq = None
        prev_r_any = None
        st_free = [None, None]
        for qr in range(4):
            cast_evs = []
            for dc in range(ND):
                b.wait("sync", stg_free)
                ev_w = b.dma("sync", wstg2[:],
                             wkvq_h[dc * 128:(dc + 1) * 128,
                                    2048 + qr * 1024:2048 + (qr + 1) * 1024])
                b.waits("scalar", [ev_w, last_pe_qr])
                ev_c = b.do("scalar", lambda s, dc=dc: s.copy(
                    wres2[:, dc * 1024:(dc + 1) * 1024], wstg2[:]),
                    mark="ac")
                stg_free = ev_c
                cast_evs.append(ev_c)
            b.waits("tensor", cast_evs)
            unit = 0
            for oc in range(8):
                h = qr * 8 + oc
                for tb in range(2):
                    ps = pA2 if unit % 2 == 0 else pB2
                    b.wait("tensor", prev_rope[unit % 2])

                    def mmq(t, oc=oc, tb=tb, ps=ps):
                        ins = None
                        for dc in range(ND):
                            for nb in range(2):
                                ins = t.matmul(
                                    ps[:, nb * 512:(nb + 1) * 512],
                                    wres2[:, dc * 1024 + oc * 128:
                                          dc * 1024 + (oc + 1) * 128],
                                    xn_fm[:, dc * T + tb * 1024 + nb * 512:
                                          dc * T + tb * 1024 +
                                          (nb + 1) * 512],
                                    start=(dc == 0), stop=(dc == ND - 1))
                        return ins
                    ev_mm = b.do("tensor", mmq, mark="pe")
                    qs_buf = qstg[unit % 2]
                    b.waits("vector", [ev_mm, st_free[unit % 2]])

                    def ropeq(v, ps=ps, tb=tb, qs_buf=qs_buf):
                        sl = slice(tb * 1024, (tb + 1) * 1024)
                        v.drain()
                        v.tensor_mul(a32q[:], ps[:], cosT[:, sl])
                        v.tensor_mul(b32q[0:64, :], ps[64:128, :],
                                     ssT[0:64, sl])
                        v.tensor_mul(b32q[64:128, :], ps[0:64, :],
                                     ssT[64:128, sl])
                        v.drain()
                        return v.tensor_add(qs_buf[:], a32q[:], b32q[:])
                    ev_r = b.do("vector", ropeq, mark="dv")
                    prev_rope[unit % 2] = ev_r
                    b.wait("gpsimd", ev_r)
                    st_free[unit % 2] = b.dma(
                        "gpsimd",
                        q_sc[h // 4, :, tb * 8:(tb + 1) * 8, h % 4, :],
                        qs_buf[:].rearrange("p (qt t) -> p qt t", qt=8))
                    unit += 1
            last_pe_qr = ("pe", b.cnt.get("pe", 0), "tensor")
        b.barrier()
        p2.close()
        p2_ps.close()
    rope_ctx.close()
    xn_ctx.close()

    b.phase = "P3flash"
    if upto >= 5:
        p3 = ExitStack()
        e3 = p3.enter_context
        qbuf = [e3(nc.sbuf_tensor(f"qbuf{j}", [128, 4096], BF16))
                for j in range(2)]
        Pp = [e3(nc.sbuf_tensor(f"Pp{j}", [128, 1024], BF16))
              for j in range(2)]
        zl = e3(nc.sbuf_tensor("zl", [128, 1024], F32))
        zr = e3(nc.sbuf_tensor("zr", [128, 1024], F32))
        atile = [e3(nc.sbuf_tensor(f"atile{j}", [128, 1024], BF16))
                 for j in range(2)]
        p3_ps = ExitStack()
        pS = [p3_ps.enter_context(nc.psum_tensor(f"pS{j}", [128, 1024], F32))
              for j in range(2)]
        pAV = p3_ps.enter_context(nc.psum_tensor("pAV", [128, 1024], F32))
        pZ = p3_ps.enter_context(nc.psum_tensor("pZ", [128, 1024], F32))
        wch1 = [e3(nc.sbuf_tensor(f"wch1_{j}", [128, DI], F32))
                for j in range(2)]
        tern1 = [e3(nc.sbuf_tensor(f"tern1_{j}", [128, DI], BF16))
                 for j in range(2)]
        w1_th = (make_w_thunks(0, w1_h, 16, DI, D * DI, wch1, tern1)
                 if upto >= 2 else [])

        q_free = [None, None]
        at_free = [None, None]
        ev_evac = None
        ev_zr = None
        ai = 0
        for qt in range(NT):
            qb = qbuf[qt % 2]
            b.wait("sync", q_free[qt % 2])
            ev_q = b.dma("sync", qb[:].rearrange("p (kvh f) -> p kvh f",
                                                 kvh=NKV),
                         q_sc[:, :, qt, :, :].rearrange(
                             "kvh hd g t -> hd kvh (g t)"))
            last_smm = None
            exp_evs = {}
            pe_z = {}
            g = 0
            for kvp in range(4):
                kvh0, kvh1 = 2 * kvp, 2 * kvp + 1

                def emit_smm(kt, g, kvh0=kvh0, kvh1=kvh1, qt=qt, qb=qb):
                    par = g % 2
                    b.waits("tensor", [ev_q, exp_evs.get(g - 2), ev_evac])

                    def smm(t, kt=kt, par=par):
                        t.matmul(pS[par][:, 0:512],
                                 k_fm[:, kvh0 * T + kt * 128:
                                      kvh0 * T + (kt + 1) * 128],
                                 qb[:, kvh0 * 512:(kvh0 + 1) * 512],
                                 start=True, stop=True)
                        return t.matmul(pS[par][:, 512:1024],
                                        k_fm[:, kvh1 * T + kt * 128:
                                             kvh1 * T + (kt + 1) * 128],
                                        qb[:, kvh1 * 512:(kvh1 + 1) * 512],
                                        start=True, stop=True)
                    ev_s = b.do("tensor", smm, mark="pe")
                    src_ev = ev_s
                    if kt == qt:
                        b.wait("vector", ev_s)
                        ev_m = b.do("vector", lambda v, par=par:
                                    v.tensor_tensor(pS[par][:], pS[par][:],
                                                    tri2[:], ALU.add),
                                    mark="dv")
                        src_ev = ev_m
                    b.waits("scalar", [src_ev, pe_z.get(g - 2)])
                    ev_e = b.do("scalar", lambda s, par=par: s.activation(
                        Pp[par][:], pS[par][:], AF.Exp, scale=SC_ATT),
                        mark="ac")
                    exp_evs[g] = ev_e
                    return ev_s

                def emit_av(kt, g, kvh0=kvh0, kvh1=kvh1, qt=qt):
                    par = g % 2
                    b.wait("tensor", exp_evs[g])
                    b.do("tensor", lambda t, kt=kt, par=par:
                         t.matmul(pAV[:, 0:512],
                                  v_km[:, (kvh0 * 16 + kt) * 128:
                                       (kvh0 * 16 + kt + 1) * 128],
                                  Pp[par][:, 0:512], start=(kt == 0),
                                  stop=(kt == qt)))
                    b.do("tensor", lambda t, kt=kt, par=par:
                         t.matmul(pAV[:, 512:1024],
                                  v_km[:, (kvh1 * 16 + kt) * 128:
                                       (kvh1 * 16 + kt + 1) * 128],
                                  Pp[par][:, 512:1024], start=(kt == 0),
                                  stop=(kt == qt)))
                    b.do("tensor", lambda t, kt=kt, par=par:
                         t.matmul(pZ[:, 0:512], ones_bf[:],
                                  Pp[par][:, 0:512], start=(kt == 0),
                                  stop=(kt == qt)))
                    pe_z[g] = b.do(
                        "tensor", lambda t, kt=kt, par=par:
                        t.matmul(pZ[:, 512:1024], ones_bf[:],
                                 Pp[par][:, 512:1024], start=(kt == 0),
                                 stop=(kt == qt)), mark="pe")
                    return pe_z[g]

                for kt in range(qt + 1):
                    last_smm = emit_smm(kt, g + kt)
                    if kt >= 1:
                        emit_av(kt - 1, g + kt - 1)
                ev_zlast = emit_av(qt, g + qt)
                g += qt + 1
                at = atile[ai % 2]
                b.waits("scalar", [ev_zlast])

                def evz(s):
                    s.activation(zl[:], pZ[:], AF.Ln)
                    s.drain()
                    return s.activation(zr[:], zl[:], AF.Exp, scale=-1.0)
                ev_zr = b.do("scalar", evz, mark="ac")
                b.waits("vector", [ev_zr, at_free[ai % 2]])
                ev_evac = b.do("vector", lambda v, at=at: v.tensor_tensor(
                    at[:], pAV[:], zr[:], ALU.mult), mark="dv")
                b.wait("gpsimd", ev_evac)
                at_free[ai % 2] = b.dma(
                    "gpsimd",
                    at_sc[kvh0:kvh0 + 2, :, qt, :].rearrange(
                        "kv hd f -> hd kv f"),
                    at[:].rearrange("p (kv f) -> p kv f", kv=2))
                ai += 1
                if w1_th:
                    w1_th.pop(0)()
            q_free[qt % 2] = last_smm
        while w1_th:
            w1_th.pop(0)()
        if debug:
            b.wait("gpsimd", ("dv", b.cnt.get("dv", 0), None))
            b.dma("gpsimd", dbg["d_wsc"][:, 0:3], wsc[:, 0:3])
        b.barrier()
        p3.close()
        p3_ps.close()
    kv_ctx.close()

    b.phase = "F1"
    if upto >= 6:
        pf = ExitStack()
        ef = pf.enter_context
        xq1b = ef(nc.sbuf_tensor("xq1b", [128, ND * T], BF16))
        w1b = [ef(nc.sbuf_tensor(f"w1b{j}", [128, DI], BF16))
               for j in range(2)]
        b1bc = ef(nc.sbuf_tensor("b1bc", [128, DI], F32))
        b1row_t = ef(nc.sbuf_tensor("b1row_t", [1, DI], F32))
        b1row = b1row_t[0:1, :]
        ev_b1l = b.dma("sync", b1row, b1_h[0:1, :])
        htmp = [ef(nc.sbuf_tensor(f"htmp{j}", [128, 512], F32))
                for j in range(2)]
        hsil = [ef(nc.sbuf_tensor(f"hsil{j}", [128, 512], F32))
                for j in range(2)]
        pf_ps = ExitStack()
        pF = [pf_ps.enter_context(nc.psum_tensor(f"pF{j}", [128, 512], F32))
              for j in range(2)]
        pB5 = pf_ps.enter_context(nc.psum_tensor("pB5", [128, 512], F32))
        wch2 = [ef(nc.sbuf_tensor(f"wch2_{j}", [128, D], F32))
                for j in range(2)]
        tern2 = [ef(nc.sbuf_tensor(f"tern2_{j}", [128, D], BF16))
                 for j in range(2)]
        w2_th = (make_w_thunks(1, w2_h, 64, D, DI * D, wch2, tern2)
                 if upto >= 2 else [])

        ev_xq = []
        for qt in range(NT):
            dst = xq1b[:].rearrange("p (dc t) -> p dc t", dc=ND)[
                :, :, qt * 128:(qt + 1) * 128]
            ev_xq.append(b.dma("sync", dst,
                               xq1_sc[qt].rearrange("dc d t -> d dc t")))
        b.wait("tensor", ev_b1l)
        for ob in range(NOB):
            ev_b = b.do("tensor", lambda t, ob=ob: t.matmul(
                pB5[:], onec[0:1, :], b1row[0:1, ob * 512:(ob + 1) * 512],
                start=True, stop=True), mark="pe")
            b.wait("vector", ev_b)
            ev_v = b.do("vector", lambda v, ob=ob: (
                v.tensor_copy(b1bc[:, ob * 512:(ob + 1) * 512], pB5[:]),
                v.drain())[0], mark="dv")
            b.wait("tensor", ev_v)

        w_free = [None, None]
        ui = 0
        h_free = [None, None]
        ev_evF = [None, None]
        for ob in range(NOB):
            wb_ = w1b[ob % 2]
            b.wait("sync", w_free[ob % 2])
            ev_w = b.dma("sync", wb_[:].rearrange("p (dc o) -> p dc o",
                                                  dc=ND),
                         w1t_sc[ob].rearrange("dc d o -> d dc o"))
            last_mm = None
            for qt in range(NT):
                ps = pF[ui % 2]
                b.waits("tensor", ev_xq + [ev_w, ev_evF[ui % 2]])

                def mmf(t, ob=ob, qt=qt, ps=ps, wb_=wb_):
                    ins = None
                    for dc in range(ND):
                        ins = t.matmul(ps[:],
                                       xq1b[:, dc * T + qt * 128:
                                            dc * T + (qt + 1) * 128],
                                       wb_[:, dc * 512:(dc + 1) * 512],
                                       start=(dc == 0), stop=(dc == ND - 1))
                    return ins
                ev_mm = b.do("tensor", mmf, mark="pe")
                last_mm = ev_mm
                b.waits("vector", [ev_mm, h_free[ui % 2]])

                def evf(v, ps=ps, qt=qt, ob=ob, j=ui % 2):
                    return v.scalar_tensor_tensor(
                        htmp[j][:], ps[:], c1c[:, qt:qt + 1],
                        b1bc[:, ob * 512:(ob + 1) * 512],
                        ALU.mult, ALU.add)
                ev_e = b.do("vector", evf, mark="dv")
                ev_evF[ui % 2] = ev_e
                b.wait("scalar", ev_e)
                ev_s = b.do("scalar", lambda s, j=ui % 2: s.activation(
                    hsil[j][:], htmp[j][:], AF.Silu), mark="ac")
                b.wait("gpsimd", ev_s)
                h_free[ui % 2] = b.dma(
                    "gpsimd", h_sc[qt, :, ob * 512:(ob + 1) * 512],
                    hsil[ui % 2][:])
                ui += 1
                if w2_th and ui % 2 == 0:
                    w2_th.pop(0)()
            w_free[ob % 2] = last_mm
        while w2_th:
            w2_th.pop(0)()
        b.barrier()
        pf.close()
        pf_ps.close()

    b.phase = "P4out"
    if upto >= 7:
        p4 = ExitStack()
        e4 = p4.enter_context
        woutb = e4(nc.sbuf_tensor("woutb", [128, 32 * D], BF16))
        wst4 = [e4(nc.sbuf_tensor(f"wst4{j}", [128, D], F32))
                for j in range(2)]
        abuf = [e4(nc.sbuf_tensor(f"abuf{j}", [128, 4096], BF16))
                for j in range(2)]
        sst = [e4(nc.sbuf_tensor(f"sst{j}", [128, D], BF16))
               for j in range(2)]
        p4_ps = ExitStack()
        pO = [p4_ps.enter_context(nc.psum_tensor(f"pO{j}", [128, 512], F32))
              for j in range(2)]

        stg_free = [None, None]
        for h in range(32):
            sb = wst4[h % 2]
            b.wait("sync", stg_free[h % 2])
            ev_w = b.dma("sync", sb[:], wout_h[h * 128:(h + 1) * 128, :])
            b.wait("scalar", ev_w)
            ev_c = b.do("scalar", lambda s, sb=sb, h=h: s.copy(
                woutb[:, h * D:(h + 1) * D], sb[:]), mark="ac")
            stg_free[h % 2] = ev_c
        cast_done = stg_free[31 % 2]
        a_free = [None, None]
        s_free = [None, None]
        ev_ev4 = [None, None]
        for qt in range(NT):
            ab = abuf[qt % 2]
            b.wait("sync", a_free[qt % 2])
            ev_a = b.dma("sync", ab[:].rearrange("p (kvh f) -> p kvh f",
                                                 kvh=NKV),
                         at_sc[:, :, qt, :].rearrange("kvh hd f -> hd kvh f"))
            ss = sst[qt % 2]
            b.wait("vector", s_free[qt % 2])
            last_mm = None
            for dpb in range(4):
                ps = pO[dpb % 2]
                b.waits("tensor", [ev_a, cast_done, ev_ev4[dpb % 2]])

                def mmo(t, dpb=dpb, ps=ps, ab=ab):
                    ins = None
                    for h in range(32):
                        ins = t.matmul(
                            ps[:],
                            ab[:, (h // 4) * 512 + (h % 4) * 128:
                               (h // 4) * 512 + (h % 4 + 1) * 128],
                            woutb[:, h * D + dpb * 512:
                                  h * D + (dpb + 1) * 512],
                            start=(h == 0), stop=(h == 31))
                    return ins
                ev_mm = b.do("tensor", mmo, mark="pe")
                last_mm = ev_mm
                b.wait("vector", ev_mm)
                ev_e = b.do("vector", lambda v, ps=ps, dpb=dpb, ss=ss:
                            v.tensor_copy(ss[:, dpb * 512:(dpb + 1) * 512],
                                          ps[:]), mark="dv")
                ev_ev4[dpb % 2] = ev_e
            a_free[qt % 2] = last_mm
            b.wait("gpsimd", ev_ev4[1])
            b.wait("gpsimd", ev_ev4[0])
            s_free[qt % 2] = b.dma("gpsimd", s_sc[qt], ss[:])
        b.barrier()
        p4.close()
        p4_ps.close()

    b.phase = "F2"
    if upto >= 8:
        pg = ExitStack()
        eg = pg.enter_context
        gbc = eg(nc.sbuf_tensor("gbc", [128, DI], F32))
        bbc = eg(nc.sbuf_tensor("bbc", [128, DI], F32))
        stq = eg(nc.sbuf_tensor("stq", [128, 34], F32))
        pg_ps = ExitStack()
        pG = pg_ps.enter_context(nc.psum_tensor("pG", [128, 512], F32))
        pT3 = [pg_ps.enter_context(nc.psum_tensor(f"pT3{j}", [128, 512],
                                                  BF16)) for j in range(2)]

        rows_ctx = ExitStack()
        growb = rows_ctx.enter_context(
            nc.sbuf_tensor("growb", [1, 2 * DI], F32))
        ev_gl = b.dma("sync", growb[0:1, 0:DI], g2_h[0:1, :])
        ev_bl = b.dma("sync", growb[0:1, DI:2 * DI], bb2_h[0:1, :])
        b.waits("tensor", [ev_gl, ev_bl])
        ev_gb_done = None
        for ri, dstt in ((0, gbc), (1, bbc)):
            for ob in range(NOB):
                ev_b = b.do("tensor", lambda t, ob=ob, ri=ri:
                            t.matmul(pG[:], onec[0:1, :],
                                     growb[0:1, ri * DI + ob * 512:
                                           ri * DI + (ob + 1) * 512],
                                     start=True, stop=True), mark="pe")
                b.wait("vector", ev_b)
                ev_v = b.do("vector", lambda v, ob=ob, dstt=dstt: (
                    v.tensor_copy(dstt[:, ob * 512:(ob + 1) * 512], pG[:]),
                    v.drain())[0], mark="dv")
                b.wait("tensor", ev_v)
                ev_gb_done = ev_v
        rows_ctx.close()
        hb = eg(nc.sbuf_tensor("hbF", [128, DI], F32))
        ab2 = [eg(nc.sbuf_tensor(f"ab2{j}", [128, DI], F32))
               for j in range(2)]
        xqt2 = [eg(nc.sbuf_tensor(f"xqt2{j}", [128, DI], BF16))
                for j in range(2)]

        (ASUM, ASQ, MU2c, EX2c, VARc, VEPSc, LNc, INVc, SQ2, EXRc, LNRc,
         RST2, T2c, M2c, RM2c, QS2c, NMIc) = range(17)

        h_free = [ev_gb_done, ev_gb_done]
        st2_free = [None, None]
        pool_prev = None
        sq2_prev = None
        hb_free = [None]
        for qt in range(NT):
            j = qt % 2
            B0 = 17 * j
            ab = ab2[j]
            xq = xqt2[j]

            def C(i):
                return stq[:, B0 + i:B0 + i + 1]
            b.waits("sync", [h_free[j], hb_free[0]])
            ev_h = b.dma("sync", hb[:], h_sc[qt])
            ev_ms = b.do("vector", lambda v, C=C: (
                v.memset(C(ASUM), 0.0),
                v.memset(C(ASQ), 0.0),
                v.memset(C(SQ2), 0.0))[-1], mark="dv")
            b.waits("scalar", [ev_h, ev_ms, h_free[j], st2_free[j]])

            def gelu_sq(s, C=C, ab=ab, xq=xq):
                s.activation(ab[:], hb[:], AF.Gelu, accum_out=C(ASUM))
                s.drain()
                return s.activation(xq[:], ab[:], AF.Square,
                                    accum_out=C(ASQ))
            ev_g = b.do("scalar", gelu_sq, mark="ac")
            hb_free[0] = ev_g
            b.wait("vector", ev_g)

            def stats2(v, C=C):
                v.tensor_scalar_mul(C(MU2c), C(ASUM), 1.0 / DI)
                v.tensor_scalar_mul(C(EX2c), C(ASQ), 1.0 / DI)
                v.drain()
                v.tensor_mul(C(VARc), C(MU2c), C(MU2c))
                v.drain()
                v.tensor_sub(C(VARc), C(EX2c), C(VARc))
                v.drain()
                return v.tensor_scalar_add(C(VEPSc), C(VARc), EPS)
            ev_s2 = b.do("vector", stats2, mark="dv")
            b.wait("scalar", ev_s2)
            ev_i2 = b.do("scalar", lambda s, C=C: (
                s.activation(C(LNc), C(VEPSc), AF.Ln),
                s.drain(),
                s.activation(C(INVc), C(LNc), AF.Exp,
                             scale=NEG_HALF))[-1], mark="ac")
            b.wait("vector", ev_i2)
            ev_nm = b.do("vector", lambda v, C=C: (
                v.drain(),
                v.tensor_mul(C(NMIc), C(MU2c), C(INVc)),
                v.drain(),
                v.tensor_scalar_mul(C(NMIc), C(NMIc), -1.0))[-1], mark="dv")
            b.wait("scalar", ev_nm)
            ev_zn = b.do("scalar", lambda s, C=C, ab=ab: (
                s.drain(),
                s.activation(ab[:], ab[:], AF.Identity, bias=C(NMIc),
                             scale=C(INVc)))[-1], mark="ac")
            b.waits("gpsimd", [ev_zn, pool_prev])
            ev_gm = b.do("gpsimd", lambda g, ab=ab: (
                g.tensor_mul(ab[:], ab[:], gbc[:]),
                g.drain(),
                g.tensor_add(ab[:], ab[:], bbc[:]))[-1], mark="gp")
            pool_prev = ev_gm
            b.wait("vector", ev_gm)
            ev_mx = b.do("vector", lambda v, C=C, ab=ab: v.reduce_max(
                C(T2c), ab[:], AX.X, apply_absolute_value=True), mark="dv")
            b.waits("scalar", [ev_gm, sq2_prev])
            ev_sq2 = b.do("scalar", lambda s, C=C, ab=ab, xq=xq:
                          s.activation(xq[:], ab[:], AF.Square,
                                       accum_out=C(SQ2)), mark="ac")
            sq2_prev = ev_sq2
            b.wait("vector", ev_sq2)

            def preR(v, C=C):
                v.tensor_scalar_mul(C(EXRc), C(SQ2), 1.0 / DI)
                v.drain()
                return v.tensor_scalar_add(C(EXRc), C(EXRc), EPS)
            ev_pr = b.do("vector", preR, mark="dv")
            b.wait("scalar", ev_pr)
            ev_r2 = b.do("scalar", lambda s, C=C: (
                s.activation(C(LNRc), C(EXRc), AF.Ln),
                s.drain(),
                s.activation(C(RST2), C(LNRc), AF.Exp,
                             scale=NEG_HALF))[-1], mark="ac")
            b.waits("vector", [ev_r2, ev_mx])

            def quant2w(v, qt=qt, C=C, ab=ab, xq=xq):
                v.tensor_mul(C(M2c), C(RST2), C(T2c))
                v.drain()
                v.tensor_scalar_max(C(M2c), C(M2c), EPS)
                v.drain()
                v.tensor_scalar(c2c[:, qt:qt + 1], C(M2c), wsc[:, 5:6],
                                1.0 / 127.0, ALU.mult, ALU.mult)
                v.reciprocal(C(RM2c), C(M2c))
                v.drain()
                v.tensor_mul(C(QS2c), C(RST2), C(RM2c))
                v.drain()
                v.tensor_scalar_mul(C(QS2c), C(QS2c), 127.0)
                v.drain()
                v.tensor_scalar(ab[:], ab[:], C(QS2c), MAGIC, ALU.mult,
                                ALU.add)
                v.drain()
                return v.tensor_single_scalar(xq[:], ab[:], MAGIC,
                                              ALU.subtract)
            ev_q2 = b.do("vector", quant2w, mark="dv")
            h_free[j] = ev_q2
            b.waits("tensor", [ev_q2, st2_free[j]])
            last_ev = None
            evx_prev = [None, None]
            for grp in range(NT):
                pt = pT3[grp % 2]

                def trx(t, grp=grp, pt=pt, xq=xq):
                    ins = None
                    for jj in range(4):
                        oc = grp * 4 + jj
                        ins = t.transpose(pt[:, jj * 128:(jj + 1) * 128],
                                          xq[:, oc * 128:(oc + 1) * 128],
                                          identb[:])
                    return ins
                b.wait("tensor", evx_prev[grp % 2])
                ev_t = b.do("tensor", trx, mark="pe")
                b.wait("vector", ev_t)

                def evx(v, grp=grp, pt=pt, xq=xq):
                    dst = xq[:].rearrange("p (oc t) -> p oc t", oc=NOC)[
                        :, grp * 4:(grp + 1) * 4, :]
                    return v.tensor_copy(dst, pt[:].rearrange(
                        "p (j t) -> p j t", j=4))
                last_ev = b.do("vector", evx, mark="dv")
                evx_prev[grp % 2] = last_ev
            b.wait("gpsimd", last_ev)
            st2_free[j] = b.dma(
                "gpsimd",
                xq2_sc[:, :, qt * 128:(qt + 1) * 128].rearrange(
                    "oc o t -> o oc t"),
                xqt2[j][:].rearrange("p (oc t) -> p oc t", oc=NOC))
        b.barrier()
        pg.close()
        pg_ps.close()

    b.phase = "M2"
    if upto >= 9:
        pm = ExitStack()
        em = pm.enter_context
        x2q = em(nc.sbuf_tensor("x2q", [128, NOC * 512], BF16))
        w2b = [em(nc.sbuf_tensor(f"w2b{j}", [128, NOC * 128], BF16))
               for j in range(2)]
        c2bc = em(nc.sbuf_tensor("c2bc", [128, 512], F32))
        c2r = em(nc.sbuf_tensor("c2r", [1, 128], F32))
        b2row_t = em(nc.sbuf_tensor("b2row_t", [1, D], F32))
        b2row = b2row_t[0:1, :]
        ev_b2l = b.dma("sync", b2row, b2_h[0:1, :])
        mtmp = [em(nc.sbuf_tensor(f"mtmp{j}", [128, 512],
                                  F32)) for j in range(2)]
        gb_ = [em(nc.sbuf_tensor(f"gb_{j}", [128, 512],
                                 BF16)) for j in range(2)]
        gq = em(nc.sbuf_tensor("gq", [128, 4 * D], BF16))
        xcb = em(nc.sbuf_tensor("xcb", [128, D], F32))
        scb2 = em(nc.sbuf_tensor("scb2", [128, D], BF16))
        sg32 = em(nc.sbuf_tensor("sg32", [128, D], F32))
        ocb = em(nc.sbuf_tensor("ocb", [128, D], F32))
        pm_ps = ExitStack()
        pM = [pm_ps.enter_context(nc.psum_tensor(f"pM{j}", [128, 512], F32))
              for j in range(2)]
        pTg = [pm_ps.enter_context(nc.psum_tensor(f"pTg{j}",
               [128, 512], BF16)) for j in range(2)]
        pCt = pm_ps.enter_context(nc.psum_tensor("pCt", [1, 128], F32))
        pCb = pm_ps.enter_context(nc.psum_tensor("pCb", [128, 128], F32))

        b.wait("tensor", ev_b2l)
        for dpc in range(ND):
            ev_b = b.do("tensor", lambda t, dpc=dpc: t.matmul(
                pCb[:, 0:1], b2row[0:1, dpc * 128:(dpc + 1) * 128],
                one11[0:1, 0:1], start=True, stop=True), mark="pe")
            b.wait("vector", ev_b)
            ev_v = b.do("vector", lambda v, dpc=dpc: v.tensor_copy(
                b2cols[:, dpc:dpc + 1], pCb[:, 0:1]), mark="dv")
            b.wait("tensor", ev_v)

        x2_free = None
        w_free = [None, None]
        ev_evM = [None, None]
        ev_gev = None
        prev_sl = None
        prev_tg = None
        for quar in range(4):
            b.wait("sync", x2_free)
            ev_x2 = b.dma("sync",
                          x2q[:].rearrange("p (oc t) -> p oc t", oc=NOC),
                          xq2_sc[:, :, quar * 512:(quar + 1) * 512]
                          .rearrange("oc o t -> o oc t"))
            for j in range(4):
                qt = quar * 4 + j
                ev_t = b.do("tensor", lambda t, qt=qt: t.matmul(
                    pCt[0:1, :], c2c[:, qt:qt + 1], identf[:],
                    start=True, stop=True), mark="pe")
                b.wait("vector", ev_t)
                ev_v = b.do("vector", lambda v: v.tensor_copy(c2r[:],
                                                              pCt[0:1, :]),
                            mark="dv")
                b.wait("tensor", ev_v)
                ev_b2 = b.do("tensor", lambda t: t.matmul(
                    pCb[:], onec[0:1, :], c2r[0:1, :], start=True,
                    stop=True), mark="pe")
                b.wait("vector", ev_b2)
                ev_c = b.do("vector", lambda v, j=j: (
                    v.tensor_copy(c2bc[:, j * 128:(j + 1) * 128], pCb[:]),
                    v.drain())[0], mark="dv")
                b.wait("tensor", ev_c)

            mm_evs = {}

            def emit_post(dpc):
                ps = pM[dpc % 2]
                b.waits("vector", [mm_evs[dpc], prev_sl])
                ev_e = b.do("vector", lambda v, ps=ps, dpc=dpc:
                            v.tensor_tensor(mtmp[dpc % 2][:], ps[:],
                                            c2bc[:], ALU.mult), mark="dv")
                ev_evM[dpc % 2] = ev_e
                b.waits("scalar", [ev_e, prev_tg])
                ev_sl = b.do("scalar", lambda s, dpc=dpc: s.activation(
                    gb_[dpc % 2][:], mtmp[dpc % 2][:], AF.Silu,
                    bias=b2cols[:, dpc:dpc + 1]), mark="ac")
                b.waits("tensor", [ev_sl, ev_gev])
                ev_tg = b.do("tensor", lambda t, dpc=dpc: (
                    [t.transpose(pTg[dpc % 2][:, j * 128:(j + 1) * 128],
                                 gb_[dpc % 2][:, j * 128:(j + 1) * 128],
                                 identb[:])
                     for j in range(4)])[-1], mark="pe")
                b.wait("vector", ev_tg)

                def evg(v, dpc=dpc):
                    dst = gq[:].rearrange("p (tl d) -> p tl d", tl=4)[
                        :, :, dpc * 128:(dpc + 1) * 128]
                    return v.tensor_copy(
                        dst, pTg[dpc % 2][:].rearrange("p (j t) -> p j t",
                                                       j=4))
                ev_g2 = b.do("vector", evg, mark="dv")
                return ev_sl, ev_tg, ev_g2

            for dpc in range(ND):
                wb2_ = w2b[dpc % 2]
                b.wait("sync", w_free[dpc % 2])
                ev_w = b.dma("sync",
                             wb2_[:].rearrange("p (oc dp) -> p oc dp",
                                               oc=NOC),
                             w2t_sc[dpc].rearrange("oc o dp -> o oc dp"))
                ps = pM[dpc % 2]
                b.waits("tensor", [ev_x2, ev_w, ev_evM[dpc % 2]])

                def mm2(t, ps=ps, wb2_=wb2_):
                    ins = None
                    for oc in range(NOC):
                        ins = t.matmul(ps[:],
                                       wb2_[:, oc * 128:(oc + 1) * 128],
                                       x2q[:, oc * 512:(oc + 1) * 512],
                                       start=(oc == 0), stop=(oc == NOC - 1))
                    return ins
                ev_mm = b.do("tensor", mm2, mark="pe")
                mm_evs[dpc] = ev_mm
                w_free[dpc % 2] = ev_mm
                if dpc == ND - 1:
                    x2_free = ev_mm
                if dpc >= 1:
                    prev_sl, prev_tg, ev_gev = emit_post(dpc - 1)
            prev_sl, prev_tg, ev_gev = emit_post(ND - 1)
            comb_free = None
            for tl in range(4):
                qt = quar * 4 + tl
                b.wait("sync", comb_free)
                ev_xr = b.dma("sync", xcb[:],
                              x_h[qt * 128:(qt + 1) * 128, :])
                ev_sr = b.dma("sync", scb2[:], s_sc[qt])
                b.waits("vector", [ev_xr, ev_sr, ev_gev])

                def comb(v, tl=tl):
                    v.drain()
                    v.tensor_mul(sg32[:], scb2[:],
                                 gq[:, tl * D:(tl + 1) * D])
                    v.drain()
                    return v.tensor_add(ocb[:], xcb[:], sg32[:])
                ev_c = b.do("vector", comb, mark="dv")
                comb_free = ev_c
                b.wait("gpsimd", ev_c)
                ev_y = b.dma("gpsimd", y_h[qt * 128:(qt + 1) * 128, :],
                             ocb[:])
                b.wait("vector", ev_y)
        pm.close()
        pm_ps.close()

    def fin(g):
        for s_ in ("dg0", "dg1", "dg2", "dg3"):
            g.wait_ge(b.semh[s_], b.cnt.get(s_, 0))
    b.emit("gpsimd", fin)

    phase_ranges = []

    def run_ops(eng, key):
        cur = None
        start = nc.n_instructions()
        for ph, f in b.ops[key]:
            if ph != cur:
                if cur is not None:
                    phase_ranges.append((key, cur, start,
                                         nc.n_instructions()))
                cur = ph
                start = nc.n_instructions()
            f(eng)
        if cur is not None:
            phase_ranges.append((key, cur, start, nc.n_instructions()))

    with nc.Block() as block:
        @block.sync
        def _(eng):
            run_ops(eng, "sync")

        @block.tensor
        def _(eng):
            run_ops(eng, "tensor")

        @block.vector
        def _(eng):
            run_ops(eng, "vector")

        @block.scalar
        def _(eng):
            run_ops(eng, "scalar")

        @block.gpsimd
        def _(eng):
            run_ops(eng, "gpsimd")

    ctx.close()
    nc._phase_ranges = phase_ranges
    return nc


def _rope_tables():
    e_ = np.arange(0, 128, 2, dtype=np.float32) / np.float32(128.0)
    p_ = np.power(np.float32(10000.0), e_).astype(np.float32)
    invf = (np.float32(1.0) / p_).astype(np.float32)
    pos = np.arange(T, dtype=np.float32)
    freqs = (pos[:, None] * invf[None, :]).astype(np.float32)
    emb = np.concatenate([freqs, freqs], axis=1)
    cos = np.cos(emb).astype(np.float32).T.copy()
    sin = np.sin(emb).astype(np.float32).T.copy()
    ss = sin.copy()
    ss[0:64] = -sin[0:64]
    return np.ascontiguousarray(cos), np.ascontiguousarray(ss)


def kernel(L0_attn_u, L0_attn_W_in, L0_attn_W_out, L0_ffn_w1, L0_ffn_b1,
           L0_ffn_ln_g, L0_ffn_ln_b, L0_ffn_w2, L0_ffn_b2):
    nb = L0_attn_u.shape[0]
    wkvq = np.ascontiguousarray(np.asarray(L0_attn_W_in).T[:, 4096:10240],
                                dtype=np.float32)
    woutT = np.ascontiguousarray(np.asarray(L0_attn_W_out).T,
                                 dtype=np.float32)
    w1T = np.ascontiguousarray(np.asarray(L0_ffn_w1).T, dtype=np.float32)
    w2T = np.ascontiguousarray(np.asarray(L0_ffn_w2).T, dtype=np.float32)
    b1 = np.asarray(L0_ffn_b1, dtype=np.float32).reshape(1, DI)
    lng = np.asarray(L0_ffn_ln_g, dtype=np.float32).reshape(1, DI)
    lnb = np.asarray(L0_ffn_ln_b, dtype=np.float32).reshape(1, DI)
    b2 = np.asarray(L0_ffn_b2, dtype=np.float32).reshape(1, D)
    cosT, sinT = _rope_tables()
    base = np.where(np.arange(128)[:, None] > np.arange(128)[None, :],
                    np.float32(-1e30), np.float32(0.0)).astype(np.float32)
    trimask = np.ascontiguousarray(np.tile(base, (1, 8)))
    ident = np.eye(128, dtype=np.float32)

    nc = build_nc(debug=DEBUG, upto=UPTO)
    in_maps = []
    for i in range(nb):
        in_maps.append(dict(
            x=np.ascontiguousarray(np.asarray(L0_attn_u[i],
                                              dtype=np.float32)),
            wkvq=wkvq, woutT=woutT, w1T=w1T, w2T=w2T, b1=b1, lng=lng,
            lnb=lnb, b2=b2, cosT=cosT, sinT=sinT, trimask=trimask,
            ident=ident))
    res = run_bass_kernel_spmd(nc, in_maps, core_ids=list(range(nb)))
    out = np.stack([np.asarray(r["y"], dtype=np.float32)
                    for r in res.results], axis=0)
    if DEBUG:
        kernel.last_results = res
    return out


# version 09
# speedup vs baseline: 1.3327x; 1.0203x over previous
import numpy as np
import ml_dtypes
from contextlib import ExitStack
import concourse.bass as bass
import concourse.mybir as mybir
from concourse.bass_utils import run_bass_kernel_spmd

F32 = mybir.dt.float32
BF16 = mybir.dt.bfloat16
AF = mybir.ActivationFunctionType
ALU = mybir.AluOpType
AX = mybir.AxisListType

MAGIC = 12582912.0
SC_ATT = 0.08838834764831845
NEG_HALF = -0.5
EPS = 1e-5
T, D, DI = 2048, 2048, 8192
NT, ND = 16, 16
NKV = 8
NOB, NOC = 16, 64

DEBUG = False
UPTO = 99
ENGS = ("sync", "tensor", "vector", "scalar", "gpsimd")


class B:
    def __init__(self):
        self.ops = {k: [] for k in ENGS}
        self.cnt = {}
        self.semh = {}
        self.lastw = {e: {} for e in ENGS}
        self.phase = "init"

    def emit(self, eng, fn):
        self.ops[eng].append((self.phase, fn))

    def wait(self, eng, evt):
        if evt is None:
            return
        sem, val, peng = evt
        if peng == eng or val <= 0:
            return
        if self.lastw[eng].get(sem, 0) >= val:
            return
        self.lastw[eng][sem] = val

        def f(e, sem=sem, val=val):
            e.wait_ge(self.semh[sem], val)
        self.emit(eng, f)

    def waits(self, eng, evts):
        for ev in evts:
            self.wait(eng, ev)

    def bump(self, sem, k):
        self.cnt[sem] = self.cnt.get(sem, 0) + k
        return self.cnt[sem]

    DMA_POOL = {"sync": ["dl0", "dl1", "dl2", "dl3"],
                "scalar": ["ds0", "ds1"],
                "gpsimd": ["dg0", "dg1", "dg2", "dg3"]}

    def dma(self, eng, dst, src):
        pool = self.DMA_POOL[eng]
        k = self.cnt.get("_rr_" + eng, 0)
        self.cnt["_rr_" + eng] = k + 1
        sem = pool[k % len(pool)]
        before = self.cnt.get(sem, 0)
        self.wait(eng, (sem, before, None))
        val = self.bump(sem, 16)

        def f(e, dst=dst, src=src, sem=sem):
            e.dma_start(out=dst, in_=src).then_inc(self.semh[sem], 16)
        self.emit(eng, f)
        return (sem, val, None)

    def do(self, eng, fn, mark=None):
        if mark is None:
            self.emit(eng, lambda e, fn=fn: fn(e))
            return None
        val = self.bump(mark, 1)

        def f(e, fn=fn, mark=mark):
            fn(e).then_inc(self.semh[mark], 1)
        self.emit(eng, f)
        return (mark, val, eng)

    def barrier(self):
        for eng, pool in self.DMA_POOL.items():
            for sem in pool:
                self.wait(eng, (sem, self.cnt.get(sem, 0), None))
        tgt = self.bump("bar", len(ENGS))
        for eng in ENGS:
            def f(e):
                try:
                    e.drain()
                except Exception:
                    pass
                e.nop().then_inc(self.semh["bar"], 1)
            self.emit(eng, f)
        for eng in ENGS:
            def f(e, tgt=tgt):
                e.wait_ge(self.semh["bar"], tgt)
            self.emit(eng, f)
            self.lastw[eng]["bar"] = tgt


def build_nc(debug=False, upto=99):
    nc = bass.Bass("TRN2", target_bir_lowering=False, debug=False)
    b = B()

    def inp(name, shape, dtype=F32):
        return nc.dram_tensor(name, shape, dtype, kind="ExternalInput").ap()

    x_h = inp("x", [T, D])
    wkvq_h = inp("wkvq", [D, 6144])
    wout_h = inp("woutT", [4096, D])
    w1_h = inp("w1T", [D, DI])
    w2_h = inp("w2T", [DI, D])
    b1_h = inp("b1", [1, DI])
    g2_h = inp("lng", [1, DI])
    bb2_h = inp("lnb", [1, DI])
    b2_h = inp("b2", [1, D])
    cosb_h = inp("cosTb", [128, T], BF16)
    sinb_h = inp("sinTb", [128, T], BF16)
    tri_h = inp("trimask", [128, 1024], BF16)
    id_h = inp("ident", [128, 128])
    y_h = nc.dram_tensor("y", [T, D], F32, kind="ExternalOutput").ap()

    skind = "ExternalOutput" if debug else "Internal"

    def scr(name, shape, dtype):
        return nc.dram_tensor(name, shape, dtype, kind=skind).ap()

    xq1_sc = scr("xq1_sc", [NT, ND, 128, 128], BF16)
    q_sc = scr("q_sc", [NKV, 128, NT, 4, 128], BF16)
    at_sc = scr("at_sc", [NKV, 128, NT, 512], BF16)
    s_sc = scr("s_sc", [NT, 128, D], BF16)
    h_sc = scr("h_sc", [NT, 128, DI], F32)
    xq2_sc = scr("xq2_sc", [NOC, 128, D], BF16)
    w1t_sc = scr("w1t_sc", [NOB, ND, 128, 512], BF16)
    wfold_sc = nc.dram_tensor("wfold_sc", [2, 32], F32, kind="Internal").ap()
    wrep_sc = nc.dram_tensor("wrep_sc", [2, 128, 3], F32,
                             kind="Internal").ap()
    w2t_sc = scr("w2t_sc", [ND, NOC, 128, 128], BF16)
    dbg = {}
    if debug:
        for nm, sh, dt in (("d_xnfm", [128, ND * T], BF16),
                           ("d_kfm", [128, NKV * T], BF16),
                           ("d_vkm", [128, NKV * T], BF16),
                           ("d_stats", [128, 64], F32),
                           ("d_wsc", [128, 8], F32)):
            dbg[nm] = nc.dram_tensor(nm, sh, dt, kind="ExternalOutput").ap()

    ctx = ExitStack()
    e = ctx.enter_context
    for s in ("dl0", "dl1", "dl2", "dl3", "ds0", "ds1", "dg0", "dg1",
              "dg2", "dg3", "pe", "dv", "ac", "gp", "bar"):
        b.semh[s] = e(nc.semaphore(s))

    cpool = e(nc.sbuf_tensor("cpool", [128, 4], F32))
    nc.const_aps.register(F32, SC_ATT, cpool[:, 0:1])
    nc.const_aps.register(F32, NEG_HALF, cpool[:, 1:2])
    nc.const_aps.register(F32, -1.0, cpool[:, 2:3])
    onec = e(nc.sbuf_tensor("onec", [1, 128], F32))
    onecol = e(nc.sbuf_tensor("onecol", [128, 1], F32))
    one11 = e(nc.sbuf_tensor("one11", [1, 1], F32))
    ones_bf = e(nc.sbuf_tensor("ones_bf", [128, 128], BF16))
    identb = e(nc.sbuf_tensor("identb", [128, 128], BF16))
    identf = e(nc.sbuf_tensor("identf", [128, 128], F32))
    tri2b = e(nc.sbuf_tensor("tri2b", [128, 1024], BF16))
    m1c = e(nc.sbuf_tensor("m1c", [128, NT], F32))
    c1c = e(nc.sbuf_tensor("c1c", [128, NT], F32))
    c2c = e(nc.sbuf_tensor("c2c", [128, NT], F32))
    wsc = e(nc.sbuf_tensor("wsc", [128, 8], F32))
    b2cols = e(nc.sbuf_tensor("b2cols", [128, ND], F32))
    srow = e(nc.sbuf_tensor("srow_w", [1, 96], F32))
    wacc = e(nc.sbuf_tensor("wacc", [128, 8], F32))

    b.dma("sync", tri2b[:], tri_h[:, :])
    ev_idf = b.dma("sync", identf[:], id_h[:, :])

    def init_consts(v):
        v.memset(onec[:], 1.0)
        v.memset(onecol[:], 1.0)
        v.memset(one11[:], 1.0)
        v.memset(ones_bf[:], 1.0)
        v.memset(cpool[:, 0:1], SC_ATT)
        v.memset(cpool[:, 2:3], -1.0)
        return v.memset(cpool[:, 1:2], NEG_HALF)
    ev_cst = b.do("vector", init_consts, mark="dv")
    b.wait("vector", ev_idf)
    ev_idb = b.do("vector", lambda v: v.tensor_copy(identb[:], identf[:]),
                  mark="dv")

    xn_ctx = ExitStack()
    xn_fm = xn_ctx.enter_context(nc.sbuf_tensor("xn_fm", [128, ND * T], BF16))
    rope_ctx = ExitStack()
    er = rope_ctx.enter_context
    cosTb = er(nc.sbuf_tensor("cosTb_sb", [128, T], BF16))
    ssTb = er(nc.sbuf_tensor("ssTb_sb", [128, T], BF16))
    b.dma("sync", cosTb[:], cosb_h[:, :])
    b.dma("sync", ssTb[:], sinb_h[:, :])

    b.phase = "A"
    if upto >= 1:
        pa = ExitStack()
        ea = pa.enter_context
        xld = [ea(nc.sbuf_tensor(f"xld{j}", [128, D], F32)) for j in range(2)]
        xn32 = ea(nc.sbuf_tensor("xn32", [128, D], F32))
        tq32 = ea(nc.sbuf_tensor("tq32", [128, D], F32))
        xnbf = ea(nc.sbuf_tensor("xnbf", [128, D], BF16))
        xqbf = ea(nc.sbuf_tensor("xqbf", [128, D], BF16))
        xqst = ea(nc.sbuf_tensor("xqst", [128, D], BF16))
        trashA = ea(nc.sbuf_tensor("trashA", [128, D], BF16))
        stc = ea(nc.sbuf_tensor("stc", [128, 20], F32))
        pA_ps = ExitStack()
        pT = pA_ps.enter_context(nc.psum_tensor("pTA", [128, 512], BF16))
        pT2 = pA_ps.enter_context(nc.psum_tensor("pTA2", [128, 512], BF16))

        (SU, SQ, MU, EX2, MU2, VAR, VEPS, LNV, INV, I2, MX2, M2E, LNM, RST,
         MX, QS, RM) = range(17)

        xld_free = [None, None]
        st_free = None
        for i in range(NT):
            buf = xld[i % 2]
            b.wait("sync", xld_free[i % 2])
            ev_x = b.dma("sync", buf[:], x_h[i * 128:(i + 1) * 128, :])
            b.wait("vector", ev_x)
            ev_ms = b.do("vector", lambda v, bu=buf: (
                v.drain(),
                v.reduce_sum(stc[:, SU:SU + 1], bu[:], AX.X),
                v.memset(stc[:, SQ:SQ + 1], 0.0))[-1], mark="dv")
            b.waits("scalar", [ev_x, ev_ms])
            ev_sq = b.do("scalar", lambda s, bu=buf: s.activation(
                trashA[:], bu[:], AF.Square, accum_out=stc[:, SQ:SQ + 1]),
                mark="ac")
            b.wait("vector", ev_sq)

            def stats1(v):
                v.tensor_scalar_mul(stc[:, MU:MU + 1], stc[:, SU:SU + 1],
                                    1.0 / D)
                v.tensor_scalar_mul(stc[:, EX2:EX2 + 1], stc[:, SQ:SQ + 1],
                                    1.0 / D)
                v.drain()
                v.tensor_mul(stc[:, MU2:MU2 + 1], stc[:, MU:MU + 1],
                             stc[:, MU:MU + 1])
                v.drain()
                v.tensor_sub(stc[:, VAR:VAR + 1], stc[:, EX2:EX2 + 1],
                             stc[:, MU2:MU2 + 1])
                v.drain()
                return v.tensor_scalar_add(stc[:, VEPS:VEPS + 1],
                                           stc[:, VAR:VAR + 1], EPS)
            ev_v1 = b.do("vector", stats1, mark="dv")
            b.wait("scalar", ev_v1)

            def act_inv(s):
                s.activation(stc[:, LNV:LNV + 1], stc[:, VEPS:VEPS + 1], AF.Ln)
                s.drain()
                return s.activation(stc[:, INV:INV + 1], stc[:, LNV:LNV + 1],
                                    AF.Exp, scale=NEG_HALF)
            ev_inv = b.do("scalar", act_inv, mark="ac")
            b.wait("vector", ev_inv)

            def xnf(v, bu=buf):
                v.drain()
                v.tensor_scalar(xn32[:], bu[:], stc[:, MU:MU + 1],
                                stc[:, INV:INV + 1], ALU.subtract, ALU.mult)
                v.tensor_mul(stc[:, I2:I2 + 1], stc[:, INV:INV + 1],
                             stc[:, INV:INV + 1])
                v.drain()
                v.tensor_copy(xnbf[:], xn32[:])
                v.tensor_mul(stc[:, MX2:MX2 + 1], stc[:, VAR:VAR + 1],
                             stc[:, I2:I2 + 1])
                v.reduce_max(stc[:, MX:MX + 1], xn32[:], AX.X,
                             apply_absolute_value=True)
                v.drain()
                return v.tensor_scalar_add(stc[:, M2E:M2E + 1],
                                           stc[:, MX2:MX2 + 1], EPS)
            ev_v2 = b.do("vector", xnf, mark="dv")
            xld_free[i % 2] = ev_v2
            b.wait("scalar", ev_v2)

            def act_rst(s):
                s.activation(stc[:, LNM:LNM + 1], stc[:, M2E:M2E + 1], AF.Ln)
                s.drain()
                return s.activation(stc[:, RST:RST + 1], stc[:, LNM:LNM + 1],
                                    AF.Exp, scale=NEG_HALF)
            ev_rst = b.do("scalar", act_rst, mark="ac")
            b.wait("vector", ev_rst)

            def quant(v, i=i):
                v.tensor_mul(m1c[:, i:i + 1], stc[:, RST:RST + 1],
                             stc[:, MX:MX + 1])
                v.drain()
                v.tensor_scalar_max(m1c[:, i:i + 1], m1c[:, i:i + 1], EPS)
                v.drain()
                v.reciprocal(stc[:, RM:RM + 1], m1c[:, i:i + 1])
                v.drain()
                v.tensor_mul(stc[:, QS:QS + 1], stc[:, RST:RST + 1],
                             stc[:, RM:RM + 1])
                v.drain()
                v.tensor_scalar_mul(stc[:, QS:QS + 1], stc[:, QS:QS + 1],
                                    127.0)
                v.drain()
                v.tensor_scalar_mul(tq32[:], xn32[:], stc[:, QS:QS + 1])
                v.drain()
                return v.tensor_scalar(xqbf[:], tq32[:], MAGIC, MAGIC,
                                       ALU.add, ALU.subtract)
            ev_q = b.do("vector", quant, mark="dv")

            b.waits("tensor", [ev_v2, ev_idb])
            for gdc in range(4):
                def tr4(t, gdc=gdc):
                    ins = None
                    for j in range(4):
                        dc = gdc * 4 + j
                        ins = t.transpose(pT[:, j * 128:(j + 1) * 128],
                                          xnbf[:, dc * 128:(dc + 1) * 128],
                                          identb[:])
                    return ins
                ev_t = b.do("tensor", tr4, mark="pe")
                b.wait("vector", ev_t)

                def evac(v, gdc=gdc, i=i):
                    dst = xn_fm[:].rearrange("p (dc t) -> p dc t", dc=ND)[
                        :, gdc * 4:(gdc + 1) * 4, i * 128:(i + 1) * 128]
                    return v.tensor_copy(
                        dst, pT[:].rearrange("p (j t) -> p j t", j=4))
                ev_ev = b.do("vector", evac, mark="dv")
                b.wait("tensor", ev_ev)
            b.waits("tensor", [ev_q, st_free])
            last_evq = None
            for gdc in range(4):
                def trq(t, gdc=gdc):
                    ins = None
                    for j in range(4):
                        dc = gdc * 4 + j
                        ins = t.transpose(pT2[:, j * 128:(j + 1) * 128],
                                          xqbf[:, dc * 128:(dc + 1) * 128],
                                          identb[:])
                    return ins
                ev_t = b.do("tensor", trq, mark="pe")
                b.wait("vector", ev_t)

                def evacq(v, gdc=gdc):
                    dst = xqst[:].rearrange("p (dc t) -> p dc t", dc=ND)[
                        :, gdc * 4:(gdc + 1) * 4, :]
                    return v.tensor_copy(
                        dst, pT2[:].rearrange("p (j t) -> p j t", j=4))
                last_evq = b.do("vector", evacq, mark="dv")
                b.wait("tensor", last_evq)
            b.wait("gpsimd", last_evq)
            st_free = b.dma("gpsimd",
                            xq1_sc[i].rearrange("dc d t -> d dc t"),
                            xqst[:].rearrange("p (dc t) -> p dc t", dc=ND))
        if debug:
            b.wait("gpsimd", ("dv", b.cnt.get("dv", 0), None))
            b.dma("gpsimd", dbg["d_xnfm"][:, :], xn_fm[:])
            b.dma("gpsimd", dbg["d_stats"][:, 0:NT], m1c[:])
        b.barrier()
        pa.close()
        pA_ps.close()

    def make_w_thunks(wi, w_h, nch, F, nelem, wch, ternb):
        st = {"ch_free": [None, None], "tern_free": [None, None],
              "z": None, "wsc_done": None}
        thunks = []
        SB = 48 * wi

        def t_zero():
            st["z"] = b.do("vector", lambda v, wi=wi: v.memset(
                wacc[:, wi:wi + 1], 0.0), mark="dv")
        thunks.append(t_zero)

        st["loads"] = {}

        def t_load(c):
            buf = wch[c % 2]
            b.wait("sync", st["ch_free"][c % 2])
            st["loads"][c] = b.dma("sync", buf[:, 0:F],
                                   w_h[c * 128:(c + 1) * 128, :])

        def t_stat(c):
            buf = wch[c % 2]
            b.waits("vector", [st["loads"][c], st["z"]])

            def statw(v, buf=buf):
                v.drain()
                v.tensor_reduce(wacc[:, 2 + wi:3 + wi], buf[:, 0:F], AX.X,
                                ALU.add, apply_absolute_value=True)
                v.drain()
                return v.tensor_add(wacc[:, wi:wi + 1], wacc[:, wi:wi + 1],
                                    wacc[:, 2 + wi:3 + wi])
            st["ch_free"][c % 2] = b.do("vector", statw, mark="dv")
        for c in range(nch):
            if c == 0:
                thunks.append(lambda: t_load(0))
            thunks.append(lambda c=c: (t_load(c + 1) if c + 1 < nch
                                       else None, t_stat(c)))

        def t_scales():
            col = 4 + 2 * wi
            b.wait("vector", st["ch_free"][(nch - 1) % 2])
            prev = b.do("vector", lambda v: (
                v.drain(),
                v.tensor_copy(wacc[:, col:col + 1],
                              wacc[:, wi:wi + 1]))[-1], mark="dv")
            n = 128
            while n > 32:
                h = n // 2
                b.wait("gpsimd", prev)
                ev_d = b.dma("gpsimd", wacc[0:h, col + 1:col + 2],
                             wacc[h:n, col:col + 1])
                b.waits("vector", [ev_d])

                def fold(v, h=h, col=col):
                    v.drain()
                    return v.tensor_tensor(
                        wacc[0:h, col:col + 1], wacc[0:h, col:col + 1],
                        wacc[0:h, col + 1:col + 2], ALU.add)
                prev = b.do("vector", fold, mark="dv")
                n = h
            b.wait("gpsimd", prev)
            ev_g1 = b.dma("gpsimd", wfold_sc[wi, :],
                          wacc[0:32, col:col + 1])
            b.wait("gpsimd", ev_g1)
            ev_g = b.dma("gpsimd", srow[0:1, SB:SB + 32], wfold_sc[wi, :])
            b.wait("vector", ev_g)

            def scals(v, nelem=nelem):
                v.drain()
                v.reduce_sum(srow[0:1, SB + 32:SB + 33],
                             srow[0:1, SB:SB + 32], AX.X)
                v.drain()
                v.tensor_scalar_mul(srow[0:1, SB + 33:SB + 34],
                                    srow[0:1, SB + 32:SB + 33], 1.0 / nelem)
                v.drain()
                v.tensor_scalar_max(srow[0:1, SB + 36:SB + 37],
                                    srow[0:1, SB + 33:SB + 34], EPS)
                v.drain()
                v.tensor_scalar_mul(srow[0:1, SB + 34:SB + 35],
                                    srow[0:1, SB + 36:SB + 37], 0.5)
                v.drain()
                return v.tensor_scalar_mul(
                    srow[0:1, SB + 35:SB + 36],
                    srow[0:1, SB + 36:SB + 37], -0.5)
            prev = b.do("vector", scals, mark="dv")
            b.wait("gpsimd", prev)
            prev = b.dma("gpsimd", wrep_sc[wi, 0:1, :],
                         srow[0:1, SB + 34:SB + 37])
            n = 1
            while n < 128:
                b.wait("gpsimd", prev)
                prev = b.dma("gpsimd", wrep_sc[wi, n:2 * n, :],
                             wrep_sc[wi, 0:n, :])
                n *= 2
            b.wait("gpsimd", prev)
            prev = b.dma("gpsimd", wsc[:, 3 * wi:3 * wi + 3],
                         wrep_sc[wi, :, :])
            st["wsc_done"] = prev
            if wi == 0:
                b.waits("vector", [prev])
                b.do("vector", lambda v: (
                    v.drain(),
                    v.tensor_scalar(c1c[:], m1c[:], wsc[:, 2:3],
                                    1.0 / 127.0, ALU.mult,
                                    ALU.mult))[-1], mark="dv")
        thunks.append(t_scales)

        def t_tern(c):
            buf = wch[c % 2]
            tb = ternb[c % 2]
            b.waits("vector", [st["loads"][c], st["wsc_done"],
                               st["tern_free"][c % 2]])

            def ternf(v, buf=buf, tb=tb, wi=wi):
                v.drain()
                v.tensor_single_scalar(tb[:, 0:F], buf[:, 0:F],
                                       wsc[:, 3 * wi + 1:3 * wi + 2],
                                       ALU.is_lt)
                v.drain()
                return v.scalar_tensor_tensor(
                    tb[:, 0:F], buf[:, 0:F],
                    wsc[:, 3 * wi:3 * wi + 1], tb[:, 0:F],
                    ALU.is_gt, ALU.subtract)
            ev_t = b.do("vector", ternf, mark="dv")
            st["ch_free"][c % 2] = ev_t
            b.wait("gpsimd", ev_t)
            if wi == 0:
                dst = w1t_sc[:, c, :, :].rearrange("ob d o -> d ob o")
                src = tb[:, 0:F].rearrange("d (ob o) -> d ob o", ob=NOB)
            else:
                dst = w2t_sc[:, c, :, :].rearrange("dpc o dp -> o dpc dp")
                src = tb[:, 0:F].rearrange("o (dpc dp) -> o dpc dp",
                                           dpc=ND)
            st["tern_free"][c % 2] = b.dma("gpsimd", dst, src)
        for c in range(nch):
            if c == 0:
                thunks.append(lambda: t_load(0))
            thunks.append(lambda c=c: (t_load(c + 1) if c + 1 < nch
                                       else None, t_tern(c)))
        return thunks

    kv_ctx = ExitStack()
    k_fm = kv_ctx.enter_context(
        nc.sbuf_tensor("k_fm", [128, NKV * T], BF16, side="right"))
    v_km = kv_ctx.enter_context(
        nc.sbuf_tensor("v_km", [128, NKV * T], BF16, side="right"))
    b.phase = "P1kv"
    if upto >= 3:
        p1 = ExitStack()
        e1 = p1.enter_context
        wres = e1(nc.sbuf_tensor("wres1", [128, 16 * 1024], BF16))
        wstg = [e1(nc.sbuf_tensor(f"wstg{j}", [128, 1024], F32))
                for j in range(2)]
        a16 = e1(nc.sbuf_tensor("a16", [128, 1024], BF16))
        b16 = e1(nc.sbuf_tensor("b16", [128, 1024], BF16))
        krawb = e1(nc.sbuf_tensor("krawb", [128, 1024], BF16))
        krotb = e1(nc.sbuf_tensor("krotb", [128, 1024], BF16))
        vstg = e1(nc.sbuf_tensor("vstg", [128, 1024], BF16))
        p1_ps = ExitStack()
        pA = p1_ps.enter_context(nc.psum_tensor("pA1", [128, 1024], F32))
        pB = p1_ps.enter_context(nc.psum_tensor("pB1", [128, 1024], F32))
        pTv = p1_ps.enter_context(nc.psum_tensor("pTv", [128, 512], BF16))

        last_pe_half = None
        stg_free = [None, None]
        prev_evac = [None, None]
        prev_kr = None
        for half in range(2):
            cast_evs = []
            for dc in range(ND):
                sb = wstg[dc % 2]
                b.wait("sync", stg_free[dc % 2])
                ev_w = b.dma("sync", sb[:],
                             wkvq_h[dc * 128:(dc + 1) * 128,
                                    half * 1024:(half + 1) * 1024])
                b.waits("scalar", [ev_w, last_pe_half])
                ev_c = b.do("scalar", lambda s, sb=sb, dc=dc: s.copy(
                    wres[:, dc * 1024:(dc + 1) * 1024], sb[:]), mark="ac")
                stg_free[dc % 2] = ev_c
                cast_evs.append(ev_c)
            b.waits("tensor", cast_evs)
            unit = 0
            for oc in range(8):
                for tb in range(2):
                    ps = pA if unit % 2 == 0 else pB
                    b.wait("tensor", prev_evac[unit % 2])

                    def mmkv(t, oc=oc, tb=tb, ps=ps):
                        ins = None
                        for dc in range(ND):
                            for nb in range(2):
                                ins = t.matmul(
                                    ps[:, nb * 512:(nb + 1) * 512],
                                    wres[:, dc * 1024 + oc * 128:
                                         dc * 1024 + (oc + 1) * 128],
                                    xn_fm[:, dc * T + tb * 1024 + nb * 512:
                                          dc * T + tb * 1024 +
                                          (nb + 1) * 512],
                                    start=(dc == 0), stop=(dc == ND - 1))
                        return ins
                    ev_mm = b.do("tensor", mmkv, mark="pe")
                    b.wait("vector", ev_mm)
                    if half == 0:
                        ev_cp = b.do("vector", lambda v, ps=ps:
                                     v.tensor_copy(vstg[:], ps[:]),
                                     mark="dv")
                        prev_evac[unit % 2] = ev_cp
                        b.wait("tensor", ev_cp)
                        for grp in range(2):
                            def trv(t, grp=grp):
                                ins = None
                                for j in range(4):
                                    ins = t.transpose(
                                        pTv[:, j * 128:(j + 1) * 128],
                                        vstg[:, (grp * 4 + j) * 128:
                                             (grp * 4 + j + 1) * 128],
                                        identb[:])
                                return ins
                            ev_t = b.do("tensor", trv, mark="pe")
                            b.wait("vector", ev_t)

                            def evv(v, oc=oc, tb=tb, grp=grp):
                                kt0 = oc * 16 + tb * 8 + grp * 4
                                return v.tensor_copy(
                                    v_km[:, kt0 * 128:(kt0 + 4) * 128],
                                    pTv[:])
                            ev_e = b.do("vector", evv, mark="dv")
                            b.wait("tensor", ev_e)
                    else:
                        b.waits("scalar", [ev_mm, prev_kr])

                        def evk(sc, ps=ps):
                            sc.copy(krawb[:], ps[:])
                            sc.copy(krotb[0:64, :], ps[64:128, :])
                            return sc.copy(krotb[64:128, :], ps[0:64, :])
                        ev_kc = b.do("scalar", evk, mark="ac")
                        prev_evac[unit % 2] = ev_kc
                        b.wait("vector", ev_kc)

                        def ropek(v, oc=oc, tb=tb):
                            sl = slice(tb * 1024, (tb + 1) * 1024)
                            v.drain()
                            v.tensor_mul(a16[:], krawb[:], cosTb[:, sl])
                            v.tensor_mul(b16[:], krotb[:], ssTb[:, sl])
                            v.drain()
                            return v.tensor_add(
                                k_fm[:, oc * T + tb * 1024:
                                     oc * T + (tb + 1) * 1024],
                                a16[:], b16[:])
                        ev_r = b.do("vector", ropek, mark="dv")
                        prev_kr = ev_r
                    unit += 1
            last_pe_half = ("pe", b.cnt.get("pe", 0), "tensor")
        if debug:
            b.wait("gpsimd", ("dv", b.cnt.get("dv", 0), None))
            b.dma("gpsimd", dbg["d_kfm"][:, :], k_fm[:])
            b.dma("gpsimd", dbg["d_vkm"][:, :], v_km[:])
        b.barrier()
        p1.close()
        p1_ps.close()

    b.phase = "P2q"
    if upto >= 4:
        p2 = ExitStack()
        e2 = p2.enter_context
        wres2 = e2(nc.sbuf_tensor("wres2", [128, 16 * 1024], BF16))
        wstg2 = e2(nc.sbuf_tensor("wstg2", [128, 1024], F32))
        a16q = e2(nc.sbuf_tensor("a16q", [128, 1024], BF16))
        b16q = e2(nc.sbuf_tensor("b16q", [128, 1024], BF16))
        qrawb = e2(nc.sbuf_tensor("qrawb", [128, 1024], BF16))
        qrotb = e2(nc.sbuf_tensor("qrotb", [128, 1024], BF16))
        qstg = [e2(nc.sbuf_tensor(f"qstg{j}", [128, 1024], BF16))
                for j in range(2)]
        p2_ps = ExitStack()
        pA2 = p2_ps.enter_context(nc.psum_tensor("pA2", [128, 1024], F32))
        pB2 = p2_ps.enter_context(nc.psum_tensor("pB2", [128, 1024], F32))

        last_pe_qr = None
        stg_free = None
        prev_rope = [None, None]
        prev_evq = [None, None]
        prev_pq = None
        prev_r_any = None
        st_free = [None, None]
        for qr in range(4):
            cast_evs = []
            for dc in range(ND):
                b.wait("sync", stg_free)
                ev_w = b.dma("sync", wstg2[:],
                             wkvq_h[dc * 128:(dc + 1) * 128,
                                    2048 + qr * 1024:2048 + (qr + 1) * 1024])
                b.waits("scalar", [ev_w, last_pe_qr])
                ev_c = b.do("scalar", lambda s, dc=dc: s.copy(
                    wres2[:, dc * 1024:(dc + 1) * 1024], wstg2[:]),
                    mark="ac")
                stg_free = ev_c
                cast_evs.append(ev_c)
            b.waits("tensor", cast_evs)
            unit = 0
            for oc in range(8):
                h = qr * 8 + oc
                for tb in range(2):
                    ps = pA2 if unit % 2 == 0 else pB2
                    b.wait("tensor", prev_rope[unit % 2])

                    def mmq(t, oc=oc, tb=tb, ps=ps):
                        ins = None
                        for dc in range(ND):
                            for nb in range(2):
                                ins = t.matmul(
                                    ps[:, nb * 512:(nb + 1) * 512],
                                    wres2[:, dc * 1024 + oc * 128:
                                          dc * 1024 + (oc + 1) * 128],
                                    xn_fm[:, dc * T + tb * 1024 + nb * 512:
                                          dc * T + tb * 1024 +
                                          (nb + 1) * 512],
                                    start=(dc == 0), stop=(dc == ND - 1))
                        return ins
                    ev_mm = b.do("tensor", mmq, mark="pe")
                    qs_buf = qstg[unit % 2]
                    b.waits("scalar", [ev_mm, prev_r_any])

                    def evq(sc, ps=ps):
                        sc.copy(qrawb[:], ps[:])
                        sc.copy(qrotb[0:64, :], ps[64:128, :])
                        return sc.copy(qrotb[64:128, :], ps[0:64, :])
                    ev_qc = b.do("scalar", evq, mark="ac")
                    prev_rope[unit % 2] = ev_qc
                    b.waits("vector", [ev_qc, st_free[unit % 2]])

                    def ropeq(v, tb=tb, qs_buf=qs_buf):
                        sl = slice(tb * 1024, (tb + 1) * 1024)
                        v.drain()
                        v.tensor_mul(a16q[:], qrawb[:], cosTb[:, sl])
                        v.tensor_mul(b16q[:], qrotb[:], ssTb[:, sl])
                        v.drain()
                        return v.tensor_add(qs_buf[:], a16q[:], b16q[:])
                    ev_r = b.do("vector", ropeq, mark="dv")
                    prev_evq[unit % 2] = ev_r
                    prev_r_any = ev_r
                    b.wait("gpsimd", ev_r)
                    st_free[unit % 2] = b.dma(
                        "gpsimd",
                        q_sc[h // 4, :, tb * 8:(tb + 1) * 8, h % 4, :],
                        qs_buf[:].rearrange("p (qt t) -> p qt t", qt=8))
                    unit += 1
            last_pe_qr = ("pe", b.cnt.get("pe", 0), "tensor")
        b.barrier()
        p2.close()
        p2_ps.close()
    rope_ctx.close()
    xn_ctx.close()

    b.phase = "P3flash"
    if upto >= 5:
        p3 = ExitStack()
        e3 = p3.enter_context
        qbuf = [e3(nc.sbuf_tensor(f"qbuf{j}", [128, 4096], BF16))
                for j in range(2)]
        Pp = [e3(nc.sbuf_tensor(f"Pp{j}", [128, 1024], BF16))
              for j in range(2)]
        zl = e3(nc.sbuf_tensor("zl", [128, 1024], F32))
        zr = e3(nc.sbuf_tensor("zr", [128, 1024], F32))
        atile = [e3(nc.sbuf_tensor(f"atile{j}", [128, 1024], BF16))
                 for j in range(2)]
        p3_ps = ExitStack()
        pS = [p3_ps.enter_context(nc.psum_tensor(f"pS{j}", [128, 1024], F32))
              for j in range(2)]
        pAV = p3_ps.enter_context(nc.psum_tensor("pAV", [128, 1024], F32))
        pZ = p3_ps.enter_context(nc.psum_tensor("pZ", [128, 1024], F32))
        wch1 = [e3(nc.sbuf_tensor(f"wch1_{j}", [128, DI], F32))
                for j in range(2)]
        tern1 = [e3(nc.sbuf_tensor(f"tern1_{j}", [128, DI], BF16))
                 for j in range(2)]
        w1_th = (make_w_thunks(0, w1_h, 16, DI, D * DI, wch1, tern1)
                 if upto >= 2 else [])

        q_free = [None, None]
        at_free = [None, None]
        ev_evac = None
        ev_zr = None
        ai = 0
        for qt in range(NT):
            qb = qbuf[qt % 2]
            b.wait("sync", q_free[qt % 2])
            ev_q = b.dma("sync", qb[:].rearrange("p (kvh f) -> p kvh f",
                                                 kvh=NKV),
                         q_sc[:, :, qt, :, :].rearrange(
                             "kvh hd g t -> hd kvh (g t)"))
            last_smm = None
            exp_evs = {}
            pe_z = {}
            g = 0
            for kvp in range(4):
                kvh0, kvh1 = 2 * kvp, 2 * kvp + 1

                def emit_smm(kt, g, kvh0=kvh0, kvh1=kvh1, qt=qt, qb=qb):
                    par = g % 2
                    b.waits("tensor", [ev_q, exp_evs.get(g - 2), ev_evac])

                    def smm(t, kt=kt, par=par, qt=qt):
                        stp = (kt != qt)
                        t.matmul(pS[par][:, 0:512],
                                 k_fm[:, kvh0 * T + kt * 128:
                                      kvh0 * T + (kt + 1) * 128],
                                 qb[:, kvh0 * 512:(kvh0 + 1) * 512],
                                 start=True, stop=stp)
                        ins = t.matmul(pS[par][:, 512:1024],
                                       k_fm[:, kvh1 * T + kt * 128:
                                            kvh1 * T + (kt + 1) * 128],
                                       qb[:, kvh1 * 512:(kvh1 + 1) * 512],
                                       start=True, stop=stp)
                        if kt == qt:
                            t.matmul(pS[par][:, 0:512], identb[:],
                                     tri2b[:, 0:512], start=False,
                                     stop=True)
                            ins = t.matmul(pS[par][:, 512:1024], identb[:],
                                           tri2b[:, 512:1024], start=False,
                                           stop=True)
                        return ins
                    ev_s = b.do("tensor", smm, mark="pe")
                    src_ev = ev_s
                    b.waits("scalar", [src_ev, pe_z.get(g - 2)])
                    ev_e = b.do("scalar", lambda s, par=par: s.activation(
                        Pp[par][:], pS[par][:], AF.Exp, scale=SC_ATT),
                        mark="ac")
                    exp_evs[g] = ev_e
                    return ev_s

                def emit_av(kt, g, kvh0=kvh0, kvh1=kvh1, qt=qt):
                    par = g % 2
                    b.wait("tensor", exp_evs[g])
                    b.do("tensor", lambda t, kt=kt, par=par:
                         t.matmul(pAV[:, 0:512],
                                  v_km[:, (kvh0 * 16 + kt) * 128:
                                       (kvh0 * 16 + kt + 1) * 128],
                                  Pp[par][:, 0:512], start=(kt == 0),
                                  stop=(kt == qt)))
                    b.do("tensor", lambda t, kt=kt, par=par:
                         t.matmul(pAV[:, 512:1024],
                                  v_km[:, (kvh1 * 16 + kt) * 128:
                                       (kvh1 * 16 + kt + 1) * 128],
                                  Pp[par][:, 512:1024], start=(kt == 0),
                                  stop=(kt == qt)))
                    b.do("tensor", lambda t, kt=kt, par=par:
                         t.matmul(pZ[:, 0:512], ones_bf[:],
                                  Pp[par][:, 0:512], start=(kt == 0),
                                  stop=(kt == qt)))
                    pe_z[g] = b.do(
                        "tensor", lambda t, kt=kt, par=par:
                        t.matmul(pZ[:, 512:1024], ones_bf[:],
                                 Pp[par][:, 512:1024], start=(kt == 0),
                                 stop=(kt == qt)), mark="pe")
                    return pe_z[g]

                for kt in range(qt + 1):
                    last_smm = emit_smm(kt, g + kt)
                    if kt >= 1:
                        emit_av(kt - 1, g + kt - 1)
                ev_zlast = emit_av(qt, g + qt)
                g += qt + 1
                at = atile[ai % 2]
                b.waits("scalar", [ev_zlast])

                def evz(s):
                    s.activation(zl[:], pZ[:], AF.Ln)
                    s.drain()
                    return s.activation(zr[:], zl[:], AF.Exp, scale=-1.0)
                ev_zr = b.do("scalar", evz, mark="ac")
                b.waits("vector", [ev_zr, at_free[ai % 2]])
                ev_evac = b.do("vector", lambda v, at=at: v.tensor_tensor(
                    at[:], pAV[:], zr[:], ALU.mult), mark="dv")
                b.wait("gpsimd", ev_evac)
                at_free[ai % 2] = b.dma(
                    "gpsimd",
                    at_sc[kvh0:kvh0 + 2, :, qt, :].rearrange(
                        "kv hd f -> hd kv f"),
                    at[:].rearrange("p (kv f) -> p kv f", kv=2))
                ai += 1
                if w1_th:
                    w1_th.pop(0)()
            q_free[qt % 2] = last_smm
        while w1_th:
            w1_th.pop(0)()
        if debug:
            b.wait("gpsimd", ("dv", b.cnt.get("dv", 0), None))
            b.dma("gpsimd", dbg["d_wsc"][:, 0:3], wsc[:, 0:3])
        b.barrier()
        p3.close()
        p3_ps.close()
    kv_ctx.close()

    b.phase = "F1"
    if upto >= 6:
        pf = ExitStack()
        ef = pf.enter_context
        xq1b = ef(nc.sbuf_tensor("xq1b", [128, ND * T], BF16))
        w1b = [ef(nc.sbuf_tensor(f"w1b{j}", [128, DI], BF16))
               for j in range(2)]
        b1bc = ef(nc.sbuf_tensor("b1bc", [128, DI], F32))
        b1row_t = ef(nc.sbuf_tensor("b1row_t", [1, DI], F32))
        b1row = b1row_t[0:1, :]
        ev_b1l = b.dma("sync", b1row, b1_h[0:1, :])
        htmp = [ef(nc.sbuf_tensor(f"htmp{j}", [128, 512], F32))
                for j in range(2)]
        hsil = [ef(nc.sbuf_tensor(f"hsil{j}", [128, 512], F32))
                for j in range(2)]
        pf_ps = ExitStack()
        pF = [pf_ps.enter_context(nc.psum_tensor(f"pF{j}", [128, 512], F32))
              for j in range(2)]
        pB5 = pf_ps.enter_context(nc.psum_tensor("pB5", [128, 512], F32))
        wch2 = [ef(nc.sbuf_tensor(f"wch2_{j}", [128, D], F32))
                for j in range(2)]
        tern2 = [ef(nc.sbuf_tensor(f"tern2_{j}", [128, D], BF16))
                 for j in range(2)]
        w2_th = (make_w_thunks(1, w2_h, 64, D, DI * D, wch2, tern2)
                 if upto >= 2 else [])

        ev_xq = []
        for qt in range(NT):
            dst = xq1b[:].rearrange("p (dc t) -> p dc t", dc=ND)[
                :, :, qt * 128:(qt + 1) * 128]
            ev_xq.append(b.dma("sync", dst,
                               xq1_sc[qt].rearrange("dc d t -> d dc t")))
        b.wait("tensor", ev_b1l)
        for ob in range(NOB):
            ev_b = b.do("tensor", lambda t, ob=ob: t.matmul(
                pB5[:], onec[0:1, :], b1row[0:1, ob * 512:(ob + 1) * 512],
                start=True, stop=True), mark="pe")
            b.wait("vector", ev_b)
            ev_v = b.do("vector", lambda v, ob=ob: (
                v.tensor_copy(b1bc[:, ob * 512:(ob + 1) * 512], pB5[:]),
                v.drain())[0], mark="dv")
            b.wait("tensor", ev_v)

        w_free = [None, None]
        ui = 0
        h_free = [None, None]
        ev_evF = [None, None]
        for ob in range(NOB):
            wb_ = w1b[ob % 2]
            b.wait("sync", w_free[ob % 2])
            ev_w = b.dma("sync", wb_[:].rearrange("p (dc o) -> p dc o",
                                                  dc=ND),
                         w1t_sc[ob].rearrange("dc d o -> d dc o"))
            last_mm = None
            for qt in range(NT):
                ps = pF[ui % 2]
                b.waits("tensor", ev_xq + [ev_w, ev_evF[ui % 2]])

                def mmf(t, ob=ob, qt=qt, ps=ps, wb_=wb_):
                    ins = None
                    for dc in range(ND):
                        ins = t.matmul(ps[:],
                                       xq1b[:, dc * T + qt * 128:
                                            dc * T + (qt + 1) * 128],
                                       wb_[:, dc * 512:(dc + 1) * 512],
                                       start=(dc == 0), stop=(dc == ND - 1))
                    return ins
                ev_mm = b.do("tensor", mmf, mark="pe")
                last_mm = ev_mm
                b.waits("vector", [ev_mm, h_free[ui % 2]])

                def evf(v, ps=ps, qt=qt, ob=ob, j=ui % 2):
                    return v.scalar_tensor_tensor(
                        htmp[j][:], ps[:], c1c[:, qt:qt + 1],
                        b1bc[:, ob * 512:(ob + 1) * 512],
                        ALU.mult, ALU.add)
                ev_e = b.do("vector", evf, mark="dv")
                ev_evF[ui % 2] = ev_e
                b.wait("scalar", ev_e)
                ev_s = b.do("scalar", lambda s, j=ui % 2: s.activation(
                    hsil[j][:], htmp[j][:], AF.Silu), mark="ac")
                b.wait("gpsimd", ev_s)
                h_free[ui % 2] = b.dma(
                    "gpsimd", h_sc[qt, :, ob * 512:(ob + 1) * 512],
                    hsil[ui % 2][:])
                ui += 1
                if w2_th and ui % 2 == 0:
                    w2_th.pop(0)()
            w_free[ob % 2] = last_mm
        while w2_th:
            w2_th.pop(0)()
        b.barrier()
        pf.close()
        pf_ps.close()

    b.phase = "P4out"
    if upto >= 7:
        p4 = ExitStack()
        e4 = p4.enter_context
        woutb = e4(nc.sbuf_tensor("woutb", [128, 32 * D], BF16))
        wst4 = [e4(nc.sbuf_tensor(f"wst4{j}", [128, D], F32))
                for j in range(2)]
        abuf = [e4(nc.sbuf_tensor(f"abuf{j}", [128, 4096], BF16))
                for j in range(2)]
        sst = [e4(nc.sbuf_tensor(f"sst{j}", [128, D], BF16))
               for j in range(2)]
        p4_ps = ExitStack()
        pO = [p4_ps.enter_context(nc.psum_tensor(f"pO{j}", [128, 512], F32))
              for j in range(2)]

        stg_free = [None, None]
        for h in range(32):
            sb = wst4[h % 2]
            b.wait("sync", stg_free[h % 2])
            ev_w = b.dma("sync", sb[:], wout_h[h * 128:(h + 1) * 128, :])
            b.wait("scalar", ev_w)
            ev_c = b.do("scalar", lambda s, sb=sb, h=h: s.copy(
                woutb[:, h * D:(h + 1) * D], sb[:]), mark="ac")
            stg_free[h % 2] = ev_c
        cast_done = stg_free[31 % 2]
        a_free = [None, None]
        s_free = [None, None]
        ev_ev4 = [None, None]
        for qt in range(NT):
            ab = abuf[qt % 2]
            b.wait("sync", a_free[qt % 2])
            ev_a = b.dma("sync", ab[:].rearrange("p (kvh f) -> p kvh f",
                                                 kvh=NKV),
                         at_sc[:, :, qt, :].rearrange("kvh hd f -> hd kvh f"))
            ss = sst[qt % 2]
            b.wait("vector", s_free[qt % 2])
            last_mm = None
            for dpb in range(4):
                ps = pO[dpb % 2]
                b.waits("tensor", [ev_a, cast_done, ev_ev4[dpb % 2]])

                def mmo(t, dpb=dpb, ps=ps, ab=ab):
                    ins = None
                    for h in range(32):
                        ins = t.matmul(
                            ps[:],
                            ab[:, (h // 4) * 512 + (h % 4) * 128:
                               (h // 4) * 512 + (h % 4 + 1) * 128],
                            woutb[:, h * D + dpb * 512:
                                  h * D + (dpb + 1) * 512],
                            start=(h == 0), stop=(h == 31))
                    return ins
                ev_mm = b.do("tensor", mmo, mark="pe")
                last_mm = ev_mm
                b.wait("vector", ev_mm)
                ev_e = b.do("vector", lambda v, ps=ps, dpb=dpb, ss=ss:
                            v.tensor_copy(ss[:, dpb * 512:(dpb + 1) * 512],
                                          ps[:]), mark="dv")
                ev_ev4[dpb % 2] = ev_e
            a_free[qt % 2] = last_mm
            b.wait("gpsimd", ev_ev4[1])
            b.wait("gpsimd", ev_ev4[0])
            s_free[qt % 2] = b.dma("gpsimd", s_sc[qt], ss[:])
        b.barrier()
        p4.close()
        p4_ps.close()

    b.phase = "F2"
    if upto >= 8:
        pg = ExitStack()
        eg = pg.enter_context
        gbc = eg(nc.sbuf_tensor("gbc", [128, DI], F32))
        bbc = eg(nc.sbuf_tensor("bbc", [128, DI], F32))
        stq = eg(nc.sbuf_tensor("stq", [128, 34], F32))
        pg_ps = ExitStack()
        pG = pg_ps.enter_context(nc.psum_tensor("pG", [128, 512], F32))
        pT3 = [pg_ps.enter_context(nc.psum_tensor(f"pT3{j}", [128, 512],
                                                  BF16)) for j in range(2)]

        rows_ctx = ExitStack()
        growb = rows_ctx.enter_context(
            nc.sbuf_tensor("growb", [1, 2 * DI], F32))
        ev_gl = b.dma("sync", growb[0:1, 0:DI], g2_h[0:1, :])
        ev_bl = b.dma("sync", growb[0:1, DI:2 * DI], bb2_h[0:1, :])
        b.waits("tensor", [ev_gl, ev_bl])
        ev_gb_done = None
        for ri, dstt in ((0, gbc), (1, bbc)):
            for ob in range(NOB):
                ev_b = b.do("tensor", lambda t, ob=ob, ri=ri:
                            t.matmul(pG[:], onec[0:1, :],
                                     growb[0:1, ri * DI + ob * 512:
                                           ri * DI + (ob + 1) * 512],
                                     start=True, stop=True), mark="pe")
                b.wait("vector", ev_b)
                ev_v = b.do("vector", lambda v, ob=ob, dstt=dstt: (
                    v.tensor_copy(dstt[:, ob * 512:(ob + 1) * 512], pG[:]),
                    v.drain())[0], mark="dv")
                b.wait("tensor", ev_v)
                ev_gb_done = ev_v
        rows_ctx.close()
        hb = eg(nc.sbuf_tensor("hbF", [128, DI], F32))
        ab2 = [eg(nc.sbuf_tensor(f"ab2{j}", [128, DI], F32))
               for j in range(2)]
        xqt2 = [eg(nc.sbuf_tensor(f"xqt2{j}", [128, DI], BF16))
                for j in range(2)]

        (ASUM, ASQ, MU2c, EX2c, VARc, VEPSc, LNc, INVc, SQ2, EXRc, LNRc,
         RST2, T2c, M2c, RM2c, QS2c, NMIc) = range(17)

        h_free = [ev_gb_done, ev_gb_done]
        st2_free = [None, None]
        pool_prev = None
        sq2_prev = None
        hb_free = [None]
        for qt in range(NT):
            j = qt % 2
            B0 = 17 * j
            ab = ab2[j]
            xq = xqt2[j]

            def C(i):
                return stq[:, B0 + i:B0 + i + 1]
            b.waits("sync", [h_free[j], hb_free[0]])
            ev_h = b.dma("sync", hb[:], h_sc[qt])
            ev_ms = b.do("vector", lambda v, C=C: (
                v.memset(C(ASUM), 0.0),
                v.memset(C(ASQ), 0.0),
                v.memset(C(SQ2), 0.0))[-1], mark="dv")
            b.waits("scalar", [ev_h, ev_ms, h_free[j], st2_free[j]])

            def gelu_sq(s, C=C, ab=ab, xq=xq):
                s.activation(ab[:], hb[:], AF.Gelu, accum_out=C(ASUM))
                s.drain()
                return s.activation(xq[:], ab[:], AF.Square,
                                    accum_out=C(ASQ))
            ev_g = b.do("scalar", gelu_sq, mark="ac")
            hb_free[0] = ev_g
            b.wait("vector", ev_g)

            def stats2(v, C=C):
                v.tensor_scalar_mul(C(MU2c), C(ASUM), 1.0 / DI)
                v.tensor_scalar_mul(C(EX2c), C(ASQ), 1.0 / DI)
                v.drain()
                v.tensor_mul(C(VARc), C(MU2c), C(MU2c))
                v.drain()
                v.tensor_sub(C(VARc), C(EX2c), C(VARc))
                v.drain()
                return v.tensor_scalar_add(C(VEPSc), C(VARc), EPS)
            ev_s2 = b.do("vector", stats2, mark="dv")
            b.wait("scalar", ev_s2)
            ev_i2 = b.do("scalar", lambda s, C=C: (
                s.activation(C(LNc), C(VEPSc), AF.Ln),
                s.drain(),
                s.activation(C(INVc), C(LNc), AF.Exp,
                             scale=NEG_HALF))[-1], mark="ac")
            b.wait("vector", ev_i2)
            ev_nm = b.do("vector", lambda v, C=C: (
                v.drain(),
                v.tensor_mul(C(NMIc), C(MU2c), C(INVc)),
                v.drain(),
                v.tensor_scalar_mul(C(NMIc), C(NMIc), -1.0))[-1], mark="dv")
            b.wait("scalar", ev_nm)
            ev_zn = b.do("scalar", lambda s, C=C, ab=ab: (
                s.drain(),
                s.activation(ab[:], ab[:], AF.Identity, bias=C(NMIc),
                             scale=C(INVc)))[-1], mark="ac")
            SPL = 5120
            b.waits("gpsimd", [ev_zn, pool_prev])
            ev_gmp = b.do("gpsimd", lambda g, ab=ab: (
                g.tensor_mul(ab[:, SPL:DI], ab[:, SPL:DI], gbc[:, SPL:DI]),
                g.drain(),
                g.tensor_add(ab[:, SPL:DI], ab[:, SPL:DI],
                             bbc[:, SPL:DI]))[-1], mark="gp")
            pool_prev = ev_gmp
            b.waits("vector", [ev_zn])
            ev_gmv = b.do("vector", lambda v, ab=ab: (
                v.drain(),
                v.tensor_mul(ab[:, 0:SPL], ab[:, 0:SPL], gbc[:, 0:SPL]),
                v.drain(),
                v.tensor_add(ab[:, 0:SPL], ab[:, 0:SPL],
                             bbc[:, 0:SPL]))[-1], mark="dv")
            ev_gm = ev_gmp
            b.waits("vector", [ev_gmp, ev_gmv])
            ev_mx = b.do("vector", lambda v, C=C, ab=ab: (
                v.drain(),
                v.reduce_max(C(T2c), ab[:], AX.X,
                             apply_absolute_value=True))[-1], mark="dv")
            b.waits("scalar", [ev_gm, ev_gmv, sq2_prev])
            ev_sq2 = b.do("scalar", lambda s, C=C, ab=ab, xq=xq:
                          s.activation(xq[:], ab[:], AF.Square,
                                       accum_out=C(SQ2)), mark="ac")
            sq2_prev = ev_sq2
            b.wait("vector", ev_sq2)

            def preR(v, C=C):
                v.tensor_scalar_mul(C(EXRc), C(SQ2), 1.0 / DI)
                v.drain()
                return v.tensor_scalar_add(C(EXRc), C(EXRc), EPS)
            ev_pr = b.do("vector", preR, mark="dv")
            b.wait("scalar", ev_pr)
            ev_r2 = b.do("scalar", lambda s, C=C: (
                s.activation(C(LNRc), C(EXRc), AF.Ln),
                s.drain(),
                s.activation(C(RST2), C(LNRc), AF.Exp,
                             scale=NEG_HALF))[-1], mark="ac")
            b.waits("vector", [ev_r2, ev_mx])

            def quant2w(v, qt=qt, C=C, ab=ab, xq=xq):
                v.tensor_mul(C(M2c), C(RST2), C(T2c))
                v.drain()
                v.tensor_scalar_max(C(M2c), C(M2c), EPS)
                v.drain()
                v.tensor_scalar(c2c[:, qt:qt + 1], C(M2c), wsc[:, 5:6],
                                1.0 / 127.0, ALU.mult, ALU.mult)
                v.reciprocal(C(RM2c), C(M2c))
                v.drain()
                v.tensor_mul(C(QS2c), C(RST2), C(RM2c))
                v.drain()
                v.tensor_scalar_mul(C(QS2c), C(QS2c), 127.0)
                v.drain()
                v.tensor_scalar(ab[:], ab[:], C(QS2c), MAGIC, ALU.mult,
                                ALU.add)
                v.drain()
                return v.tensor_single_scalar(xq[:], ab[:], MAGIC,
                                              ALU.subtract)
            ev_q2 = b.do("vector", quant2w, mark="dv")
            h_free[j] = ev_q2
            b.waits("tensor", [ev_q2, st2_free[j]])
            last_ev = None
            evx_prev = [None, None]
            for grp in range(NT):
                pt = pT3[grp % 2]

                def trx(t, grp=grp, pt=pt, xq=xq):
                    ins = None
                    for jj in range(4):
                        oc = grp * 4 + jj
                        ins = t.transpose(pt[:, jj * 128:(jj + 1) * 128],
                                          xq[:, oc * 128:(oc + 1) * 128],
                                          identb[:])
                    return ins
                b.wait("tensor", evx_prev[grp % 2])
                ev_t = b.do("tensor", trx, mark="pe")
                b.wait("vector", ev_t)

                def evx(v, grp=grp, pt=pt, xq=xq):
                    dst = xq[:].rearrange("p (oc t) -> p oc t", oc=NOC)[
                        :, grp * 4:(grp + 1) * 4, :]
                    return v.tensor_copy(dst, pt[:].rearrange(
                        "p (j t) -> p j t", j=4))
                last_ev = b.do("vector", evx, mark="dv")
                evx_prev[grp % 2] = last_ev
            b.wait("gpsimd", last_ev)
            st2_free[j] = b.dma(
                "gpsimd",
                xq2_sc[:, :, qt * 128:(qt + 1) * 128].rearrange(
                    "oc o t -> o oc t"),
                xqt2[j][:].rearrange("p (oc t) -> p oc t", oc=NOC))
        b.barrier()
        pg.close()
        pg_ps.close()

    b.phase = "M2"
    if upto >= 9:
        pm = ExitStack()
        em = pm.enter_context
        x2q = em(nc.sbuf_tensor("x2q", [128, NOC * 512], BF16))
        w2b = [em(nc.sbuf_tensor(f"w2b{j}", [128, NOC * 128], BF16))
               for j in range(2)]
        c2bc = em(nc.sbuf_tensor("c2bc", [128, 512], F32))
        c2r = em(nc.sbuf_tensor("c2r", [1, 128], F32))
        b2row_t = em(nc.sbuf_tensor("b2row_t", [1, D], F32))
        b2row = b2row_t[0:1, :]
        ev_b2l = b.dma("sync", b2row, b2_h[0:1, :])
        mtmp = [em(nc.sbuf_tensor(f"mtmp{j}", [128, 512],
                                  F32)) for j in range(2)]
        gb_ = [em(nc.sbuf_tensor(f"gb_{j}", [128, 512],
                                 BF16)) for j in range(2)]
        gq = em(nc.sbuf_tensor("gq", [128, 4 * D], BF16))
        xcb = em(nc.sbuf_tensor("xcb", [128, D], F32))
        scb2 = em(nc.sbuf_tensor("scb2", [128, D], BF16))
        sg32 = em(nc.sbuf_tensor("sg32", [128, D], F32))
        ocb = em(nc.sbuf_tensor("ocb", [128, D], F32))
        pm_ps = ExitStack()
        pM = [pm_ps.enter_context(nc.psum_tensor(f"pM{j}", [128, 512], F32))
              for j in range(2)]
        pTg = [pm_ps.enter_context(nc.psum_tensor(f"pTg{j}",
               [128, 512], BF16)) for j in range(2)]
        pCt = pm_ps.enter_context(nc.psum_tensor("pCt", [1, 128], F32))
        pCb = pm_ps.enter_context(nc.psum_tensor("pCb", [128, 128], F32))

        b.wait("tensor", ev_b2l)
        for dpc in range(ND):
            ev_b = b.do("tensor", lambda t, dpc=dpc: t.matmul(
                pCb[:, 0:1], b2row[0:1, dpc * 128:(dpc + 1) * 128],
                one11[0:1, 0:1], start=True, stop=True), mark="pe")
            b.wait("vector", ev_b)
            ev_v = b.do("vector", lambda v, dpc=dpc: v.tensor_copy(
                b2cols[:, dpc:dpc + 1], pCb[:, 0:1]), mark="dv")
            b.wait("tensor", ev_v)

        x2_free = None
        w_free = [None, None]
        ev_evM = [None, None]
        ev_gev = None
        prev_sl = None
        prev_tg = None
        for quar in range(4):
            b.wait("sync", x2_free)
            ev_x2 = b.dma("sync",
                          x2q[:].rearrange("p (oc t) -> p oc t", oc=NOC),
                          xq2_sc[:, :, quar * 512:(quar + 1) * 512]
                          .rearrange("oc o t -> o oc t"))
            for j in range(4):
                qt = quar * 4 + j
                ev_t = b.do("tensor", lambda t, qt=qt: t.matmul(
                    pCt[0:1, :], c2c[:, qt:qt + 1], identf[:],
                    start=True, stop=True), mark="pe")
                b.wait("vector", ev_t)
                ev_v = b.do("vector", lambda v: v.tensor_copy(c2r[:],
                                                              pCt[0:1, :]),
                            mark="dv")
                b.wait("tensor", ev_v)
                ev_b2 = b.do("tensor", lambda t: t.matmul(
                    pCb[:], onec[0:1, :], c2r[0:1, :], start=True,
                    stop=True), mark="pe")
                b.wait("vector", ev_b2)
                ev_c = b.do("vector", lambda v, j=j: (
                    v.tensor_copy(c2bc[:, j * 128:(j + 1) * 128], pCb[:]),
                    v.drain())[0], mark="dv")
                b.wait("tensor", ev_c)

            mm_evs = {}

            def emit_post(dpc):
                ps = pM[dpc % 2]
                b.waits("vector", [mm_evs[dpc], prev_sl])
                ev_e = b.do("vector", lambda v, ps=ps, dpc=dpc:
                            v.tensor_tensor(mtmp[dpc % 2][:], ps[:],
                                            c2bc[:], ALU.mult), mark="dv")
                ev_evM[dpc % 2] = ev_e
                b.waits("scalar", [ev_e, prev_tg])
                ev_sl = b.do("scalar", lambda s, dpc=dpc: s.activation(
                    gb_[dpc % 2][:], mtmp[dpc % 2][:], AF.Silu,
                    bias=b2cols[:, dpc:dpc + 1]), mark="ac")
                b.waits("tensor", [ev_sl, ev_gev])
                ev_tg = b.do("tensor", lambda t, dpc=dpc: (
                    [t.transpose(pTg[dpc % 2][:, j * 128:(j + 1) * 128],
                                 gb_[dpc % 2][:, j * 128:(j + 1) * 128],
                                 identb[:])
                     for j in range(4)])[-1], mark="pe")
                b.wait("vector", ev_tg)

                def evg(v, dpc=dpc):
                    dst = gq[:].rearrange("p (tl d) -> p tl d", tl=4)[
                        :, :, dpc * 128:(dpc + 1) * 128]
                    return v.tensor_copy(
                        dst, pTg[dpc % 2][:].rearrange("p (j t) -> p j t",
                                                       j=4))
                ev_g2 = b.do("vector", evg, mark="dv")
                return ev_sl, ev_tg, ev_g2

            for dpc in range(ND):
                wb2_ = w2b[dpc % 2]
                b.wait("sync", w_free[dpc % 2])
                ev_w = b.dma("sync",
                             wb2_[:].rearrange("p (oc dp) -> p oc dp",
                                               oc=NOC),
                             w2t_sc[dpc].rearrange("oc o dp -> o oc dp"))
                ps = pM[dpc % 2]
                b.waits("tensor", [ev_x2, ev_w, ev_evM[dpc % 2]])

                def mm2(t, ps=ps, wb2_=wb2_):
                    ins = None
                    for oc in range(NOC):
                        ins = t.matmul(ps[:],
                                       wb2_[:, oc * 128:(oc + 1) * 128],
                                       x2q[:, oc * 512:(oc + 1) * 512],
                                       start=(oc == 0), stop=(oc == NOC - 1))
                    return ins
                ev_mm = b.do("tensor", mm2, mark="pe")
                mm_evs[dpc] = ev_mm
                w_free[dpc % 2] = ev_mm
                if dpc == ND - 1:
                    x2_free = ev_mm
                if dpc >= 1:
                    prev_sl, prev_tg, ev_gev = emit_post(dpc - 1)
            prev_sl, prev_tg, ev_gev = emit_post(ND - 1)
            comb_free = None
            for tl in range(4):
                qt = quar * 4 + tl
                b.wait("sync", comb_free)
                ev_xr = b.dma("sync", xcb[:],
                              x_h[qt * 128:(qt + 1) * 128, :])
                ev_sr = b.dma("sync", scb2[:], s_sc[qt])
                b.waits("vector", [ev_xr, ev_sr, ev_gev])

                def comb(v, tl=tl):
                    v.drain()
                    v.tensor_mul(sg32[:], scb2[:],
                                 gq[:, tl * D:(tl + 1) * D])
                    v.drain()
                    return v.tensor_add(ocb[:], xcb[:], sg32[:])
                ev_c = b.do("vector", comb, mark="dv")
                comb_free = ev_c
                b.wait("gpsimd", ev_c)
                ev_y = b.dma("gpsimd", y_h[qt * 128:(qt + 1) * 128, :],
                             ocb[:])
                b.wait("vector", ev_y)
        pm.close()
        pm_ps.close()

    def fin(g):
        for s_ in ("dg0", "dg1", "dg2", "dg3"):
            g.wait_ge(b.semh[s_], b.cnt.get(s_, 0))
    b.emit("gpsimd", fin)

    phase_ranges = []

    def run_ops(eng, key):
        cur = None
        start = nc.n_instructions()
        for ph, f in b.ops[key]:
            if ph != cur:
                if cur is not None:
                    phase_ranges.append((key, cur, start,
                                         nc.n_instructions()))
                cur = ph
                start = nc.n_instructions()
            f(eng)
        if cur is not None:
            phase_ranges.append((key, cur, start, nc.n_instructions()))

    with nc.Block() as block:
        @block.sync
        def _(eng):
            run_ops(eng, "sync")

        @block.tensor
        def _(eng):
            run_ops(eng, "tensor")

        @block.vector
        def _(eng):
            run_ops(eng, "vector")

        @block.scalar
        def _(eng):
            run_ops(eng, "scalar")

        @block.gpsimd
        def _(eng):
            run_ops(eng, "gpsimd")

    ctx.close()
    nc._phase_ranges = phase_ranges
    return nc


def _rope_tables():
    e_ = np.arange(0, 128, 2, dtype=np.float32) / np.float32(128.0)
    p_ = np.power(np.float32(10000.0), e_).astype(np.float32)
    invf = (np.float32(1.0) / p_).astype(np.float32)
    pos = np.arange(T, dtype=np.float32)
    freqs = (pos[:, None] * invf[None, :]).astype(np.float32)
    emb = np.concatenate([freqs, freqs], axis=1)
    cos = np.cos(emb).astype(np.float32).T.copy()
    sin = np.sin(emb).astype(np.float32).T.copy()
    ss = sin.copy()
    ss[0:64] = -sin[0:64]
    return np.ascontiguousarray(cos), np.ascontiguousarray(ss)


def kernel(L0_attn_u, L0_attn_W_in, L0_attn_W_out, L0_ffn_w1, L0_ffn_b1,
           L0_ffn_ln_g, L0_ffn_ln_b, L0_ffn_w2, L0_ffn_b2):
    nb = L0_attn_u.shape[0]
    wkvq = np.ascontiguousarray(np.asarray(L0_attn_W_in).T[:, 4096:10240],
                                dtype=np.float32)
    woutT = np.ascontiguousarray(np.asarray(L0_attn_W_out).T,
                                 dtype=np.float32)
    w1T = np.ascontiguousarray(np.asarray(L0_ffn_w1).T, dtype=np.float32)
    w2T = np.ascontiguousarray(np.asarray(L0_ffn_w2).T, dtype=np.float32)
    b1 = np.asarray(L0_ffn_b1, dtype=np.float32).reshape(1, DI)
    lng = np.asarray(L0_ffn_ln_g, dtype=np.float32).reshape(1, DI)
    lnb = np.asarray(L0_ffn_ln_b, dtype=np.float32).reshape(1, DI)
    b2 = np.asarray(L0_ffn_b2, dtype=np.float32).reshape(1, D)
    cosT, sinT = _rope_tables()
    cosTb = cosT.astype(ml_dtypes.bfloat16)
    sinTb = sinT.astype(ml_dtypes.bfloat16)
    base = np.where(np.arange(128)[:, None] > np.arange(128)[None, :],
                    np.float32(-1e30), np.float32(0.0)).astype(np.float32)
    trimask = np.ascontiguousarray(
        np.tile(base, (1, 8)).astype(ml_dtypes.bfloat16))
    ident = np.eye(128, dtype=np.float32)

    nc = build_nc(debug=DEBUG, upto=UPTO)
    in_maps = []
    for i in range(nb):
        in_maps.append(dict(
            x=np.ascontiguousarray(np.asarray(L0_attn_u[i],
                                              dtype=np.float32)),
            wkvq=wkvq, woutT=woutT, w1T=w1T, w2T=w2T, b1=b1, lng=lng,
            lnb=lnb, b2=b2, cosTb=cosTb, sinTb=sinTb, trimask=trimask,
            ident=ident))
    res = run_bass_kernel_spmd(nc, in_maps, core_ids=list(range(nb)))
    out = np.stack([np.asarray(r["y"], dtype=np.float32)
                    for r in res.results], axis=0)
    if DEBUG:
        kernel.last_results = res
    return out


# version 10
# speedup vs baseline: 1.3556x; 1.0172x over previous
import numpy as np
import ml_dtypes
from contextlib import ExitStack
import concourse.bass as bass
import concourse.mybir as mybir
from concourse.bass_utils import run_bass_kernel_spmd

F32 = mybir.dt.float32
BF16 = mybir.dt.bfloat16
AF = mybir.ActivationFunctionType
ALU = mybir.AluOpType
AX = mybir.AxisListType

MAGIC = 12582912.0
SC_ATT = 0.08838834764831845
NEG_HALF = -0.5
EPS = 1e-5
T, D, DI = 2048, 2048, 8192
NT, ND = 16, 16
NKV = 8
NOB, NOC = 16, 64

DEBUG = False
UPTO = 99
ENGS = ("sync", "tensor", "vector", "scalar", "gpsimd")


class B:
    def __init__(self):
        self.ops = {k: [] for k in ENGS}
        self.cnt = {}
        self.semh = {}
        self.lastw = {e: {} for e in ENGS}
        self.phase = "init"

    def emit(self, eng, fn):
        self.ops[eng].append((self.phase, fn))

    def wait(self, eng, evt):
        if evt is None:
            return
        sem, val, peng = evt
        if peng == eng or val <= 0:
            return
        if self.lastw[eng].get(sem, 0) >= val:
            return
        self.lastw[eng][sem] = val

        def f(e, sem=sem, val=val):
            e.wait_ge(self.semh[sem], val)
        self.emit(eng, f)

    def waits(self, eng, evts):
        for ev in evts:
            self.wait(eng, ev)

    def bump(self, sem, k):
        self.cnt[sem] = self.cnt.get(sem, 0) + k
        return self.cnt[sem]

    DMA_POOL = {"sync": ["dl0", "dl1", "dl2", "dl3"],
                "scalar": ["ds0", "ds1"],
                "gpsimd": ["dg0", "dg1", "dg2", "dg3"]}

    def dma(self, eng, dst, src):
        pool = self.DMA_POOL[eng]
        k = self.cnt.get("_rr_" + eng, 0)
        self.cnt["_rr_" + eng] = k + 1
        sem = pool[k % len(pool)]
        before = self.cnt.get(sem, 0)
        self.wait(eng, (sem, before, None))
        val = self.bump(sem, 16)

        def f(e, dst=dst, src=src, sem=sem):
            e.dma_start(out=dst, in_=src).then_inc(self.semh[sem], 16)
        self.emit(eng, f)
        return (sem, val, None)

    def do(self, eng, fn, mark=None):
        if mark is None:
            self.emit(eng, lambda e, fn=fn: fn(e))
            return None
        val = self.bump(mark, 1)

        def f(e, fn=fn, mark=mark):
            fn(e).then_inc(self.semh[mark], 1)
        self.emit(eng, f)
        return (mark, val, eng)

    def barrier(self):
        for eng, pool in self.DMA_POOL.items():
            for sem in pool:
                self.wait(eng, (sem, self.cnt.get(sem, 0), None))
        tgt = self.bump("bar", len(ENGS))
        for eng in ENGS:
            def f(e):
                try:
                    e.drain()
                except Exception:
                    pass
                e.nop().then_inc(self.semh["bar"], 1)
            self.emit(eng, f)
        for eng in ENGS:
            def f(e, tgt=tgt):
                e.wait_ge(self.semh["bar"], tgt)
            self.emit(eng, f)
            self.lastw[eng]["bar"] = tgt


def build_nc(debug=False, upto=99):
    nc = bass.Bass("TRN2", target_bir_lowering=False, debug=False)
    b = B()

    def inp(name, shape, dtype=F32):
        return nc.dram_tensor(name, shape, dtype, kind="ExternalInput").ap()

    x_h = inp("x", [T, D])
    wkvq_h = inp("wkvq", [D, 6144])
    wout_h = inp("woutT", [4096, D])
    w1_h = inp("w1T", [D, DI])
    w2_h = inp("w2T", [DI, D])
    b1_h = inp("b1", [1, DI])
    g2_h = inp("lng", [1, DI])
    bb2_h = inp("lnb", [1, DI])
    b2_h = inp("b2", [1, D])
    cosb_h = inp("cosTb", [128, T], BF16)
    sinb_h = inp("sinTb", [128, T], BF16)
    tri_h = inp("trimask", [128, 1024], BF16)
    id_h = inp("ident", [128, 128])
    y_h = nc.dram_tensor("y", [T, D], F32, kind="ExternalOutput").ap()

    skind = "ExternalOutput" if debug else "Internal"

    def scr(name, shape, dtype):
        return nc.dram_tensor(name, shape, dtype, kind=skind).ap()

    xq1_sc = scr("xq1_sc", [NT, ND, 128, 128], BF16)
    q_sc = scr("q_sc", [NKV, 128, NT, 4, 128], BF16)
    at_sc = scr("at_sc", [NKV, 128, NT, 512], BF16)
    s_sc = scr("s_sc", [NT, 128, D], BF16)
    h_sc = scr("h_sc", [NT, 128, DI], F32)
    xq2_sc = scr("xq2_sc", [NOC, 128, D], BF16)
    w1t_sc = scr("w1t_sc", [NOB, ND, 128, 512], BF16)
    wfold_sc = nc.dram_tensor("wfold_sc", [2, 32], F32, kind="Internal").ap()
    wrep_sc = nc.dram_tensor("wrep_sc", [2, 128, 3], F32,
                             kind="Internal").ap()
    w2t_sc = scr("w2t_sc", [ND, NOC, 128, 128], BF16)
    dbg = {}
    if debug:
        for nm, sh, dt in (("d_xnfm", [128, ND * T], BF16),
                           ("d_kfm", [128, NKV * T], BF16),
                           ("d_vkm", [128, NKV * T], BF16),
                           ("d_stats", [128, 64], F32),
                           ("d_wsc", [128, 8], F32)):
            dbg[nm] = nc.dram_tensor(nm, sh, dt, kind="ExternalOutput").ap()

    ctx = ExitStack()
    e = ctx.enter_context
    for s in ("dl0", "dl1", "dl2", "dl3", "ds0", "ds1", "dg0", "dg1",
              "dg2", "dg3", "pe", "dv", "ac", "gp", "bar"):
        b.semh[s] = e(nc.semaphore(s))

    cpool = e(nc.sbuf_tensor("cpool", [128, 4], F32))
    nc.const_aps.register(F32, SC_ATT, cpool[:, 0:1])
    nc.const_aps.register(F32, NEG_HALF, cpool[:, 1:2])
    nc.const_aps.register(F32, -1.0, cpool[:, 2:3])
    onec = e(nc.sbuf_tensor("onec", [1, 128], F32))
    onecol = e(nc.sbuf_tensor("onecol", [128, 1], F32))
    one11 = e(nc.sbuf_tensor("one11", [1, 1], F32))
    ones_bf = e(nc.sbuf_tensor("ones_bf", [128, 128], BF16))
    identb = e(nc.sbuf_tensor("identb", [128, 128], BF16))
    identf = e(nc.sbuf_tensor("identf", [128, 128], F32))
    tri2b = e(nc.sbuf_tensor("tri2b", [128, 1024], BF16))
    m1c = e(nc.sbuf_tensor("m1c", [128, NT], F32))
    c1c = e(nc.sbuf_tensor("c1c", [128, NT], F32))
    c2c = e(nc.sbuf_tensor("c2c", [128, NT], F32))
    wsc = e(nc.sbuf_tensor("wsc", [128, 8], F32))
    b2cols = e(nc.sbuf_tensor("b2cols", [128, ND], F32))
    srow = e(nc.sbuf_tensor("srow_w", [1, 96], F32))
    wacc = e(nc.sbuf_tensor("wacc", [128, 8], F32))

    b.dma("sync", tri2b[:], tri_h[:, :])
    ev_idf = b.dma("sync", identf[:], id_h[:, :])

    def init_consts(v):
        v.memset(onec[:], 1.0)
        v.memset(onecol[:], 1.0)
        v.memset(one11[:], 1.0)
        v.memset(ones_bf[:], 1.0)
        v.memset(cpool[:, 0:1], SC_ATT)
        v.memset(cpool[:, 2:3], -1.0)
        return v.memset(cpool[:, 1:2], NEG_HALF)
    ev_cst = b.do("vector", init_consts, mark="dv")
    b.wait("vector", ev_idf)
    ev_idb = b.do("vector", lambda v: v.tensor_copy(identb[:], identf[:]),
                  mark="dv")

    xn_ctx = ExitStack()
    xn_fm = xn_ctx.enter_context(nc.sbuf_tensor("xn_fm", [128, ND * T], BF16))
    rope_ctx = ExitStack()
    er = rope_ctx.enter_context
    cosTb = er(nc.sbuf_tensor("cosTb_sb", [128, T], BF16))
    ssTb = er(nc.sbuf_tensor("ssTb_sb", [128, T], BF16))
    b.dma("sync", cosTb[:], cosb_h[:, :])
    b.dma("sync", ssTb[:], sinb_h[:, :])

    b.phase = "A"
    if upto >= 1:
        pa = ExitStack()
        ea = pa.enter_context
        xld = [ea(nc.sbuf_tensor(f"xld{j}", [128, D], F32)) for j in range(2)]
        xn32 = ea(nc.sbuf_tensor("xn32", [128, D], F32))
        tq32 = ea(nc.sbuf_tensor("tq32", [128, D], F32))
        xnbf = ea(nc.sbuf_tensor("xnbf", [128, D], BF16))
        xqbf = ea(nc.sbuf_tensor("xqbf", [128, D], BF16))
        xqst = ea(nc.sbuf_tensor("xqst", [128, D], BF16))
        trashA = ea(nc.sbuf_tensor("trashA", [128, D], BF16))
        stc = ea(nc.sbuf_tensor("stc", [128, 20], F32))
        pA_ps = ExitStack()
        pT = pA_ps.enter_context(nc.psum_tensor("pTA", [128, 512], BF16))
        pT2 = pA_ps.enter_context(nc.psum_tensor("pTA2", [128, 512], BF16))

        (SU, SQ, MU, EX2, MU2, VAR, VEPS, LNV, INV, I2, MX2, M2E, LNM, RST,
         MX, QS, RM) = range(17)

        xld_free = [None, None]
        st_free = None
        for i in range(NT):
            buf = xld[i % 2]
            b.wait("sync", xld_free[i % 2])
            ev_x = b.dma("sync", buf[:], x_h[i * 128:(i + 1) * 128, :])
            b.wait("vector", ev_x)
            ev_ms = b.do("vector", lambda v, bu=buf: (
                v.drain(),
                v.reduce_sum(stc[:, SU:SU + 1], bu[:], AX.X),
                v.memset(stc[:, SQ:SQ + 1], 0.0))[-1], mark="dv")
            b.waits("scalar", [ev_x, ev_ms])
            ev_sq = b.do("scalar", lambda s, bu=buf: s.activation(
                trashA[:], bu[:], AF.Square, accum_out=stc[:, SQ:SQ + 1]),
                mark="ac")
            b.wait("vector", ev_sq)

            def stats1(v):
                v.tensor_scalar_mul(stc[:, MU:MU + 1], stc[:, SU:SU + 1],
                                    1.0 / D)
                v.tensor_scalar_mul(stc[:, EX2:EX2 + 1], stc[:, SQ:SQ + 1],
                                    1.0 / D)
                v.drain()
                v.tensor_mul(stc[:, MU2:MU2 + 1], stc[:, MU:MU + 1],
                             stc[:, MU:MU + 1])
                v.drain()
                v.tensor_sub(stc[:, VAR:VAR + 1], stc[:, EX2:EX2 + 1],
                             stc[:, MU2:MU2 + 1])
                v.drain()
                return v.tensor_scalar_add(stc[:, VEPS:VEPS + 1],
                                           stc[:, VAR:VAR + 1], EPS)
            ev_v1 = b.do("vector", stats1, mark="dv")
            b.wait("scalar", ev_v1)

            def act_inv(s):
                s.activation(stc[:, LNV:LNV + 1], stc[:, VEPS:VEPS + 1], AF.Ln)
                s.drain()
                return s.activation(stc[:, INV:INV + 1], stc[:, LNV:LNV + 1],
                                    AF.Exp, scale=NEG_HALF)
            ev_inv = b.do("scalar", act_inv, mark="ac")
            b.wait("vector", ev_inv)

            def xnf(v, bu=buf):
                v.drain()
                v.tensor_scalar(xn32[:], bu[:], stc[:, MU:MU + 1],
                                stc[:, INV:INV + 1], ALU.subtract, ALU.mult)
                v.tensor_mul(stc[:, I2:I2 + 1], stc[:, INV:INV + 1],
                             stc[:, INV:INV + 1])
                v.drain()
                v.tensor_copy(xnbf[:], xn32[:])
                v.tensor_mul(stc[:, MX2:MX2 + 1], stc[:, VAR:VAR + 1],
                             stc[:, I2:I2 + 1])
                v.reduce_max(stc[:, MX:MX + 1], xn32[:], AX.X,
                             apply_absolute_value=True)
                v.drain()
                return v.tensor_scalar_add(stc[:, M2E:M2E + 1],
                                           stc[:, MX2:MX2 + 1], EPS)
            ev_v2 = b.do("vector", xnf, mark="dv")
            xld_free[i % 2] = ev_v2
            b.wait("scalar", ev_v2)

            def act_rst(s):
                s.activation(stc[:, LNM:LNM + 1], stc[:, M2E:M2E + 1], AF.Ln)
                s.drain()
                return s.activation(stc[:, RST:RST + 1], stc[:, LNM:LNM + 1],
                                    AF.Exp, scale=NEG_HALF)
            ev_rst = b.do("scalar", act_rst, mark="ac")
            b.wait("vector", ev_rst)

            def quant(v, i=i):
                v.tensor_mul(m1c[:, i:i + 1], stc[:, RST:RST + 1],
                             stc[:, MX:MX + 1])
                v.drain()
                v.tensor_scalar_max(m1c[:, i:i + 1], m1c[:, i:i + 1], EPS)
                v.drain()
                v.reciprocal(stc[:, RM:RM + 1], m1c[:, i:i + 1])
                v.drain()
                v.tensor_mul(stc[:, QS:QS + 1], stc[:, RST:RST + 1],
                             stc[:, RM:RM + 1])
                v.drain()
                v.tensor_scalar_mul(stc[:, QS:QS + 1], stc[:, QS:QS + 1],
                                    127.0)
                v.drain()
                v.tensor_scalar_mul(tq32[:], xn32[:], stc[:, QS:QS + 1])
                v.drain()
                return v.tensor_scalar(xqbf[:], tq32[:], MAGIC, MAGIC,
                                       ALU.add, ALU.subtract)
            ev_q = b.do("vector", quant, mark="dv")

            b.waits("tensor", [ev_v2, ev_idb])
            for gdc in range(4):
                def tr4(t, gdc=gdc):
                    ins = None
                    for j in range(4):
                        dc = gdc * 4 + j
                        ins = t.transpose(pT[:, j * 128:(j + 1) * 128],
                                          xnbf[:, dc * 128:(dc + 1) * 128],
                                          identb[:])
                    return ins
                ev_t = b.do("tensor", tr4, mark="pe")
                b.wait("vector", ev_t)

                def evac(v, gdc=gdc, i=i):
                    dst = xn_fm[:].rearrange("p (dc t) -> p dc t", dc=ND)[
                        :, gdc * 4:(gdc + 1) * 4, i * 128:(i + 1) * 128]
                    return v.tensor_copy(
                        dst, pT[:].rearrange("p (j t) -> p j t", j=4))
                ev_ev = b.do("vector", evac, mark="dv")
                b.wait("tensor", ev_ev)
            b.waits("tensor", [ev_q, st_free])
            last_evq = None
            for gdc in range(4):
                def trq(t, gdc=gdc):
                    ins = None
                    for j in range(4):
                        dc = gdc * 4 + j
                        ins = t.transpose(pT2[:, j * 128:(j + 1) * 128],
                                          xqbf[:, dc * 128:(dc + 1) * 128],
                                          identb[:])
                    return ins
                ev_t = b.do("tensor", trq, mark="pe")
                b.wait("vector", ev_t)

                def evacq(v, gdc=gdc):
                    dst = xqst[:].rearrange("p (dc t) -> p dc t", dc=ND)[
                        :, gdc * 4:(gdc + 1) * 4, :]
                    return v.tensor_copy(
                        dst, pT2[:].rearrange("p (j t) -> p j t", j=4))
                last_evq = b.do("vector", evacq, mark="dv")
                b.wait("tensor", last_evq)
            b.wait("gpsimd", last_evq)
            st_free = b.dma("gpsimd",
                            xq1_sc[i].rearrange("dc d t -> d dc t"),
                            xqst[:].rearrange("p (dc t) -> p dc t", dc=ND))
        if debug:
            b.wait("gpsimd", ("dv", b.cnt.get("dv", 0), None))
            b.dma("gpsimd", dbg["d_xnfm"][:, :], xn_fm[:])
            b.dma("gpsimd", dbg["d_stats"][:, 0:NT], m1c[:])
        b.barrier()
        pa.close()
        pA_ps.close()

    def make_w_thunks(wi, w_h, nch, F, nelem, wch, ternb):
        st = {"ch_free": [None, None], "tern_free": [None, None],
              "z": None, "wsc_done": None}
        thunks = []
        SB = 48 * wi

        def t_zero():
            st["z"] = b.do("vector", lambda v, wi=wi: v.memset(
                wacc[:, wi:wi + 1], 0.0), mark="dv")
        thunks.append(t_zero)

        st["loads"] = {}

        def t_load(c):
            buf = wch[c % 2]
            b.wait("sync", st["ch_free"][c % 2])
            st["loads"][c] = b.dma("sync", buf[:, 0:F],
                                   w_h[c * 128:(c + 1) * 128, :])

        def t_stat(c):
            buf = wch[c % 2]
            b.waits("vector", [st["loads"][c], st["z"]])

            def statw(v, buf=buf):
                v.drain()
                v.tensor_reduce(wacc[:, 2 + wi:3 + wi], buf[:, 0:F], AX.X,
                                ALU.add, apply_absolute_value=True)
                v.drain()
                return v.tensor_add(wacc[:, wi:wi + 1], wacc[:, wi:wi + 1],
                                    wacc[:, 2 + wi:3 + wi])
            st["ch_free"][c % 2] = b.do("vector", statw, mark="dv")
        for c in range(nch):
            if c == 0:
                thunks.append(lambda: t_load(0))
            thunks.append(lambda c=c: (t_load(c + 1) if c + 1 < nch
                                       else None, t_stat(c)))

        def t_scales():
            col = 4 + 2 * wi
            b.wait("vector", st["ch_free"][(nch - 1) % 2])
            prev = b.do("vector", lambda v: (
                v.drain(),
                v.tensor_copy(wacc[:, col:col + 1],
                              wacc[:, wi:wi + 1]))[-1], mark="dv")
            n = 128
            while n > 32:
                h = n // 2
                b.wait("gpsimd", prev)
                ev_d = b.dma("gpsimd", wacc[0:h, col + 1:col + 2],
                             wacc[h:n, col:col + 1])
                b.waits("vector", [ev_d])

                def fold(v, h=h, col=col):
                    v.drain()
                    return v.tensor_tensor(
                        wacc[0:h, col:col + 1], wacc[0:h, col:col + 1],
                        wacc[0:h, col + 1:col + 2], ALU.add)
                prev = b.do("vector", fold, mark="dv")
                n = h
            b.wait("gpsimd", prev)
            ev_g1 = b.dma("gpsimd", wfold_sc[wi, :],
                          wacc[0:32, col:col + 1])
            b.wait("gpsimd", ev_g1)
            ev_g = b.dma("gpsimd", srow[0:1, SB:SB + 32], wfold_sc[wi, :])
            b.wait("vector", ev_g)

            def scals(v, nelem=nelem):
                v.drain()
                v.reduce_sum(srow[0:1, SB + 32:SB + 33],
                             srow[0:1, SB:SB + 32], AX.X)
                v.drain()
                v.tensor_scalar_mul(srow[0:1, SB + 33:SB + 34],
                                    srow[0:1, SB + 32:SB + 33], 1.0 / nelem)
                v.drain()
                v.tensor_scalar_max(srow[0:1, SB + 36:SB + 37],
                                    srow[0:1, SB + 33:SB + 34], EPS)
                v.drain()
                v.tensor_scalar_mul(srow[0:1, SB + 34:SB + 35],
                                    srow[0:1, SB + 36:SB + 37], 0.5)
                v.drain()
                return v.tensor_scalar_mul(
                    srow[0:1, SB + 35:SB + 36],
                    srow[0:1, SB + 36:SB + 37], -0.5)
            prev = b.do("vector", scals, mark="dv")
            b.wait("gpsimd", prev)
            prev = b.dma("gpsimd", wrep_sc[wi, 0:1, :],
                         srow[0:1, SB + 34:SB + 37])
            n = 1
            while n < 128:
                b.wait("gpsimd", prev)
                prev = b.dma("gpsimd", wrep_sc[wi, n:2 * n, :],
                             wrep_sc[wi, 0:n, :])
                n *= 2
            b.wait("gpsimd", prev)
            prev = b.dma("gpsimd", wsc[:, 3 * wi:3 * wi + 3],
                         wrep_sc[wi, :, :])
            st["wsc_done"] = prev
            if wi == 0:
                b.waits("vector", [prev])
                b.do("vector", lambda v: (
                    v.drain(),
                    v.tensor_scalar(c1c[:], m1c[:], wsc[:, 2:3],
                                    1.0 / 127.0, ALU.mult,
                                    ALU.mult))[-1], mark="dv")
        thunks.append(t_scales)

        def t_tern(c):
            buf = wch[c % 2]
            tb = ternb[c % 2]
            b.waits("vector", [st["loads"][c], st["wsc_done"],
                               st["tern_free"][c % 2]])

            def ternf(v, buf=buf, tb=tb, wi=wi):
                v.drain()
                v.tensor_single_scalar(tb[:, 0:F], buf[:, 0:F],
                                       wsc[:, 3 * wi + 1:3 * wi + 2],
                                       ALU.is_lt)
                v.drain()
                return v.scalar_tensor_tensor(
                    tb[:, 0:F], buf[:, 0:F],
                    wsc[:, 3 * wi:3 * wi + 1], tb[:, 0:F],
                    ALU.is_gt, ALU.subtract)
            ev_t = b.do("vector", ternf, mark="dv")
            st["ch_free"][c % 2] = ev_t
            b.wait("gpsimd", ev_t)
            if wi == 0:
                dst = w1t_sc[:, c, :, :].rearrange("ob d o -> d ob o")
                src = tb[:, 0:F].rearrange("d (ob o) -> d ob o", ob=NOB)
            else:
                dst = w2t_sc[:, c, :, :].rearrange("dpc o dp -> o dpc dp")
                src = tb[:, 0:F].rearrange("o (dpc dp) -> o dpc dp",
                                           dpc=ND)
            st["tern_free"][c % 2] = b.dma("gpsimd", dst, src)
        for c in range(nch):
            if c == 0:
                thunks.append(lambda: t_load(0))
            thunks.append(lambda c=c: (t_load(c + 1) if c + 1 < nch
                                       else None, t_tern(c)))
        return thunks

    kv_ctx = ExitStack()
    k_fm = kv_ctx.enter_context(
        nc.sbuf_tensor("k_fm", [128, NKV * T], BF16, side="right"))
    v_km = kv_ctx.enter_context(
        nc.sbuf_tensor("v_km", [128, NKV * T], BF16, side="right"))
    b.phase = "P1kv"
    if upto >= 3:
        p1 = ExitStack()
        e1 = p1.enter_context
        wres = e1(nc.sbuf_tensor("wres1", [128, 16 * 1024], BF16))
        wstg = [e1(nc.sbuf_tensor(f"wstg{j}", [128, 1024], F32))
                for j in range(2)]
        a16 = e1(nc.sbuf_tensor("a16", [128, 1024], BF16))
        b16 = e1(nc.sbuf_tensor("b16", [128, 1024], BF16))
        krawb = e1(nc.sbuf_tensor("krawb", [128, 1024], BF16))
        krotb = e1(nc.sbuf_tensor("krotb", [128, 1024], BF16))
        vstg = e1(nc.sbuf_tensor("vstg", [128, 1024], BF16))
        p1_ps = ExitStack()
        pA = p1_ps.enter_context(nc.psum_tensor("pA1", [128, 1024], F32))
        pB = p1_ps.enter_context(nc.psum_tensor("pB1", [128, 1024], F32))
        pTv = p1_ps.enter_context(nc.psum_tensor("pTv", [128, 512], BF16))

        last_pe_half = None
        stg_free = [None, None]
        prev_evac = [None, None]
        prev_kr = None
        for half in range(2):
            cast_evs = []
            for dc in range(ND):
                sb = wstg[dc % 2]
                b.wait("sync", stg_free[dc % 2])
                ev_w = b.dma("sync", sb[:],
                             wkvq_h[dc * 128:(dc + 1) * 128,
                                    half * 1024:(half + 1) * 1024])
                b.waits("scalar", [ev_w, last_pe_half])
                ev_c = b.do("scalar", lambda s, sb=sb, dc=dc: s.copy(
                    wres[:, dc * 1024:(dc + 1) * 1024], sb[:]), mark="ac")
                stg_free[dc % 2] = ev_c
                cast_evs.append(ev_c)
            b.waits("tensor", cast_evs)
            unit = 0
            for oc in range(8):
                for tb in range(2):
                    ps = pA if unit % 2 == 0 else pB
                    b.wait("tensor", prev_evac[unit % 2])

                    def mmkv(t, oc=oc, tb=tb, ps=ps):
                        ins = None
                        for dc in range(ND):
                            for nb in range(2):
                                ins = t.matmul(
                                    ps[:, nb * 512:(nb + 1) * 512],
                                    wres[:, dc * 1024 + oc * 128:
                                         dc * 1024 + (oc + 1) * 128],
                                    xn_fm[:, dc * T + tb * 1024 + nb * 512:
                                          dc * T + tb * 1024 +
                                          (nb + 1) * 512],
                                    start=(dc == 0), stop=(dc == ND - 1))
                        return ins
                    ev_mm = b.do("tensor", mmkv, mark="pe")
                    b.wait("vector", ev_mm)
                    if half == 0:
                        ev_cp = b.do("vector", lambda v, ps=ps:
                                     v.tensor_copy(vstg[:], ps[:]),
                                     mark="dv")
                        prev_evac[unit % 2] = ev_cp
                        b.wait("tensor", ev_cp)
                        for grp in range(2):
                            def trv(t, grp=grp):
                                ins = None
                                for j in range(4):
                                    ins = t.transpose(
                                        pTv[:, j * 128:(j + 1) * 128],
                                        vstg[:, (grp * 4 + j) * 128:
                                             (grp * 4 + j + 1) * 128],
                                        identb[:])
                                return ins
                            ev_t = b.do("tensor", trv, mark="pe")
                            b.wait("vector", ev_t)

                            def evv(v, oc=oc, tb=tb, grp=grp):
                                kt0 = oc * 16 + tb * 8 + grp * 4
                                return v.tensor_copy(
                                    v_km[:, kt0 * 128:(kt0 + 4) * 128],
                                    pTv[:])
                            ev_e = b.do("vector", evv, mark="dv")
                            b.wait("tensor", ev_e)
                    else:
                        b.waits("scalar", [ev_mm, prev_kr])

                        def evk(sc, ps=ps):
                            sc.copy(krawb[:], ps[:])
                            sc.copy(krotb[0:64, :], ps[64:128, :])
                            return sc.copy(krotb[64:128, :], ps[0:64, :])
                        ev_kc = b.do("scalar", evk, mark="ac")
                        prev_evac[unit % 2] = ev_kc
                        b.wait("vector", ev_kc)

                        def ropek(v, oc=oc, tb=tb):
                            sl = slice(tb * 1024, (tb + 1) * 1024)
                            v.drain()
                            v.tensor_mul(a16[:], krawb[:], cosTb[:, sl])
                            v.tensor_mul(b16[:], krotb[:], ssTb[:, sl])
                            v.drain()
                            return v.tensor_add(
                                k_fm[:, oc * T + tb * 1024:
                                     oc * T + (tb + 1) * 1024],
                                a16[:], b16[:])
                        ev_r = b.do("vector", ropek, mark="dv")
                        prev_kr = ev_r
                    unit += 1
            last_pe_half = ("pe", b.cnt.get("pe", 0), "tensor")
        if debug:
            b.wait("gpsimd", ("dv", b.cnt.get("dv", 0), None))
            b.dma("gpsimd", dbg["d_kfm"][:, :], k_fm[:])
            b.dma("gpsimd", dbg["d_vkm"][:, :], v_km[:])
        b.barrier()
        p1.close()
        p1_ps.close()

    b.phase = "P2q"
    if upto >= 4:
        p2 = ExitStack()
        e2 = p2.enter_context
        wres2 = e2(nc.sbuf_tensor("wres2", [128, 16 * 1024], BF16))
        wstg2 = [e2(nc.sbuf_tensor(f"wstg2{j}", [128, 1024],
                                   F32)) for j in range(2)]
        a16q = e2(nc.sbuf_tensor("a16q", [128, 1024], BF16))
        b16q = e2(nc.sbuf_tensor("b16q", [128, 1024], BF16))
        qrawb = e2(nc.sbuf_tensor("qrawb", [128, 1024], BF16))
        qrotb = e2(nc.sbuf_tensor("qrotb", [128, 1024], BF16))
        qstg = [e2(nc.sbuf_tensor(f"qstg{j}", [128, 1024], BF16))
                for j in range(2)]
        p2_ps = ExitStack()
        pA2 = p2_ps.enter_context(nc.psum_tensor("pA2", [128, 1024], F32))
        pB2 = p2_ps.enter_context(nc.psum_tensor("pB2", [128, 1024], F32))

        last_pe_qr = None
        stg_free = None
        prev_rope = [None, None]
        prev_evq = [None, None]
        prev_pq = None
        prev_r_any = None
        st_free = [None, None]
        for qr in range(4):
            cast_evs = []
            for dc in range(ND):
                sb2 = wstg2[dc % 2]
                b.wait("sync", stg_free[dc % 2] if isinstance(stg_free, list)
                       else stg_free)
                ev_w = b.dma("sync", sb2[:],
                             wkvq_h[dc * 128:(dc + 1) * 128,
                                    2048 + qr * 1024:2048 + (qr + 1) * 1024])
                b.waits("scalar", [ev_w, last_pe_qr])
                ev_c = b.do("scalar", lambda s, dc=dc, sb2=sb2: s.copy(
                    wres2[:, dc * 1024:(dc + 1) * 1024], sb2[:]),
                    mark="ac")
                if not isinstance(stg_free, list):
                    stg_free = [None, None]
                stg_free[dc % 2] = ev_c
                cast_evs.append(ev_c)
            b.waits("tensor", cast_evs)
            unit = 0
            for oc in range(8):
                h = qr * 8 + oc
                for tb in range(2):
                    ps = pA2 if unit % 2 == 0 else pB2
                    b.wait("tensor", prev_rope[unit % 2])

                    def mmq(t, oc=oc, tb=tb, ps=ps):
                        ins = None
                        for dc in range(ND):
                            for nb in range(2):
                                ins = t.matmul(
                                    ps[:, nb * 512:(nb + 1) * 512],
                                    wres2[:, dc * 1024 + oc * 128:
                                          dc * 1024 + (oc + 1) * 128],
                                    xn_fm[:, dc * T + tb * 1024 + nb * 512:
                                          dc * T + tb * 1024 +
                                          (nb + 1) * 512],
                                    start=(dc == 0), stop=(dc == ND - 1))
                        return ins
                    ev_mm = b.do("tensor", mmq, mark="pe")
                    qs_buf = qstg[unit % 2]
                    b.waits("scalar", [ev_mm, prev_r_any])

                    def evq(sc, ps=ps):
                        sc.copy(qrawb[:], ps[:])
                        sc.copy(qrotb[0:64, :], ps[64:128, :])
                        return sc.copy(qrotb[64:128, :], ps[0:64, :])
                    ev_qc = b.do("scalar", evq, mark="ac")
                    prev_rope[unit % 2] = ev_qc
                    b.waits("vector", [ev_qc, st_free[unit % 2]])

                    def ropeq(v, tb=tb, qs_buf=qs_buf):
                        sl = slice(tb * 1024, (tb + 1) * 1024)
                        v.drain()
                        v.tensor_mul(a16q[:], qrawb[:], cosTb[:, sl])
                        v.tensor_mul(b16q[:], qrotb[:], ssTb[:, sl])
                        v.drain()
                        return v.tensor_add(qs_buf[:], a16q[:], b16q[:])
                    ev_r = b.do("vector", ropeq, mark="dv")
                    prev_evq[unit % 2] = ev_r
                    prev_r_any = ev_r
                    b.wait("gpsimd", ev_r)
                    st_free[unit % 2] = b.dma(
                        "gpsimd",
                        q_sc[h // 4, :, tb * 8:(tb + 1) * 8, h % 4, :],
                        qs_buf[:].rearrange("p (qt t) -> p qt t", qt=8))
                    unit += 1
            last_pe_qr = ("pe", b.cnt.get("pe", 0), "tensor")
        b.barrier()
        p2.close()
        p2_ps.close()
    rope_ctx.close()
    xn_ctx.close()

    b.phase = "P3flash"
    if upto >= 5:
        p3 = ExitStack()
        e3 = p3.enter_context
        qbuf = [e3(nc.sbuf_tensor(f"qbuf{j}", [128, 4096], BF16))
                for j in range(2)]
        Pp = [e3(nc.sbuf_tensor(f"Pp{j}", [128, 1024], BF16))
              for j in range(3)]
        zl = e3(nc.sbuf_tensor("zl", [128, 1024], F32))
        zr = e3(nc.sbuf_tensor("zr", [128, 1024], F32))
        atile = [e3(nc.sbuf_tensor(f"atile{j}", [128, 1024], BF16))
                 for j in range(2)]
        p3_ps = ExitStack()
        pS = [p3_ps.enter_context(nc.psum_tensor(f"pS{j}", [128, 1024], F32))
              for j in range(2)]
        pAV = p3_ps.enter_context(nc.psum_tensor("pAV", [128, 1024], F32))
        pZ = p3_ps.enter_context(nc.psum_tensor("pZ", [128, 1024], F32))
        wch1 = [e3(nc.sbuf_tensor(f"wch1_{j}", [128, DI], F32))
                for j in range(2)]
        tern1 = [e3(nc.sbuf_tensor(f"tern1_{j}", [128, DI], BF16))
                 for j in range(2)]
        w1_th = (make_w_thunks(0, w1_h, 16, DI, D * DI, wch1, tern1)
                 if upto >= 2 else [])

        q_free = [None, None]
        at_free = [None, None]
        ev_evac = None
        ev_zr = None
        ai = 0
        for qt in range(NT):
            qb = qbuf[qt % 2]
            b.wait("sync", q_free[qt % 2])
            ev_q = b.dma("sync", qb[:].rearrange("p (kvh f) -> p kvh f",
                                                 kvh=NKV),
                         q_sc[:, :, qt, :, :].rearrange(
                             "kvh hd g t -> hd kvh (g t)"))
            last_smm = None
            exp_evs = {}
            pe_z = {}
            g = 0
            for kvp in range(4):
                kvh0, kvh1 = 2 * kvp, 2 * kvp + 1

                def emit_smm(kt, g, kvh0=kvh0, kvh1=kvh1, qt=qt, qb=qb):
                    par = g % 2
                    pp = g % 3
                    b.waits("tensor", [ev_q, exp_evs.get(g - 2), ev_evac])

                    def smm(t, kt=kt, par=par, qt=qt):
                        stp = (kt != qt)
                        t.matmul(pS[par][:, 0:512],
                                 k_fm[:, kvh0 * T + kt * 128:
                                      kvh0 * T + (kt + 1) * 128],
                                 qb[:, kvh0 * 512:(kvh0 + 1) * 512],
                                 start=True, stop=stp)
                        ins = t.matmul(pS[par][:, 512:1024],
                                       k_fm[:, kvh1 * T + kt * 128:
                                            kvh1 * T + (kt + 1) * 128],
                                       qb[:, kvh1 * 512:(kvh1 + 1) * 512],
                                       start=True, stop=stp)
                        if kt == qt:
                            t.matmul(pS[par][:, 0:512], identb[:],
                                     tri2b[:, 0:512], start=False,
                                     stop=True)
                            ins = t.matmul(pS[par][:, 512:1024], identb[:],
                                           tri2b[:, 512:1024], start=False,
                                           stop=True)
                        return ins
                    ev_s = b.do("tensor", smm, mark="pe")
                    src_ev = ev_s
                    b.waits("scalar", [src_ev, pe_z.get(g - 3)])
                    ev_e = b.do("scalar", lambda s, par=par, pp=pp:
                                s.activation(Pp[pp][:], pS[par][:], AF.Exp,
                                             scale=SC_ATT), mark="ac")
                    exp_evs[g] = ev_e
                    return ev_s

                def emit_av(kt, g, kvh0=kvh0, kvh1=kvh1, qt=qt):
                    pp = g % 3
                    b.wait("tensor", exp_evs[g])
                    b.do("tensor", lambda t, kt=kt, pp=pp:
                         t.matmul(pAV[:, 0:512],
                                  v_km[:, (kvh0 * 16 + kt) * 128:
                                       (kvh0 * 16 + kt + 1) * 128],
                                  Pp[pp][:, 0:512], start=(kt == 0),
                                  stop=(kt == qt)))
                    b.do("tensor", lambda t, kt=kt, pp=pp:
                         t.matmul(pAV[:, 512:1024],
                                  v_km[:, (kvh1 * 16 + kt) * 128:
                                       (kvh1 * 16 + kt + 1) * 128],
                                  Pp[pp][:, 512:1024], start=(kt == 0),
                                  stop=(kt == qt)))
                    b.do("tensor", lambda t, kt=kt, pp=pp:
                         t.matmul(pZ[:, 0:512], ones_bf[:],
                                  Pp[pp][:, 0:512], start=(kt == 0),
                                  stop=(kt == qt)))
                    pe_z[g] = b.do(
                        "tensor", lambda t, kt=kt, pp=pp:
                        t.matmul(pZ[:, 512:1024], ones_bf[:],
                                 Pp[pp][:, 512:1024], start=(kt == 0),
                                 stop=(kt == qt)), mark="pe")
                    return pe_z[g]

                for kt in range(qt + 1):
                    last_smm = emit_smm(kt, g + kt)
                    if kt >= 1:
                        emit_av(kt - 1, g + kt - 1)
                ev_zlast = emit_av(qt, g + qt)
                g += qt + 1
                at = atile[ai % 2]
                b.waits("scalar", [ev_zlast])

                def evz(s):
                    s.activation(zl[:], pZ[:], AF.Ln)
                    s.drain()
                    return s.activation(zr[:], zl[:], AF.Exp, scale=-1.0)
                ev_zr = b.do("scalar", evz, mark="ac")
                b.waits("vector", [ev_zr, at_free[ai % 2]])
                ev_evac = b.do("vector", lambda v, at=at: v.tensor_tensor(
                    at[:], pAV[:], zr[:], ALU.mult), mark="dv")
                b.wait("gpsimd", ev_evac)
                at_free[ai % 2] = b.dma(
                    "gpsimd",
                    at_sc[kvh0:kvh0 + 2, :, qt, :].rearrange(
                        "kv hd f -> hd kv f"),
                    at[:].rearrange("p (kv f) -> p kv f", kv=2))
                ai += 1
                if w1_th:
                    w1_th.pop(0)()
            q_free[qt % 2] = last_smm
        while w1_th:
            w1_th.pop(0)()
        if debug:
            b.wait("gpsimd", ("dv", b.cnt.get("dv", 0), None))
            b.dma("gpsimd", dbg["d_wsc"][:, 0:3], wsc[:, 0:3])
        b.barrier()
        p3.close()
        p3_ps.close()
    kv_ctx.close()

    b.phase = "F1"
    if upto >= 6:
        pf = ExitStack()
        ef = pf.enter_context
        xq1b = ef(nc.sbuf_tensor("xq1b", [128, ND * T], BF16))
        w1b = [ef(nc.sbuf_tensor(f"w1b{j}", [128, DI], BF16))
               for j in range(2)]
        b1bc = ef(nc.sbuf_tensor("b1bc", [128, DI], F32))
        b1row_t = ef(nc.sbuf_tensor("b1row_t", [1, DI], F32))
        b1row = b1row_t[0:1, :]
        ev_b1l = b.dma("sync", b1row, b1_h[0:1, :])
        htmp = [ef(nc.sbuf_tensor(f"htmp{j}", [128, 512], F32))
                for j in range(2)]
        hsil = [ef(nc.sbuf_tensor(f"hsil{j}", [128, 512], F32))
                for j in range(2)]
        pf_ps = ExitStack()
        pF = [pf_ps.enter_context(nc.psum_tensor(f"pF{j}", [128, 512], F32))
              for j in range(2)]
        pB5 = pf_ps.enter_context(nc.psum_tensor("pB5", [128, 512], F32))
        wch2 = [ef(nc.sbuf_tensor(f"wch2_{j}", [128, D], F32))
                for j in range(2)]
        tern2 = [ef(nc.sbuf_tensor(f"tern2_{j}", [128, D], BF16))
                 for j in range(2)]
        w2_th = (make_w_thunks(1, w2_h, 64, D, DI * D, wch2, tern2)
                 if upto >= 2 else [])

        ev_xq = []
        for qt in range(NT):
            dst = xq1b[:].rearrange("p (dc t) -> p dc t", dc=ND)[
                :, :, qt * 128:(qt + 1) * 128]
            ev_xq.append(b.dma("sync", dst,
                               xq1_sc[qt].rearrange("dc d t -> d dc t")))
        b.wait("tensor", ev_b1l)
        for ob in range(NOB):
            ev_b = b.do("tensor", lambda t, ob=ob: t.matmul(
                pB5[:], onec[0:1, :], b1row[0:1, ob * 512:(ob + 1) * 512],
                start=True, stop=True), mark="pe")
            b.wait("vector", ev_b)
            ev_v = b.do("vector", lambda v, ob=ob: (
                v.tensor_copy(b1bc[:, ob * 512:(ob + 1) * 512], pB5[:]),
                v.drain())[0], mark="dv")
            b.wait("tensor", ev_v)

        w_free = [None, None]
        ui = 0
        h_free = [None, None]
        ev_evF = [None, None]
        for ob in range(NOB):
            wb_ = w1b[ob % 2]
            b.wait("sync", w_free[ob % 2])
            ev_w = b.dma("sync", wb_[:].rearrange("p (dc o) -> p dc o",
                                                  dc=ND),
                         w1t_sc[ob].rearrange("dc d o -> d dc o"))
            last_mm = None
            for qt in range(NT):
                ps = pF[ui % 2]
                b.waits("tensor", ev_xq + [ev_w, ev_evF[ui % 2]])

                def mmf(t, ob=ob, qt=qt, ps=ps, wb_=wb_):
                    ins = None
                    for dc in range(ND):
                        ins = t.matmul(ps[:],
                                       xq1b[:, dc * T + qt * 128:
                                            dc * T + (qt + 1) * 128],
                                       wb_[:, dc * 512:(dc + 1) * 512],
                                       start=(dc == 0), stop=(dc == ND - 1))
                    return ins
                ev_mm = b.do("tensor", mmf, mark="pe")
                last_mm = ev_mm
                b.waits("vector", [ev_mm, h_free[ui % 2]])

                def evf(v, ps=ps, qt=qt, ob=ob, j=ui % 2):
                    return v.scalar_tensor_tensor(
                        htmp[j][:], ps[:], c1c[:, qt:qt + 1],
                        b1bc[:, ob * 512:(ob + 1) * 512],
                        ALU.mult, ALU.add)
                ev_e = b.do("vector", evf, mark="dv")
                ev_evF[ui % 2] = ev_e
                b.wait("scalar", ev_e)
                ev_s = b.do("scalar", lambda s, j=ui % 2: s.activation(
                    hsil[j][:], htmp[j][:], AF.Silu), mark="ac")
                b.wait("gpsimd", ev_s)
                h_free[ui % 2] = b.dma(
                    "gpsimd", h_sc[qt, :, ob * 512:(ob + 1) * 512],
                    hsil[ui % 2][:])
                ui += 1
                if w2_th and ui % 2 == 0:
                    w2_th.pop(0)()
            w_free[ob % 2] = last_mm
        while w2_th:
            w2_th.pop(0)()
        b.barrier()
        pf.close()
        pf_ps.close()

    b.phase = "P4out"
    if upto >= 7:
        p4 = ExitStack()
        e4 = p4.enter_context
        woutb = e4(nc.sbuf_tensor("woutb", [128, 32 * D], BF16))
        wst4 = [e4(nc.sbuf_tensor(f"wst4{j}", [128, D], F32))
                for j in range(2)]
        abuf = [e4(nc.sbuf_tensor(f"abuf{j}", [128, 4096], BF16))
                for j in range(2)]
        sst = [e4(nc.sbuf_tensor(f"sst{j}", [128, D], BF16))
               for j in range(2)]
        p4_ps = ExitStack()
        pO = [p4_ps.enter_context(nc.psum_tensor(f"pO{j}", [128, 2048],
                                                 F32)) for j in range(2)]

        stg_free = [None, None]
        for h in range(32):
            sb = wst4[h % 2]
            b.wait("sync", stg_free[h % 2])
            ev_w = b.dma("sync", sb[:], wout_h[h * 128:(h + 1) * 128, :])
            b.wait("scalar", ev_w)
            ev_c = b.do("scalar", lambda s, sb=sb, h=h: s.copy(
                woutb[:, h * D:(h + 1) * D], sb[:]), mark="ac")
            stg_free[h % 2] = ev_c
        cast_done = stg_free[31 % 2]
        a_free = [None, None]
        s_free = [None, None]
        ev_ev4 = [None, None]
        for qt in range(NT):
            ab = abuf[qt % 2]
            b.wait("sync", a_free[qt % 2])
            ev_a = b.dma("sync", ab[:].rearrange("p (kvh f) -> p kvh f",
                                                 kvh=NKV),
                         at_sc[:, :, qt, :].rearrange("kvh hd f -> hd kvh f"))
            ss = sst[qt % 2]
            b.wait("vector", s_free[qt % 2])
            par4 = qt % 2
            b.waits("tensor", [ev_a, cast_done, ev_ev4[par4]])

            def mmo(t, ab=ab, par4=par4):
                ins = None
                for h in range(32):
                    for dpb in range(4):
                        ins = t.matmul(
                            pO[par4][:, dpb * 512:(dpb + 1) * 512],
                            ab[:, (h // 4) * 512 + (h % 4) * 128:
                               (h // 4) * 512 + (h % 4 + 1) * 128],
                            woutb[:, h * D + dpb * 512:
                                  h * D + (dpb + 1) * 512],
                            start=(h == 0), stop=(h == 31))
                return ins
            ev_mm = b.do("tensor", mmo, mark="pe")
            a_free[qt % 2] = ev_mm
            b.wait("vector", ev_mm)
            ev_e = b.do("vector", lambda v, ss=ss, par4=par4:
                        v.tensor_copy(ss[:], pO[par4][:]), mark="dv")
            ev_ev4[par4] = ev_e
            b.wait("gpsimd", ev_e)
            s_free[qt % 2] = b.dma("gpsimd", s_sc[qt], ss[:])
        b.barrier()
        p4.close()
        p4_ps.close()

    b.phase = "F2"
    if upto >= 8:
        pg = ExitStack()
        eg = pg.enter_context
        gbc = eg(nc.sbuf_tensor("gbc", [128, DI], F32))
        bbc = eg(nc.sbuf_tensor("bbc", [128, DI], F32))
        stq = eg(nc.sbuf_tensor("stq", [128, 34], F32))
        pg_ps = ExitStack()
        pG = pg_ps.enter_context(nc.psum_tensor("pG", [128, 512], F32))
        pT3 = [pg_ps.enter_context(nc.psum_tensor(f"pT3{j}", [128, 512],
                                                  BF16)) for j in range(2)]

        rows_ctx = ExitStack()
        growb = rows_ctx.enter_context(
            nc.sbuf_tensor("growb", [1, 2 * DI], F32))
        ev_gl = b.dma("sync", growb[0:1, 0:DI], g2_h[0:1, :])
        ev_bl = b.dma("sync", growb[0:1, DI:2 * DI], bb2_h[0:1, :])
        b.waits("tensor", [ev_gl, ev_bl])
        ev_gb_done = None
        for ri, dstt in ((0, gbc), (1, bbc)):
            for ob in range(NOB):
                ev_b = b.do("tensor", lambda t, ob=ob, ri=ri:
                            t.matmul(pG[:], onec[0:1, :],
                                     growb[0:1, ri * DI + ob * 512:
                                           ri * DI + (ob + 1) * 512],
                                     start=True, stop=True), mark="pe")
                b.wait("vector", ev_b)
                ev_v = b.do("vector", lambda v, ob=ob, dstt=dstt: (
                    v.tensor_copy(dstt[:, ob * 512:(ob + 1) * 512], pG[:]),
                    v.drain())[0], mark="dv")
                b.wait("tensor", ev_v)
                ev_gb_done = ev_v
        rows_ctx.close()
        hb = eg(nc.sbuf_tensor("hbF", [128, DI], F32))
        ab2 = [eg(nc.sbuf_tensor(f"ab2{j}", [128, DI], F32))
               for j in range(2)]
        xqt2 = [eg(nc.sbuf_tensor(f"xqt2{j}", [128, DI], BF16))
                for j in range(2)]

        (ASUM, ASQ, MU2c, EX2c, VARc, VEPSc, LNc, INVc, SQ2, EXRc, LNRc,
         RST2, T2c, M2c, RM2c, QS2c, NMIc) = range(17)

        h_free = [ev_gb_done, ev_gb_done]
        st2_free = [None, None]
        pool_prev = None
        sq2_prev = None
        hb_free = [None]
        for qt in range(NT):
            j = qt % 2
            B0 = 17 * j
            ab = ab2[j]
            xq = xqt2[j]

            def C(i):
                return stq[:, B0 + i:B0 + i + 1]
            b.waits("sync", [h_free[j], hb_free[0]])
            ev_h = b.dma("sync", hb[:], h_sc[qt])
            ev_ms = b.do("vector", lambda v, C=C: (
                v.memset(C(ASUM), 0.0),
                v.memset(C(ASQ), 0.0),
                v.memset(C(SQ2), 0.0))[-1], mark="dv")
            b.waits("scalar", [ev_h, ev_ms, h_free[j], st2_free[j]])

            def gelu_sq(s, C=C, ab=ab, xq=xq):
                s.activation(ab[:], hb[:], AF.Gelu, accum_out=C(ASUM))
                s.drain()
                return s.activation(xq[:], ab[:], AF.Square,
                                    accum_out=C(ASQ))
            ev_g = b.do("scalar", gelu_sq, mark="ac")
            hb_free[0] = ev_g
            b.wait("vector", ev_g)

            def stats2(v, C=C):
                v.tensor_scalar_mul(C(MU2c), C(ASUM), 1.0 / DI)
                v.tensor_scalar_mul(C(EX2c), C(ASQ), 1.0 / DI)
                v.drain()
                v.tensor_mul(C(VARc), C(MU2c), C(MU2c))
                v.drain()
                v.tensor_sub(C(VARc), C(EX2c), C(VARc))
                v.drain()
                return v.tensor_scalar_add(C(VEPSc), C(VARc), EPS)
            ev_s2 = b.do("vector", stats2, mark="dv")
            b.wait("scalar", ev_s2)
            ev_i2 = b.do("scalar", lambda s, C=C: (
                s.activation(C(LNc), C(VEPSc), AF.Ln),
                s.drain(),
                s.activation(C(INVc), C(LNc), AF.Exp,
                             scale=NEG_HALF))[-1], mark="ac")
            b.wait("vector", ev_i2)
            ev_nm = b.do("vector", lambda v, C=C: (
                v.drain(),
                v.tensor_mul(C(NMIc), C(MU2c), C(INVc)),
                v.drain(),
                v.tensor_scalar_mul(C(NMIc), C(NMIc), -1.0))[-1], mark="dv")
            b.wait("scalar", ev_nm)
            ev_zn = b.do("scalar", lambda s, C=C, ab=ab: (
                s.drain(),
                s.activation(ab[:], ab[:], AF.Identity, bias=C(NMIc),
                             scale=C(INVc)))[-1], mark="ac")
            SPL = 5120
            b.waits("gpsimd", [ev_zn, pool_prev])
            ev_gmp = b.do("gpsimd", lambda g, ab=ab: (
                g.tensor_mul(ab[:, SPL:DI], ab[:, SPL:DI], gbc[:, SPL:DI]),
                g.drain(),
                g.tensor_add(ab[:, SPL:DI], ab[:, SPL:DI],
                             bbc[:, SPL:DI]))[-1], mark="gp")
            pool_prev = ev_gmp
            b.waits("vector", [ev_zn])
            ev_gmv = b.do("vector", lambda v, ab=ab: (
                v.drain(),
                v.tensor_mul(ab[:, 0:SPL], ab[:, 0:SPL], gbc[:, 0:SPL]),
                v.drain(),
                v.tensor_add(ab[:, 0:SPL], ab[:, 0:SPL],
                             bbc[:, 0:SPL]))[-1], mark="dv")
            ev_gm = ev_gmp
            b.waits("vector", [ev_gmp, ev_gmv])
            ev_mx = b.do("vector", lambda v, C=C, ab=ab: (
                v.drain(),
                v.reduce_max(C(T2c), ab[:], AX.X,
                             apply_absolute_value=True))[-1], mark="dv")
            b.waits("scalar", [ev_gm, ev_gmv, sq2_prev])
            ev_sq2 = b.do("scalar", lambda s, C=C, ab=ab, xq=xq:
                          s.activation(xq[:], ab[:], AF.Square,
                                       accum_out=C(SQ2)), mark="ac")
            sq2_prev = ev_sq2
            b.wait("vector", ev_sq2)

            def preR(v, C=C):
                v.tensor_scalar_mul(C(EXRc), C(SQ2), 1.0 / DI)
                v.drain()
                return v.tensor_scalar_add(C(EXRc), C(EXRc), EPS)
            ev_pr = b.do("vector", preR, mark="dv")
            b.wait("scalar", ev_pr)
            ev_r2 = b.do("scalar", lambda s, C=C: (
                s.activation(C(LNRc), C(EXRc), AF.Ln),
                s.drain(),
                s.activation(C(RST2), C(LNRc), AF.Exp,
                             scale=NEG_HALF))[-1], mark="ac")
            b.waits("vector", [ev_r2, ev_mx])

            def quant2w(v, qt=qt, C=C, ab=ab, xq=xq):
                v.tensor_mul(C(M2c), C(RST2), C(T2c))
                v.drain()
                v.tensor_scalar_max(C(M2c), C(M2c), EPS)
                v.drain()
                v.tensor_scalar(c2c[:, qt:qt + 1], C(M2c), wsc[:, 5:6],
                                1.0 / 127.0, ALU.mult, ALU.mult)
                v.reciprocal(C(RM2c), C(M2c))
                v.drain()
                v.tensor_mul(C(QS2c), C(RST2), C(RM2c))
                v.drain()
                v.tensor_scalar_mul(C(QS2c), C(QS2c), 127.0)
                v.drain()
                v.tensor_scalar(ab[:], ab[:], C(QS2c), MAGIC, ALU.mult,
                                ALU.add)
                v.drain()
                return v.tensor_single_scalar(xq[:], ab[:], MAGIC,
                                              ALU.subtract)
            ev_q2 = b.do("vector", quant2w, mark="dv")
            h_free[j] = ev_q2
            b.waits("tensor", [ev_q2, st2_free[j]])
            last_ev = None
            evx_prev = [None, None]
            for grp in range(NT):
                pt = pT3[grp % 2]

                def trx(t, grp=grp, pt=pt, xq=xq):
                    ins = None
                    for jj in range(4):
                        oc = grp * 4 + jj
                        ins = t.transpose(pt[:, jj * 128:(jj + 1) * 128],
                                          xq[:, oc * 128:(oc + 1) * 128],
                                          identb[:])
                    return ins
                b.wait("tensor", evx_prev[grp % 2])
                ev_t = b.do("tensor", trx, mark="pe")
                b.wait("scalar", ev_t)

                def evx(sc, grp=grp, pt=pt, xq=xq):
                    dst = xq[:].rearrange("p (oc t) -> p oc t", oc=NOC)[
                        :, grp * 4:(grp + 1) * 4, :]
                    return sc.copy(dst, pt[:].rearrange(
                        "p (j t) -> p j t", j=4))
                last_ev = b.do("scalar", evx, mark="ac")
                evx_prev[grp % 2] = last_ev
            b.wait("gpsimd", last_ev)
            st2_free[j] = b.dma(
                "gpsimd",
                xq2_sc[:, :, qt * 128:(qt + 1) * 128].rearrange(
                    "oc o t -> o oc t"),
                xqt2[j][:].rearrange("p (oc t) -> p oc t", oc=NOC))
        b.barrier()
        pg.close()
        pg_ps.close()

    b.phase = "M2"
    if upto >= 9:
        pm = ExitStack()
        em = pm.enter_context
        x2q = em(nc.sbuf_tensor("x2q", [128, NOC * 512], BF16))
        w2b = [em(nc.sbuf_tensor(f"w2b{j}", [128, NOC * 128], BF16))
               for j in range(2)]
        c2bc = em(nc.sbuf_tensor("c2bc", [128, 512], F32))
        c2r = em(nc.sbuf_tensor("c2r", [1, 128], F32))
        b2row_t = em(nc.sbuf_tensor("b2row_t", [1, D], F32))
        b2row = b2row_t[0:1, :]
        ev_b2l = b.dma("sync", b2row, b2_h[0:1, :])
        mtmp = [em(nc.sbuf_tensor(f"mtmp{j}", [128, 512],
                                  F32)) for j in range(2)]
        gb_ = [em(nc.sbuf_tensor(f"gb_{j}", [128, 512],
                                 BF16)) for j in range(2)]
        gq = em(nc.sbuf_tensor("gq", [128, 4 * D], BF16))
        xcb = em(nc.sbuf_tensor("xcb", [128, D], F32))
        scb2 = em(nc.sbuf_tensor("scb2", [128, D], BF16))
        sg32 = em(nc.sbuf_tensor("sg32", [128, D], F32))
        ocb = em(nc.sbuf_tensor("ocb", [128, D], F32))
        pm_ps = ExitStack()
        pM = [pm_ps.enter_context(nc.psum_tensor(f"pM{j}", [128, 512], F32))
              for j in range(2)]
        pTg = [pm_ps.enter_context(nc.psum_tensor(f"pTg{j}",
               [128, 512], BF16)) for j in range(2)]
        pCt = pm_ps.enter_context(nc.psum_tensor("pCt", [1, 128], F32))
        pCb = pm_ps.enter_context(nc.psum_tensor("pCb", [128, 128], F32))

        b.wait("tensor", ev_b2l)
        for dpc in range(ND):
            ev_b = b.do("tensor", lambda t, dpc=dpc: t.matmul(
                pCb[:, 0:1], b2row[0:1, dpc * 128:(dpc + 1) * 128],
                one11[0:1, 0:1], start=True, stop=True), mark="pe")
            b.wait("vector", ev_b)
            ev_v = b.do("vector", lambda v, dpc=dpc: v.tensor_copy(
                b2cols[:, dpc:dpc + 1], pCb[:, 0:1]), mark="dv")
            b.wait("tensor", ev_v)

        x2_free = None
        w_free = [None, None]
        ev_evM = [None, None]
        ev_gev = None
        prev_sl = None
        prev_tg = None
        for quar in range(4):
            b.wait("sync", x2_free)
            ev_x2 = b.dma("sync",
                          x2q[:].rearrange("p (oc t) -> p oc t", oc=NOC),
                          xq2_sc[:, :, quar * 512:(quar + 1) * 512]
                          .rearrange("oc o t -> o oc t"))
            for j in range(4):
                qt = quar * 4 + j
                ev_t = b.do("tensor", lambda t, qt=qt: t.matmul(
                    pCt[0:1, :], c2c[:, qt:qt + 1], identf[:],
                    start=True, stop=True), mark="pe")
                b.wait("vector", ev_t)
                ev_v = b.do("vector", lambda v: v.tensor_copy(c2r[:],
                                                              pCt[0:1, :]),
                            mark="dv")
                b.wait("tensor", ev_v)
                ev_b2 = b.do("tensor", lambda t: t.matmul(
                    pCb[:], onec[0:1, :], c2r[0:1, :], start=True,
                    stop=True), mark="pe")
                b.wait("vector", ev_b2)
                ev_c = b.do("vector", lambda v, j=j: (
                    v.tensor_copy(c2bc[:, j * 128:(j + 1) * 128], pCb[:]),
                    v.drain())[0], mark="dv")
                b.wait("tensor", ev_c)

            mm_evs = {}

            def emit_post(dpc):
                ps = pM[dpc % 2]
                b.waits("vector", [mm_evs[dpc], prev_sl])
                ev_e = b.do("vector", lambda v, ps=ps, dpc=dpc:
                            v.tensor_tensor(mtmp[dpc % 2][:], ps[:],
                                            c2bc[:], ALU.mult), mark="dv")
                ev_evM[dpc % 2] = ev_e
                b.waits("scalar", [ev_e, prev_tg])
                ev_sl = b.do("scalar", lambda s, dpc=dpc: s.activation(
                    gb_[dpc % 2][:], mtmp[dpc % 2][:], AF.Silu,
                    bias=b2cols[:, dpc:dpc + 1]), mark="ac")
                b.waits("tensor", [ev_sl, ev_gev])
                ev_tg = b.do("tensor", lambda t, dpc=dpc: (
                    [t.transpose(pTg[dpc % 2][:, j * 128:(j + 1) * 128],
                                 gb_[dpc % 2][:, j * 128:(j + 1) * 128],
                                 identb[:])
                     for j in range(4)])[-1], mark="pe")
                b.wait("vector", ev_tg)

                def evg(v, dpc=dpc):
                    dst = gq[:].rearrange("p (tl d) -> p tl d", tl=4)[
                        :, :, dpc * 128:(dpc + 1) * 128]
                    return v.tensor_copy(
                        dst, pTg[dpc % 2][:].rearrange("p (j t) -> p j t",
                                                       j=4))
                ev_g2 = b.do("vector", evg, mark="dv")
                return ev_sl, ev_tg, ev_g2

            for dpc in range(ND):
                wb2_ = w2b[dpc % 2]
                b.wait("sync", w_free[dpc % 2])
                ev_w = b.dma("sync",
                             wb2_[:].rearrange("p (oc dp) -> p oc dp",
                                               oc=NOC),
                             w2t_sc[dpc].rearrange("oc o dp -> o oc dp"))
                ps = pM[dpc % 2]
                b.waits("tensor", [ev_x2, ev_w, ev_evM[dpc % 2]])

                def mm2(t, ps=ps, wb2_=wb2_):
                    ins = None
                    for oc in range(NOC):
                        ins = t.matmul(ps[:],
                                       wb2_[:, oc * 128:(oc + 1) * 128],
                                       x2q[:, oc * 512:(oc + 1) * 512],
                                       start=(oc == 0), stop=(oc == NOC - 1))
                    return ins
                ev_mm = b.do("tensor", mm2, mark="pe")
                mm_evs[dpc] = ev_mm
                w_free[dpc % 2] = ev_mm
                if dpc == ND - 1:
                    x2_free = ev_mm
                if dpc >= 1:
                    prev_sl, prev_tg, ev_gev = emit_post(dpc - 1)
            prev_sl, prev_tg, ev_gev = emit_post(ND - 1)
            comb_free = None
            for tl in range(4):
                qt = quar * 4 + tl
                b.wait("sync", comb_free)
                ev_xr = b.dma("sync", xcb[:],
                              x_h[qt * 128:(qt + 1) * 128, :])
                ev_sr = b.dma("sync", scb2[:], s_sc[qt])
                b.waits("vector", [ev_xr, ev_sr, ev_gev])

                def comb(v, tl=tl):
                    v.drain()
                    v.tensor_mul(sg32[:], scb2[:],
                                 gq[:, tl * D:(tl + 1) * D])
                    v.drain()
                    return v.tensor_add(ocb[:], xcb[:], sg32[:])
                ev_c = b.do("vector", comb, mark="dv")
                comb_free = ev_c
                b.wait("gpsimd", ev_c)
                ev_y = b.dma("gpsimd", y_h[qt * 128:(qt + 1) * 128, :],
                             ocb[:])
                b.wait("vector", ev_y)
        pm.close()
        pm_ps.close()

    def fin(g):
        for s_ in ("dg0", "dg1", "dg2", "dg3"):
            g.wait_ge(b.semh[s_], b.cnt.get(s_, 0))
    b.emit("gpsimd", fin)

    phase_ranges = []

    def run_ops(eng, key):
        cur = None
        start = nc.n_instructions()
        for ph, f in b.ops[key]:
            if ph != cur:
                if cur is not None:
                    phase_ranges.append((key, cur, start,
                                         nc.n_instructions()))
                cur = ph
                start = nc.n_instructions()
            f(eng)
        if cur is not None:
            phase_ranges.append((key, cur, start, nc.n_instructions()))

    with nc.Block() as block:
        @block.sync
        def _(eng):
            run_ops(eng, "sync")

        @block.tensor
        def _(eng):
            run_ops(eng, "tensor")

        @block.vector
        def _(eng):
            run_ops(eng, "vector")

        @block.scalar
        def _(eng):
            run_ops(eng, "scalar")

        @block.gpsimd
        def _(eng):
            run_ops(eng, "gpsimd")

    ctx.close()
    nc._phase_ranges = phase_ranges
    return nc


def _rope_tables():
    e_ = np.arange(0, 128, 2, dtype=np.float32) / np.float32(128.0)
    p_ = np.power(np.float32(10000.0), e_).astype(np.float32)
    invf = (np.float32(1.0) / p_).astype(np.float32)
    pos = np.arange(T, dtype=np.float32)
    freqs = (pos[:, None] * invf[None, :]).astype(np.float32)
    emb = np.concatenate([freqs, freqs], axis=1)
    cos = np.cos(emb).astype(np.float32).T.copy()
    sin = np.sin(emb).astype(np.float32).T.copy()
    ss = sin.copy()
    ss[0:64] = -sin[0:64]
    return np.ascontiguousarray(cos), np.ascontiguousarray(ss)


def kernel(L0_attn_u, L0_attn_W_in, L0_attn_W_out, L0_ffn_w1, L0_ffn_b1,
           L0_ffn_ln_g, L0_ffn_ln_b, L0_ffn_w2, L0_ffn_b2):
    nb = L0_attn_u.shape[0]
    wkvq = np.ascontiguousarray(np.asarray(L0_attn_W_in).T[:, 4096:10240],
                                dtype=np.float32)
    woutT = np.ascontiguousarray(np.asarray(L0_attn_W_out).T,
                                 dtype=np.float32)
    w1T = np.ascontiguousarray(np.asarray(L0_ffn_w1).T, dtype=np.float32)
    w2T = np.ascontiguousarray(np.asarray(L0_ffn_w2).T, dtype=np.float32)
    b1 = np.asarray(L0_ffn_b1, dtype=np.float32).reshape(1, DI)
    lng = np.asarray(L0_ffn_ln_g, dtype=np.float32).reshape(1, DI)
    lnb = np.asarray(L0_ffn_ln_b, dtype=np.float32).reshape(1, DI)
    b2 = np.asarray(L0_ffn_b2, dtype=np.float32).reshape(1, D)
    cosT, sinT = _rope_tables()
    cosTb = cosT.astype(ml_dtypes.bfloat16)
    sinTb = sinT.astype(ml_dtypes.bfloat16)
    base = np.where(np.arange(128)[:, None] > np.arange(128)[None, :],
                    np.float32(-1e30), np.float32(0.0)).astype(np.float32)
    trimask = np.ascontiguousarray(
        np.tile(base, (1, 8)).astype(ml_dtypes.bfloat16))
    ident = np.eye(128, dtype=np.float32)

    nc = build_nc(debug=DEBUG, upto=UPTO)
    in_maps = []
    for i in range(nb):
        in_maps.append(dict(
            x=np.ascontiguousarray(np.asarray(L0_attn_u[i],
                                              dtype=np.float32)),
            wkvq=wkvq, woutT=woutT, w1T=w1T, w2T=w2T, b1=b1, lng=lng,
            lnb=lnb, b2=b2, cosTb=cosTb, sinTb=sinTb, trimask=trimask,
            ident=ident))
    res = run_bass_kernel_spmd(nc, in_maps, core_ids=list(range(nb)))
    out = np.stack([np.asarray(r["y"], dtype=np.float32)
                    for r in res.results], axis=0)
    if DEBUG:
        kernel.last_results = res
    return out
